# version 01
# baseline (speedup 1.0000x reference)
import math
import numpy as np
import ml_dtypes
import concourse.bass as bass
import concourse.mybir as mybir
from concourse.bass_utils import run_bass_kernel_spmd

F32 = mybir.dt.float32
BF16 = mybir.dt.bfloat16
F16 = mybir.dt.float16
AF = mybir.ActivationFunctionType
ALU = mybir.AluOpType
AX = mybir.AxisListType

T = 1024
D = 2048
NT = T // 128
NC = D // 128
NHEADS = 16
HS = 128
MASKVAL = -30000.0
LN_EPS = 1e-5
RMS_EPS = float(np.finfo(np.float32).eps)
INV2048 = 1.0 / 2048.0
INV16384 = 1.0 / 16384.0
QSCALE = 1.0 / math.sqrt(128.0)
FWHT_SCALE = 1.0 / math.sqrt(8192.0)
THR = 0.5009765625


def bcast(ap, n):
    return bass.AP(ap.tensor, ap.offset, [list(x) for x in ap.ap] + [[0, n]])


class Ph:

    def __init__(self, nc):
        self.nc = nc
        self.sem = {}
        self.cnt = {}
        for name in ("dW", "dX", "eP", "eV", "eA", "eG"):
            self.sem[name] = nc.alloc_semaphore(name)
            self.cnt[name] = 0

    def inc(self, instr, name, n=1):
        instr.then_inc(self.sem[name], n)
        self.cnt[name] += n
        return instr

    def wait(self, eng, *names):
        for name in names:
            if self.cnt[name] > 0:
                eng.wait_ge(self.sem[name], self.cnt[name])

    def gdma(self, out, in_):
        return self.inc(self.nc.gpsimd.dma_start(out=out, in_=in_), "dW", 16)

    def sdma(self, out, in_):
        return self.inc(self.nc.sync.dma_start(out=out, in_=in_), "dX", 16)



def emit_ln_tail(p, ps_big, xres_sb, xsum_sb, xout_sb, trash_f32, stats, tt,
                 eps_col, rstd_extra=None):
    V, A = p.nc.vector, p.nc.scalar
    m, tmp = stats["m"], stats["tmp"]
    if rstd_extra is None:
        ins = V.scalar_tensor_tensor(xsum_sb[:], ps_big[:], 0.0, xres_sb[:],
                                     ALU.add, ALU.add)
    else:
        ins = V.scalar_tensor_tensor(xsum_sb[:], ps_big[:], rstd_extra, xres_sb[:],
                                     ALU.mult, ALU.add)
    p.inc(ins, "eV")
    p.inc(V.tensor_reduce(stats["ssum"][:, tt:tt + 1], xsum_sb[:], AX.XY, ALU.add),
          "eV")
    p.wait(A, "eV")
    p.inc(A.activation(trash_f32[:], xsum_sb[:], AF.Square,
                       accum_out=stats["ssq"][:, tt:tt + 1]), "eA")
    p.wait(V, "eA")
    V.tensor_scalar_mul(m[:], stats["ssum"][:, tt:tt + 1], INV2048)
    p.inc(V.tensor_scalar_mul(tmp[:, 0:1], stats["ssq"][:, tt:tt + 1], INV2048), "eV")
    p.wait(A, "eV")
    p.inc(A.activation(tmp[:, 1:2], m[:], AF.Square), "eA")
    p.wait(V, "eA")
    p.inc(V.scalar_tensor_tensor(tmp[:, 2:3], tmp[:, 1:2], -1.0, tmp[:, 0:1],
                                 ALU.mult, ALU.add), "eV")
    p.wait(A, "eV")
    p.inc(A.activation(tmp[:, 3:4], tmp[:, 2:3], AF.Sqrt, bias=eps_col[:]), "eA")
    p.wait(V, "eA")
    p.inc(V.reciprocal(tmp[:, 4:5], tmp[:, 3:4]), "eV")
    p.wait(A, "eV")
    p.inc(A.copy(tmp[:, 5:6], tmp[:, 4:5]), "eA")
    p.wait(V, "eA")
    p.inc(V.tensor_scalar(xout_sb[:], xsum_sb[:], m[:], tmp[:, 5:6],
                          ALU.subtract, ALU.mult), "eV")


def emit_stream_to_fm(p, x_dram_rows, xf_sb, xot_sb, ps_t, ident_bf, ntiles=NT):
    nc = p.nc
    PE, V = nc.tensor, nc.vector
    Tp = ntiles * 128
    for t in range(ntiles):
        buf = xot_sb[:, 0:D]
        p.wait(nc.gpsimd, "eP")
        p.gdma(buf, x_dram_rows(t))
        p.wait(PE, "dW")
        for half in range(2):
            p.wait(PE, "eV")
            for cc in range(8):
                c = half * 8 + cc
                ins = PE.transpose(ps_t[:, cc * 128:(cc + 1) * 128],
                                   buf[:, c * 128:(c + 1) * 128], ident_bf[:])
                if cc == 7:
                    p.inc(ins, "eP")
            p.wait(V, "eP")
            for cc in range(8):
                c = half * 8 + cc
                ins = V.tensor_copy(xf_sb[:, c * Tp + t * 128: c * Tp + (t + 1) * 128],
                                    ps_t[:, cc * 128:(cc + 1) * 128])
                if cc == 7:
                    p.inc(ins, "eV")



def build_l0a():
    nc = bass.Bass("TRN2", target_bir_lowering=False)
    p = Ph(nc)

    x_own = nc.dram_tensor("x_own", [T, D], F32, kind="ExternalInput").ap()
    x_ctx = nc.dram_tensor("x_ctx", [T, D], F32, kind="ExternalInput").ap()
    qkv_w = nc.dram_tensor("qkv_w", [D, 3 * D], F32, kind="ExternalInput").ap()
    qkv_b = nc.dram_tensor("qkv_b", [3 * D], F32, kind="ExternalInput").ap()
    out_w = nc.dram_tensor("out_w", [D, D], F32, kind="ExternalInput").ap()
    out_b = nc.dram_tensor("out_b", [D], F32, kind="ExternalInput").ap()
    maskB = nc.dram_tensor("maskB", [128, 2048], F16, kind="ExternalInput").ap()
    tri = nc.dram_tensor("tri", [128, 128], F32, kind="ExternalInput").ap()
    ident = nc.dram_tensor("ident", [128, 128], F32, kind="ExternalInput").ap()
    x1_out = nc.dram_tensor("x1", [T, D], F32, kind="ExternalOutput").ap()
    DEBUG = globals().get("L0A_DEBUG", False)
    if DEBUG:
        dbg = {n: nc.dram_tensor(n, s, F32, kind="ExternalOutput").ap()
               for n, s in [("dbg_k", [128, 2048]), ("dbg_q", [128, 1024]),
                            ("dbg_v", [128, 2048]), ("dbg_p", [128, 2048]),
                            ("dbg_s", [128, 2048]), ("dbg_ctx", [128, 1024]),
                            ("dbg_xf", [128, 1024]),
                            ("dbg_xsum", [128, 2048]), ("dbg_st", [128, 32]),
                            ("dbg_xout", [128, 2048])]}

    A = nc.alloc_sbuf_tensor
    xot = A("xot", [128, D], BF16)
    xf_own = A("xf_own", [128, NC * T], BF16)
    xf_ctx = A("xf_ctx", [128, NC * T], BF16)
    K_sb = [A(f"K{i}", [128, 2048], BF16) for i in range(2)]
    V_sb = [A(f"V{i}", [128, 2048], BF16) for i in range(2)]
    Q_sb = [A(f"Q{i}", [128, 1024], BF16) for i in range(2)]
    ctx_sb = A("ctx", [128, NHEADS * T], BF16)
    maskB_sb = A("maskB_sb", [128, 2048], F16)
    tri_bf = A("tri_bf", [128, 128], BF16)
    s_tile = A("s_tile", [128, 2048], F32)
    p_bf = A("p_bf", [128, 2048], BF16)
    pT_sb = A("pT_sb", [128, 2048], BF16)
    wq_sb = [A(f"wq{i}", [128, 16 * 128], BF16) for i in range(2)]
    wk_sb = [A(f"wk{i}", [128, 16 * 128], BF16) for i in range(2)]
    wv_sb = [A(f"wv{i}", [128, 16 * 128], BF16) for i in range(2)]
    ow_sb = [A(f"ow{i}", [128, 2048], BF16) for i in range(2)]
    qb16 = A("qb16", [1, 3 * D], F16)
    ob16 = A("ob16", [1, D], F16)
    ones16 = A("ones16", [1, 512], F16)
    ident_bf = A("ident_bf", [128, 128], BF16)
    xres_sb = A("xres_sb", [128, D], F32)
    xsum_sb = A("xsum_sb", [128, D], F32)
    xout_sb = xres_sb
    negmax = A("negmax", [128, 1], F32)
    rowsum = A("rowsum", [128, 1], F32)
    rcol = A("rcol", [128, 1], F32)
    eps_col = A("eps_col", [128, 1], F32)
    stats = {
        "ssum": A("ssum", [128, NT], F32),
        "ssq": A("ssq", [128, NT], F32),
        "m": A("mcol", [128, 1], F32),
        "tmp": A("tmpc", [128, 6], F32),
    }

    PS_BIG = nc.alloc_psum_tensor("PS_BIG", [128, 2048], F32)
    PS_A = nc.alloc_psum_tensor("PS_A", [128, 512], F32)
    PS_B = nc.alloc_psum_tensor("PS_B", [128, 512], F32)
    PS_C = nc.alloc_psum_tensor("PS_C", [128, 512], F32)
    PS_T = nc.alloc_psum_tensor("PS_T", [128, 1024], BF16)

    PE, V, Sc, G, SY = nc.tensor, nc.vector, nc.scalar, nc.gpsimd, nc.sync

    with nc.body("main"):
        p.sdma(maskB_sb[:], maskB)
        p.gdma(tri_bf[:], tri)
        p.gdma(qb16[:], qkv_b.rearrange("(o n) -> o n", o=1))
        p.gdma(ob16[:], out_b.rearrange("(o n) -> o n", o=1))
        p.gdma(ident_bf[:], ident)
        V.memset(ones16[:], 1.0)
        V.memset(eps_col[:], LN_EPS)
        p.inc(V.memset(stats["tmp"][:], 0.0), "eV")

        emit_stream_to_fm(p, lambda t: x_own[t * 128:(t + 1) * 128, :],
                          xf_own, xot, PS_T, ident_bf)
        emit_stream_to_fm(p, lambda t: x_ctx[t * 128:(t + 1) * 128, :],
                          xf_ctx, xot, PS_T, ident_bf)

        def xf_tokslice(kt512):
            src = xf_ctx if kt512 < 2 else xf_own
            off = (kt512 % 2) * 512
            return src, off

        prod_ps = [PS_A, PS_B]
        for g in range(NHEADS // 2):
            for hh in range(2):
                h = 2 * g + hh
                p.wait(G, "eP")
                for wsb, col0 in ((wq_sb[hh], h * HS), (wk_sb[hh], D + h * HS),
                                  (wv_sb[hh], 2 * D + h * HS)):
                    p.gdma(wsb[:].rearrange("p (c j) -> p c j", j=128),
                           qkv_w[:, col0:col0 + HS].rearrange("(c p) j -> p c j", p=128))
                p.wait(PE, "dW", "eV")
                npd = 0
                for kt in range(4):
                    ps = prod_ps[npd % 2]
                    npd += 1
                    src, off = xf_tokslice(kt)
                    for c in range(NC):
                        PE.matmul(ps[:], wk_sb[hh][:, c * 128:(c + 1) * 128],
                                  src[:, c * T + off: c * T + off + 512],
                                  start=(c == 0), stop=False)
                    p.inc(PE.matmul(ps[:], qb16[0:1, D + h * HS:D + (h + 1) * HS],
                                    ones16[0:1, :], start=False, stop=True), "eP")
                    p.wait(V, "eP")
                    p.inc(V.tensor_copy(K_sb[hh][:, kt * 512:(kt + 1) * 512], ps[:]), "eV")
                    p.wait(PE, "eV")
                for vt in range(16):
                    ps = prod_ps[npd % 2]
                    npd += 1
                    src = xf_ctx if vt < 8 else xf_own
                    toff = (vt % 8) * 128
                    for c in range(NC):
                        PE.matmul(ps[:, 0:128],
                                  src[:, c * T + toff: c * T + toff + 128],
                                  wv_sb[hh][:, c * 128:(c + 1) * 128],
                                  start=(c == 0), stop=False)
                    p.inc(PE.matmul(ps[:, 0:128], ones16[0:1, 0:128],
                                    qb16[0:1, 2 * D + h * HS:2 * D + (h + 1) * HS],
                                    start=False, stop=True), "eP")
                    p.wait(V, "eP")
                    p.inc(V.tensor_copy(V_sb[hh][:, vt * 128:(vt + 1) * 128],
                                        ps[:, 0:128]), "eV")
                    p.wait(PE, "eV")
                for qs in range(2):
                    ps = prod_ps[npd % 2]
                    npd += 1
                    for c in range(NC):
                        PE.matmul(ps[:], wq_sb[hh][:, c * 128:(c + 1) * 128],
                                  xf_own[:, c * T + qs * 512: c * T + qs * 512 + 512],
                                  start=(c == 0), stop=False)
                    p.inc(PE.matmul(ps[:], qb16[0:1, h * HS:(h + 1) * HS],
                                    ones16[0:1, :], start=False, stop=True), "eP")
                    p.wait(V, "eP")
                    p.inc(V.tensor_copy(Q_sb[hh][:, qs * 512:(qs + 1) * 512], ps[:]), "eV")
                    p.wait(PE, "eV")

                slope = 1.0 / 2.0 ** (h / 2.0)
                for qt in range(NT):
                    nk = 3 if qt < 4 else 4
                    ve = 1024 + (qt + 1) * 128
                    kch = ve // 128
                    p.wait(PE, "eV")
                    dj = 1024 + qt * 128
                    w = nk * 512
                    for kt in range(nk):
                        dslc = PS_BIG[:, kt * 512:(kt + 1) * 512]
                        PE.matmul(dslc, Q_sb[hh][:, qt * 128:(qt + 1) * 128],
                                  K_sb[hh][:, kt * 512:(kt + 1) * 512],
                                  start=True, stop=(dj // 512 != kt))
                    p.inc(PE.matmul(PS_BIG[:, dj:dj + 128], ident_bf[:], tri_bf[:],
                                    start=False, stop=True), "eP")
                    p.wait(V, "eP", "eA")
                    if ve < w:
                        V.memset(s_tile[:, ve:w], -1e30)
                    V.scalar_tensor_tensor(s_tile[:, 0:ve], maskB_sb[:, 0:ve], slope,
                                           PS_BIG[:, 0:ve], ALU.mult, ALU.add)
                    p.inc(V.tensor_reduce(negmax[:], s_tile[:, 0:w], AX.XY,
                                          ALU.max, negate=True), "eV")
                    p.wait(Sc, "eV")
                    p.inc(Sc.activation(s_tile[:, 0:w], s_tile[:, 0:w], AF.Exp,
                                        bias=negmax[:], accum_out=rowsum[:]), "eA")
                    p.wait(V, "eA")
                    p.inc(V.reciprocal(rcol[:], rowsum[:]), "eV")
                    p.wait(Sc, "eV")
                    p.inc(Sc.activation(p_bf[:, 0:ve], s_tile[:, 0:ve], AF.Copy,
                                        scale=rcol[:]), "eA")
                    done = 0
                    while done < kch:
                        bn = min(8, kch - done)
                        p.wait(PE, "eV", "eA")
                        for j in range(bn):
                            kc = done + j
                            ins = PE.transpose(PS_T[:, j * 128:(j + 1) * 128],
                                               p_bf[:, kc * 128:(kc + 1) * 128],
                                               ident_bf[:])
                            if j == bn - 1:
                                p.inc(ins, "eP")
                        p.wait(V, "eP")
                        for j in range(bn):
                            kc = done + j
                            ins = V.tensor_copy(pT_sb[:, kc * 128:(kc + 1) * 128],
                                                PS_T[:, j * 128:(j + 1) * 128])
                            if j == bn - 1:
                                p.inc(ins, "eV")
                        done += bn
                    p.wait(PE, "eV", "eA")
                    for kc in range(kch):
                        ins = PE.matmul(PS_C[:, 0:128],
                                        V_sb[hh][:, kc * 128:(kc + 1) * 128],
                                        pT_sb[:, kc * 128:(kc + 1) * 128],
                                        start=(kc == 0), stop=(kc == kch - 1))
                    p.inc(ins, "eP")
                    p.wait(Sc, "eP")
                    p.inc(Sc.copy(ctx_sb[:, h * T + qt * 128: h * T + (qt + 1) * 128],
                                  PS_C[:, 0:128]), "eA")

        if DEBUG:
            p.wait(G, "eV", "eA")
            for name, sb in [("dbg_k", K_sb[0]), ("dbg_q", Q_sb[0]),
                             ("dbg_v", V_sb[0]), ("dbg_p", p_bf),
                             ("dbg_s", s_tile)]:
                p.gdma(dbg[name], sb[:])
            p.gdma(dbg["dbg_ctx"], ctx_sb[:, 0:T])
            p.gdma(dbg["dbg_xf"], xf_own[:, 0:T])

        for tt in range(NT):
            p.wait(SY, "eV")
            p.sdma(xres_sb[:], x_own[tt * 128:(tt + 1) * 128, :])
            p.wait(PE, "eV")
            for hc in range(NC):
                p.wait(G, "eP")
                p.gdma(ow_sb[hc % 2][:], out_w[hc * 128:(hc + 1) * 128, :])
                p.wait(PE, "dW")
                for js in range(4):
                    ins = PE.matmul(PS_BIG[:, js * 512:(js + 1) * 512],
                                    ctx_sb[:, hc * T + tt * 128: hc * T + (tt + 1) * 128],
                                    ow_sb[hc % 2][:, js * 512:(js + 1) * 512],
                                    start=(hc == 0), stop=False)
                p.inc(ins, "eP")
            for js in range(4):
                ins = PE.matmul(PS_BIG[:, js * 512:(js + 1) * 512], ones16[0:1, 0:128],
                                ob16[0:1, js * 512:(js + 1) * 512], start=False,
                                stop=True)
            p.inc(ins, "eP")
            p.wait(V, "eP", "dX")
            emit_ln_tail(p, PS_BIG, xres_sb, xsum_sb, xout_sb, xres_sb, stats, tt,
                         eps_col)
            if DEBUG and tt == 0:
                p.wait(SY, "eV", "eA")
                p.sdma(dbg["dbg_xsum"], xsum_sb[:])
                p.sdma(dbg["dbg_st"][:, 0:8], stats["ssum"][:])
                p.sdma(dbg["dbg_st"][:, 8:16], stats["ssq"][:])
                p.sdma(dbg["dbg_st"][:, 16:22], stats["tmp"][:])
                with nc.allow_non_contiguous_dma(reason="debug col"):
                    p.sdma(dbg["dbg_st"][:, 22:23], stats["m"][:])
                p.wait(SY, "eV")
                p.sdma(dbg["dbg_xout"], xout_sb[:])
            p.wait(SY, "eV")
            p.sdma(x1_out[tt * 128:(tt + 1) * 128, :], xout_sb[:])
        p.wait(SY, "dX")
    return nc



def make_l0a_consts():
    iden = np.eye(128, dtype=np.float32)
    trim = np.where(np.arange(128)[None, :] <= np.arange(128)[:, None],
                    0.0, MASKVAL).astype(np.float32)
    masks = []
    for c in range(8):
        half = c % 2
        B = np.zeros((128, 2048), dtype=np.float16)
        if half == 0:
            B[:, 0:1024] = np.float16(MASKVAL)
            B[:, 1024:2048] = np.arange(0, 1024, dtype=np.float16)[None, :]
        else:
            B[:, :] = np.arange(0, 2048, dtype=np.float16)[None, :]
        masks.append(B)
    return masks, trim, iden


def shard_rows(x):
    return [np.ascontiguousarray(x[c * T:(c + 1) * T]) for c in range(8)]


def ctx_rows(x):
    out = []
    for c in range(8):
        b, half = c // 2, c % 2
        o = (b * 2 + (1 - half)) * T
        out.append(np.ascontiguousarray(x[o:o + T]))
    return out


def run_l0a(x0_full, ins):
    nc = build_l0a()
    masks, trim, iden = make_l0a_consts()
    owns = shard_rows(x0_full)
    ctxs = ctx_rows(x0_full)
    in_maps = []
    for c in range(8):
        in_maps.append({
            "x_own": owns[c], "x_ctx": ctxs[c],
            "qkv_w": ins["L0_attn_qkv_w"], "qkv_b": ins["L0_attn_qkv_b"],
            "out_w": ins["L0_attn_out_w"], "out_b": ins["L0_attn_out_b"],
            "maskB": masks[c], "tri": trim, "ident": iden,
        })
    res = run_bass_kernel_spmd(nc, in_maps, core_ids=list(range(8)))
    return np.concatenate([r["x1"] for r in res.results], axis=0)





INV64 = 1.0 / 64.0
INV8192 = 1.0 / 8192.0
HGU = 16384
HH = 8192


def emit_quant(p, wb, bufA, bufB, gsum, scale_bf, sc2_bf, thr_f32,
               inv64_col, thr_col, G):
    V, A = p.nc.vector, p.nc.scalar
    F = G * 64
    bv = lambda t: t[:, 0:F].rearrange("p (g e) -> p g e", e=64)
    p.inc(A.activation(bufA[:, 0:F], wb[:, 0:F], AF.Abs), "eA")
    p.inc(A.activation(bufB[:, 0:F], wb[:, 0:F], AF.Sign), "eA")
    p.inc(V.tensor_reduce(gsum[:, 0:G], bv(wb), AX.X, ALU.add,
                          apply_absolute_value=True), "eV")
    p.wait(A, "eV")
    p.inc(A.activation(scale_bf[:, 0:G], gsum[:, 0:G], AF.Copy,
                       scale=inv64_col[:]), "eA")
    p.wait(V, "eA")
    p.inc(V.tensor_scalar_max(sc2_bf[:, 0:G], scale_bf[:, 0:G], 1e-8), "eV")
    p.wait(A, "eV")
    p.inc(A.activation(thr_f32[:, 0:G], sc2_bf[:, 0:G], AF.Copy,
                       scale=thr_col[:]), "eA")
    p.wait(V, "eA")
    V.tensor_tensor(bv(bufA), bv(bufA), bcast(thr_f32[:, 0:G], 64), ALU.is_gt)
    V.tensor_tensor(bufB[:, 0:F], bufA[:, 0:F], bufB[:, 0:F], ALU.mult)
    V.tensor_tensor(bv(bufB), bv(bufB), bcast(sc2_bf[:, 0:G], 64), ALU.mult)
    V.tensor_tensor(bufB[:, 0:F], bufB[:, 0:F], wb[:, 0:F], ALU.subtract)
    p.inc(V.tensor_tensor(wb[:, 0:F], bufB[:, 0:F], wb[:, 0:F], ALU.add), "eV")


def build_l0f():
    nc = bass.Bass("TRN2", target_bir_lowering=False)
    p = Ph(nc)

    x_own = nc.dram_tensor("x_own", [T, D], F32, kind="ExternalInput").ap()
    wgu = nc.dram_tensor("wgu", [HGU, D], F32, kind="ExternalInput").ap()
    wpr = nc.dram_tensor("wpr", [D, HH], F32, kind="ExternalInput").ap()
    ident = nc.dram_tensor("ident", [128, 128], F32, kind="ExternalInput").ap()
    x2_out = nc.dram_tensor("x2", [T, D], F32, kind="ExternalOutput").ap()
    wpt_d = nc.dram_tensor("wpt_d", [64, 16, 128, 128], BF16, kind="Internal")
    wpt = wpt_d.ap()
    DEBUG = globals().get("L0F_DEBUG", False)
    if DEBUG:
        dbg = {n: nc.dram_tensor(n, s, F32, kind="ExternalOutput").ap()
               for n, s in [("d_weffg", [128, 4096]), ("d_wgt", [128, 4096]),
                            ("d_gu", [128, 2048]), ("d_hid", [128, 2048]),
                            ("d_rms", [128, 16]), ("d_proj", [128, 2048]),
                            ("d_w2s", [128, 2048]), ("d_weffp", [128, 2048])]}

    A_ = nc.alloc_sbuf_tensor
    ident_bf = A_("ident_bf", [128, 128], BF16)
    ones_col16 = A_("ones_col16", [128, 1], F16)
    onesf32 = A_("onesf32", [1, 1], F32)
    eps_col = A_("eps_col", [128, 1], F32)
    inv64_col = A_("inv64_col", [128, 1], F32)
    thr_col = A_("thr_col", [128, 1], F32)
    epsR = A_("epsR", [1, 1], F32)
    inv8192c = A_("inv8192c", [1, 1], F32)
    gsum = A_("gsum", [128, 128], F32)
    scale_bf = A_("scale_bf", [128, 128], BF16)
    sc2_bf = A_("sc2_bf", [128, 128], BF16)
    thr_f32 = A_("thr_f32", [128, 128], F32)
    msq_sb = A_("msq_sb", [1, 1024], F32)
    rstd_row = msq_sb
    rstd_cols = A_("rstd_cols", [128, 8], F32)
    stats = {
        "ssum": A_("ssum", [128, NT], F32),
        "ssq": A_("ssq", [128, NT], F32),
        "m": A_("mcol", [128, 1], F32),
        "tmp": A_("tmpc", [128, 6], F32),
    }

    PS_BIG = nc.alloc_psum_tensor("PS_BIG", [128, 2048], F32)
    PS_A = nc.alloc_psum_tensor("PS_A", [128, 512], F32)
    PS_B = nc.alloc_psum_tensor("PS_B", [128, 512], F32)
    PS_C = nc.alloc_psum_tensor("PS_C", [128, 512], F32)
    PS_T = nc.alloc_psum_tensor("PS_T", [128, 1024], BF16)

    PE, V, Sc, G, SY = nc.tensor, nc.vector, nc.scalar, nc.gpsimd, nc.sync

    import contextlib
    with contextlib.ExitStack() as stk:
        ctx_body = stk.enter_context(nc.body("main"))
        p.gdma(ident_bf[:], ident)
        V.memset(ones_col16[:], 1.0)
        V.memset(onesf32[:], 1.0)
        V.memset(eps_col[:], LN_EPS)
        V.memset(inv64_col[:], INV64)
        V.memset(thr_col[:], THR)
        V.memset(epsR[:], RMS_EPS)
        p.inc(V.memset(inv8192c[:], INV8192), "eV")

        hid = stk.enter_context(nc.sbuf_tensor("hid", [128, 64 * 1024], BF16))
        with (nc.sbuf_tensor("wbP", [128, HH], BF16) as wbP,
              nc.sbuf_tensor("bufA", [128, HH], BF16) as bufA,
              nc.sbuf_tensor("bufB", [128, HH], BF16) as bufB,
              nc.sbuf_tensor("stage", [128, 1024], BF16) as stage):
            for dr in range(16):
                p.wait(G, "eV", "eA", "eP")
                p.gdma(wbP[:], wpr[dr * 128:(dr + 1) * 128, :])
                p.wait(V, "dW")
                p.wait(Sc, "dW")
                emit_quant(p, wbP, bufA, bufB, gsum, scale_bf, sc2_bf,
                           thr_f32, inv64_col, thr_col, HH // 64)
                if DEBUG and dr == 0:
                    p.wait(G, "eV")
                    p.gdma(dbg["d_weffp"], wbP[:, 0:2048])
                for half in range(8):
                    p.wait(PE, "eV")
                    for j in range(8):
                        hc = half * 8 + j
                        ins = PE.transpose(PS_T[:, j * 128:(j + 1) * 128],
                                           wbP[:, hc * 128:(hc + 1) * 128],
                                           ident_bf[:])
                        if j == 7:
                            p.inc(ins, "eP")
                    p.wait(V, "eP", "dW")
                    for j in range(8):
                        ins = V.tensor_copy(stage[:, j * 128:(j + 1) * 128],
                                            PS_T[:, j * 128:(j + 1) * 128])
                        if j == 7:
                            p.inc(ins, "eV")
                    p.wait(G, "eV")
                    p.gdma(wpt[half * 8:half * 8 + 8, dr].rearrange(
                               "j p f -> p j f"),
                           stage[:].rearrange("p (j f) -> p j f", f=128))

        with nc.sbuf_tensor("xf", [128, NC * T], BF16) as xf:
            with nc.sbuf_tensor("xot", [128, D], BF16) as xot:
                emit_stream_to_fm(p, lambda t: x_own[t * 128:(t + 1) * 128, :],
                                  xf, xot, PS_T, ident_bf)

            with (nc.sbuf_tensor("wbG", [128, 2 * D], BF16) as wbG,
                  nc.sbuf_tensor("bufAg", [128, 2 * D], BF16) as bufAg,
                  nc.sbuf_tensor("bufBg", [128, 2 * D], BF16) as bufBg,
                  nc.sbuf_tensor("wgT", [128, 2 * NC * 128], BF16) as wgT,
                  nc.sbuf_tensor("gsil", [128, 1024], BF16) as gsil):
                h2 = bufAg
                for hc in range(64):
                    p.wait(G, "eV", "eA", "eP")
                    p.gdma(wbG[:, 0:D], wgu[hc * 128:(hc + 1) * 128, :])
                    p.gdma(wbG[:, D:2 * D], wgu[(64 + hc) * 128:(65 + hc) * 128, :])
                    p.wait(V, "dW")
                    p.wait(Sc, "dW")
                    emit_quant(p, wbG, bufAg, bufBg, gsum, scale_bf, sc2_bf,
                               thr_f32, inv64_col, thr_col, 2 * D // 64)
                    for half in range(4):
                        p.wait(PE, "eV")
                        for j in range(8):
                            c = half * 8 + j
                            ins = PE.transpose(PS_T[:, j * 128:(j + 1) * 128],
                                               wbG[:, c * 128:(c + 1) * 128],
                                               ident_bf[:])
                            if j == 7:
                                p.inc(ins, "eP")
                        p.wait(V, "eP")
                        for j in range(8):
                            c = half * 8 + j
                            ins = V.tensor_copy(wgT[:, c * 128:(c + 1) * 128],
                                                PS_T[:, j * 128:(j + 1) * 128])
                            if j == 7:
                                p.inc(ins, "eV")
                    p.wait(PE, "eV", "eA")
                    for gu_i in range(2):
                        for ts in range(2):
                            dst = PS_BIG[:, gu_i * 1024 + ts * 512:
                                         gu_i * 1024 + (ts + 1) * 512]
                            for c in range(NC):
                                ins = PE.matmul(
                                    dst, wgT[:, (gu_i * NC + c) * 128:
                                             (gu_i * NC + c + 1) * 128],
                                    xf[:, c * T + ts * 512: c * T + ts * 512 + 512],
                                    start=(c == 0), stop=(c == NC - 1))
                    p.inc(ins, "eP")
                    if DEBUG and hc == 0:
                        p.wait(G, "eV")
                        p.gdma(dbg["d_weffg"], wbG[:])
                        p.gdma(dbg["d_wgt"], wgT[:])
                        if globals().get("L0F_DBG_LVL", 2) >= 3:
                            dgu = hid[:, 60 * 1024:64 * 1024].bitcast(F32)
                            p.wait(V, "eP")
                            p.inc(V.tensor_copy(dgu, PS_BIG[:]), "eV")
                            p.wait(G, "eV")
                            p.gdma(dbg["d_gu"], dgu)
                    p.wait(Sc, "eP")
                    p.inc(Sc.activation(gsil[:], PS_BIG[:, 0:1024], AF.Silu), "eA")
                    p.wait(V, "eA")
                    p.inc(V.tensor_tensor(hid[:, hc * 1024:(hc + 1) * 1024], gsil[:],
                                          PS_BIG[:, 1024:2048], ALU.mult), "eV")
                    p.wait(Sc, "eV")
                    p.inc(Sc.activation(h2[:, 0:1024], hid[:, hc * 1024:(hc + 1) * 1024],
                                        AF.Square), "eA")
                    p.wait(PE, "eA")
                    PE.matmul(PS_A[0:1, :], ones_col16[:], h2[:, 0:512],
                              start=(hc == 0), stop=(hc == 63), skip_group_check=True)
                    ins = PE.matmul(PS_B[0:1, :], ones_col16[:], h2[:, 512:1024],
                                   start=(hc == 0), stop=(hc == 63),
                                   skip_group_check=True)
                    p.inc(ins, "eP")

                if DEBUG:
                    p.wait(G, "eV")
                    p.gdma(dbg["d_hid"][:, 0:1024], hid[:, 0:1024])
                    p.gdma(dbg["d_hid"][:, 1024:2048], hid[:, 63 * 1024:64 * 1024])
                p.wait(V, "eP")
                V.tensor_copy(msq_sb[0:1, 0:512], PS_A[0:1, :])
                p.inc(V.tensor_copy(msq_sb[0:1, 512:1024], PS_B[0:1, :]), "eV")
                p.wait(Sc, "eV")
                p.inc(Sc.activation(rstd_row[0:1, :], msq_sb[0:1, :], AF.Sqrt,
                                    bias=epsR[:], scale=inv8192c[:]), "eA")
                p.wait(V, "eA")
                p.inc(V.reciprocal(rstd_row[0:1, :], rstd_row[0:1, :]), "eV")
                p.wait(PE, "eV")
                for j in range(NT):
                    ins = PE.matmul(PS_C[:, j:j + 1],
                                    rstd_row[0:1, j * 128:(j + 1) * 128],
                                    onesf32[0:1, 0:1], start=True, stop=True)
                p.inc(ins, "eP")
                p.wait(V, "eP")
                p.inc(V.tensor_copy(rstd_cols[:, 0:NT], PS_C[:, 0:NT]), "eV")

            if DEBUG and globals().get("L0F_DBG_LVL", 2) >= 2:
                p.wait(G, "eV")
                p.gdma(dbg["d_rms"][:, 0:8], rstd_cols[:])

        proj_bf = stk.enter_context(nc.sbuf_tensor("proj_bf", [128, NC * T], BF16))
        with nc.sbuf_tensor("w2s", [128, 64 * 128], BF16) as w2s:
            pp = [PS_A, PS_B]
            for dc in range(NC):
                p.wait(G, "eP")
                for qj in range(4):
                    p.gdma(w2s[:, qj * 16 * 128:(qj + 1) * 16 * 128].rearrange(
                               "p (j f) -> p j f", f=128),
                           wpt[qj * 16:(qj + 1) * 16, dc].rearrange(
                               "j p f -> p j f"))
                if DEBUG and globals().get("L0F_DBG_LVL", 2) >= 2 and dc == 0:
                    p.wait(G, "dW")
                    p.gdma(dbg["d_w2s"], w2s[:, 0:2048])
                p.wait(PE, "dW", "eV")
                for ts in range(2):
                    ps = pp[ts][:]
                    for hc in range(64):
                        ins = PE.matmul(ps, w2s[:, hc * 128:(hc + 1) * 128],
                                        hid[:, hc * 1024 + ts * 512:
                                            hc * 1024 + ts * 512 + 512],
                                        start=(hc == 0), stop=(hc == 63))
                    p.inc(ins, "eP")
                p.wait(V, "eP")
                for ts in range(2):
                    ins = V.tensor_copy(
                        proj_bf[:, dc * T + ts * 512: dc * T + (ts + 1) * 512],
                        pp[ts][:])
                p.inc(ins, "eV")

        if DEBUG and globals().get("L0F_DBG_LVL", 2) >= 2:
            p.wait(G, "eV")
            p.gdma(dbg["d_proj"][:, 0:1024], proj_bf[:, 0:1024])
            p.gdma(dbg["d_proj"][:, 1024:2048], proj_bf[:, 15 * 1024:16 * 1024])

        with (nc.sbuf_tensor("xproj", [128, D], BF16) as xproj,
              nc.sbuf_tensor("xres_g", [128, D], F32) as xres_t,
              nc.sbuf_tensor("xsum_g", [128, D], F32) as xsum_t):
            xres = xres_t[:]
            xsum = xsum_t[:]
            for tt in range(NT):
                p.wait(SY, "eV", "eP")
                p.sdma(xres[:], x_own[tt * 128:(tt + 1) * 128, :])
                for half in range(2):
                    p.wait(PE, "eV")
                    for j in range(8):
                        dc = half * 8 + j
                        ins = PE.transpose(
                            PS_T[:, j * 128:(j + 1) * 128],
                            proj_bf[:, dc * T + tt * 128: dc * T + (tt + 1) * 128],
                            ident_bf[:])
                        if j == 7:
                            p.inc(ins, "eP")
                    p.wait(V, "eP")
                    for j in range(8):
                        dc = half * 8 + j
                        ins = V.tensor_copy(xproj[:, dc * 128:(dc + 1) * 128],
                                            PS_T[:, j * 128:(j + 1) * 128])
                        if j == 7:
                            p.inc(ins, "eV")
                p.wait(V, "dX")
                emit_ln_tail(p, xproj, xres, xsum, xres, xres, stats, tt,
                             eps_col, rstd_extra=rstd_cols[:, tt:tt + 1])
                p.wait(SY, "eV")
                p.sdma(x2_out[tt * 128:(tt + 1) * 128, :], xres[:])
            p.wait(SY, "dX")
    return nc


def run_l0f(x1_full, ins):
    nc = build_l0f()
    iden = np.eye(128, dtype=np.float32)
    owns = shard_rows(x1_full)
    in_maps = [{"x_own": owns[c], "wgu": ins["L0_ffn_w_gate_up"],
                "wpr": ins["L0_ffn_w_proj"], "ident": iden} for c in range(8)]
    res = run_bass_kernel_spmd(nc, in_maps, core_ids=list(range(8)))
    return np.concatenate([r["x2"] for r in res.results], axis=0)





def build_l1a():
    nc = bass.Bass("TRN2", target_bir_lowering=False)
    p = Ph(nc)

    x_own = nc.dram_tensor("x_own", [T, D], F32, kind="ExternalInput").ap()
    x_kv = nc.dram_tensor("x_kv", [2 * T, D], F32, kind="ExternalInput").ap()
    Wq = nc.dram_tensor("Wq", [D, D], F32, kind="ExternalInput").ap()
    bq = nc.dram_tensor("bq", [D], F32, kind="ExternalInput").ap()
    Wk = nc.dram_tensor("Wk", [D, 512], F32, kind="ExternalInput").ap()
    bk = nc.dram_tensor("bk", [512], F32, kind="ExternalInput").ap()
    Wv = nc.dram_tensor("Wv", [D, 512], F32, kind="ExternalInput").ap()
    bv = nc.dram_tensor("bv", [512], F32, kind="ExternalInput").ap()
    Wo = nc.dram_tensor("Wo", [D, D], F32, kind="ExternalInput").ap()
    bo = nc.dram_tensor("bo", [D], F32, kind="ExternalInput").ap()
    ident = nc.dram_tensor("ident", [128, 128], F32, kind="ExternalInput").ap()
    x3_out = nc.dram_tensor("x3", [T, D], F32, kind="ExternalOutput").ap()

    A_ = nc.alloc_sbuf_tensor
    xf_own = A_("xf_own", [128, NC * T], BF16)
    xf_kv = A_("xf_kv", [128, NC * 2 * T], BF16)
    K_sb = A_("K_sb", [128, 2048], BF16)
    V_sb = A_("V_sb", [128, 2048], BF16)
    Q_sb = A_("Q_sb", [128, 1024], BF16)
    ctx_sb = A_("ctx", [128, NHEADS * T], BF16)
    s_tile = A_("s_tile", [128, 2048], F32)
    p_bf = A_("p_bf", [128, 2048], BF16)
    pT_sb = A_("pT_sb", [128, 2048], BF16)
    wq_sb = A_("wq_sb", [128, 16 * 128], BF16)
    wk_sb = A_("wk_sb", [128, 16 * 128], BF16)
    wv_sb = A_("wv_sb", [128, 16 * 128], BF16)
    ow_sb = [A_(f"ow{i}", [128, 2048], BF16) for i in range(2)]
    bq16 = A_("bq16", [1, 128], F16)
    bk16 = A_("bk16", [1, 128], F16)
    bv16 = A_("bv16", [1, 128], F16)
    ob16 = A_("ob16", [1, D], F16)
    ones16 = A_("ones16", [1, 512], F16)
    ident_bf = A_("ident_bf", [128, 128], BF16)
    xot = A_("xot", [128, D], BF16)
    xres_sb = A_("xres_sb", [128, D], F32)
    xsum_sb = A_("xsum_sb", [128, D], F32)
    xout_sb = xres_sb
    negmax = A_("negmax", [128, 1], F32)
    rowsum = A_("rowsum", [128, 1], F32)
    rcol = A_("rcol", [128, 1], F32)
    eps_col = A_("eps_col", [128, 1], F32)
    stats = {
        "ssum": A_("ssum", [128, NT], F32),
        "ssq": A_("ssq", [128, NT], F32),
        "m": A_("mcol", [128, 1], F32),
        "tmp": A_("tmpc", [128, 6], F32),
    }

    PS_BIG = nc.alloc_psum_tensor("PS_BIG", [128, 2048], F32)
    PS_A = nc.alloc_psum_tensor("PS_A", [128, 512], F32)
    PS_B = nc.alloc_psum_tensor("PS_B", [128, 512], F32)
    PS_C = nc.alloc_psum_tensor("PS_C", [128, 512], F32)
    PS_T = nc.alloc_psum_tensor("PS_T", [128, 1024], BF16)

    PE, V, Sc, G, SY = nc.tensor, nc.vector, nc.scalar, nc.gpsimd, nc.sync

    with nc.body("main"):
        p.gdma(ident_bf[:], ident)
        p.gdma(ob16[:], bo.rearrange("(o n) -> o n", o=1))
        V.memset(ones16[:], 1.0)
        p.inc(V.memset(eps_col[:], LN_EPS), "eV")

        emit_stream_to_fm(p, lambda t: x_own[t * 128:(t + 1) * 128, :],
                          xf_own, xot, PS_T, ident_bf)
        emit_stream_to_fm(p, lambda t: x_kv[t * 128:(t + 1) * 128, :],
                          xf_kv, xot, PS_T, ident_bf, ntiles=2 * NT)

        prod_ps = [PS_A, PS_B]
        npd = 0
        for g in range(4):
            p.wait(G, "eP")
            p.gdma(wk_sb[:].rearrange("p (c j) -> p c j", j=128),
                   Wk[:, g * 128:(g + 1) * 128].rearrange("(c p) j -> p c j", p=128))
            p.gdma(wv_sb[:].rearrange("p (c j) -> p c j", j=128),
                   Wv[:, g * 128:(g + 1) * 128].rearrange("(c p) j -> p c j", p=128))
            p.gdma(bk16[:], bk.rearrange("(o n) -> o n", o=1)[0:1, g * 128:(g + 1) * 128])
            p.gdma(bv16[:], bv.rearrange("(o n) -> o n", o=1)[0:1, g * 128:(g + 1) * 128])
            p.wait(PE, "dW", "eV")
            for kt in range(4):
                ps = prod_ps[npd % 2][:]
                npd += 1
                for c in range(NC):
                    PE.matmul(ps, wk_sb[:, c * 128:(c + 1) * 128],
                              xf_kv[:, c * 2 * T + kt * 512: c * 2 * T + (kt + 1) * 512],
                              start=(c == 0), stop=False)
                p.inc(PE.matmul(ps, bk16[0:1, :], ones16[0:1, :],
                                start=False, stop=True), "eP")
                p.wait(V, "eP")
                p.inc(V.tensor_copy(K_sb[:, kt * 512:(kt + 1) * 512], ps), "eV")
                p.wait(PE, "eV")
            for vt in range(16):
                ps = prod_ps[npd % 2][:]
                npd += 1
                for c in range(NC):
                    PE.matmul(ps[:, 0:128],
                              xf_kv[:, c * 2 * T + vt * 128: c * 2 * T + vt * 128 + 128],
                              wv_sb[:, c * 128:(c + 1) * 128],
                              start=(c == 0), stop=False)
                p.inc(PE.matmul(ps[:, 0:128], ones16[0:1, 0:128], bv16[0:1, :],
                                start=False, stop=True), "eP")
                p.wait(V, "eP")
                p.inc(V.tensor_copy(V_sb[:, vt * 128:(vt + 1) * 128], ps[:, 0:128]),
                      "eV")
                p.wait(PE, "eV")
            for hh in range(4):
                h = 4 * g + hh
                p.wait(G, "eP")
                p.gdma(wq_sb[:].rearrange("p (c j) -> p c j", j=128),
                       Wq[:, h * 128:(h + 1) * 128].rearrange("(c p) j -> p c j", p=128))
                p.gdma(bq16[:], bq.rearrange("(o n) -> o n", o=1)[0:1, h * 128:(h + 1) * 128])
                p.wait(PE, "dW", "eV")
                for qs in range(2):
                    ps = prod_ps[npd % 2][:]
                    npd += 1
                    for c in range(NC):
                        PE.matmul(ps, wq_sb[:, c * 128:(c + 1) * 128],
                                  xf_own[:, c * T + qs * 512: c * T + qs * 512 + 512],
                                  start=(c == 0), stop=False)
                    p.inc(PE.matmul(ps, bq16[0:1, :], ones16[0:1, :],
                                    start=False, stop=True), "eP")
                    p.wait(V, "eP")
                    p.inc(V.tensor_scalar_mul(Q_sb[:, qs * 512:(qs + 1) * 512], ps,
                                              QSCALE), "eV")
                    p.wait(PE, "eV")
                for qt in range(NT):
                    p.wait(PE, "eV", "eA")
                    for kt in range(4):
                        ins = PE.matmul(PS_BIG[:, kt * 512:(kt + 1) * 512],
                                        Q_sb[:, qt * 128:(qt + 1) * 128],
                                        K_sb[:, kt * 512:(kt + 1) * 512],
                                        start=True, stop=True)
                    p.inc(ins, "eP")
                    p.wait(V, "eP")
                    p.inc(V.tensor_reduce(negmax[:], PS_BIG[:], AX.XY,
                                          ALU.max, negate=True), "eV")
                    p.wait(Sc, "eV")
                    p.inc(Sc.activation(s_tile[:], PS_BIG[:], AF.Exp,
                                        bias=negmax[:], accum_out=rowsum[:]), "eA")
                    p.wait(V, "eA")
                    p.inc(V.reciprocal(rcol[:], rowsum[:]), "eV")
                    p.wait(Sc, "eV")
                    p.inc(Sc.activation(p_bf[:], s_tile[:], AF.Copy,
                                        scale=rcol[:]), "eA")
                    done = 0
                    while done < 16:
                        p.wait(PE, "eV", "eA")
                        for j in range(8):
                            kc = done + j
                            ins = PE.transpose(PS_T[:, j * 128:(j + 1) * 128],
                                               p_bf[:, kc * 128:(kc + 1) * 128],
                                               ident_bf[:])
                        p.inc(ins, "eP")
                        p.wait(V, "eP")
                        for j in range(8):
                            kc = done + j
                            ins = V.tensor_copy(pT_sb[:, kc * 128:(kc + 1) * 128],
                                                PS_T[:, j * 128:(j + 1) * 128])
                        p.inc(ins, "eV")
                        done += 8
                    p.wait(PE, "eV", "eA")
                    for kc in range(16):
                        ins = PE.matmul(PS_C[:, 0:128], V_sb[:, kc * 128:(kc + 1) * 128],
                                        pT_sb[:, kc * 128:(kc + 1) * 128],
                                        start=(kc == 0), stop=(kc == 15))
                    p.inc(ins, "eP")
                    p.wait(Sc, "eP")
                    p.inc(Sc.copy(ctx_sb[:, h * T + qt * 128: h * T + (qt + 1) * 128],
                                  PS_C[:, 0:128]), "eA")

        for tt in range(NT):
            p.wait(SY, "eV")
            p.sdma(xres_sb[:], x_own[tt * 128:(tt + 1) * 128, :])
            p.wait(PE, "eV")
            for hc in range(NC):
                p.wait(G, "eP")
                p.gdma(ow_sb[hc % 2][:], Wo[hc * 128:(hc + 1) * 128, :])
                p.wait(PE, "dW")
                for js in range(4):
                    ins = PE.matmul(PS_BIG[:, js * 512:(js + 1) * 512],
                                    ctx_sb[:, hc * T + tt * 128: hc * T + (tt + 1) * 128],
                                    ow_sb[hc % 2][:, js * 512:(js + 1) * 512],
                                    start=(hc == 0), stop=False)
                p.inc(ins, "eP")
            for js in range(4):
                ins = PE.matmul(PS_BIG[:, js * 512:(js + 1) * 512], ones16[0:1, 0:128],
                                ob16[0:1, js * 512:(js + 1) * 512], start=False,
                                stop=True)
            p.inc(ins, "eP")
            p.wait(V, "eP", "dX")
            emit_ln_tail(p, PS_BIG[:], xres_sb[:], xsum_sb[:], xout_sb[:], xres_sb[:],
                         stats, tt, eps_col)
            p.wait(SY, "eV")
            p.sdma(x3_out[tt * 128:(tt + 1) * 128, :], xout_sb[:])
        p.wait(SY, "dX")
    return nc


def kv_rows(x):
    out = []
    for c in range(8):
        b = c // 2
        out.append(np.ascontiguousarray(x[b * 2 * T:(b + 1) * 2 * T]))
    return out


def run_l1a(x2_full, ins):
    nc = build_l1a()
    iden = np.eye(128, dtype=np.float32)
    owns = shard_rows(x2_full)
    kvs = kv_rows(x2_full)
    in_maps = [{"x_own": owns[c], "x_kv": kvs[c],
                "Wq": ins["L1_attn_Wq"], "bq": ins["L1_attn_bq"],
                "Wk": ins["L1_attn_Wk"], "bk": ins["L1_attn_bk"],
                "Wv": ins["L1_attn_Wv"], "bv": ins["L1_attn_bv"],
                "Wo": ins["L1_attn_Wo"], "bo": ins["L1_attn_bo"],
                "ident": iden} for c in range(8)]
    res = run_bass_kernel_spmd(nc, in_maps, core_ids=list(range(8)))
    return np.concatenate([r["x3"] for r in res.results], axis=0)





def build_l1f():
    nc = bass.Bass("TRN2", target_bir_lowering=False)
    p = Ph(nc)

    x_own = nc.dram_tensor("x_own", [T, D], F32, kind="ExternalInput").ap()
    w1t = nc.dram_tensor("w1t", [D, HH], F32, kind="ExternalInput").ap()
    w3t = nc.dram_tensor("w3t", [D, HH], F32, kind="ExternalInput").ap()
    w2t = nc.dram_tensor("w2t", [HH, D], F32, kind="ExternalInput").ap()
    h128 = nc.dram_tensor("h128", [128, 128], F32, kind="ExternalInput").ap()
    ident = nc.dram_tensor("ident", [128, 128], F32, kind="ExternalInput").ap()
    x4_out = nc.dram_tensor("x4", [T, D], F32, kind="ExternalOutput").ap()

    A_ = nc.alloc_sbuf_tensor
    h128_16 = A_("h128_16", [128, 128], F16)
    ident_16 = A_("ident_16", [128, 128], F16)
    fsc_col = A_("fsc_col", [128, 1], F32)
    eps_col = A_("eps_col", [128, 1], F32)
    gsil = A_("gsil", [128, 1024], F16)
    tmp16 = A_("tmp16", [128, 1024], F16)
    xres_sb = A_("xres_sb", [128, D], F32)
    xsum_sb = A_("xsum_sb", [128, D], F32)
    xout_sb = xres_sb
    stats = {
        "ssum": A_("ssum", [128, NT], F32),
        "ssq": A_("ssq", [128, NT], F32),
        "m": A_("mcol", [128, 1], F32),
        "tmp": A_("tmpc", [128, 6], F32),
    }

    PS_BIG = nc.alloc_psum_tensor("PS_BIG", [128, 2048], F32)
    PS_A = nc.alloc_psum_tensor("PS_A", [128, 512], F32)
    PS_B = nc.alloc_psum_tensor("PS_B", [128, 512], F32)
    PS_C = nc.alloc_psum_tensor("PS_C", [128, 512], F32)
    PS_T = nc.alloc_psum_tensor("PS_T", [128, 1024], F16)

    PE, V, Sc, G, SY = nc.tensor, nc.vector, nc.scalar, nc.gpsimd, nc.sync

    import contextlib
    with contextlib.ExitStack() as stk:
        stk.enter_context(nc.body("main"))
        p.gdma(h128_16[:], h128)
        p.gdma(ident_16[:], ident)
        V.memset(fsc_col[:], FWHT_SCALE)
        p.inc(V.memset(eps_col[:], LN_EPS), "eV")

        hid = stk.enter_context(nc.sbuf_tensor("hid", [128, 64 * 1024], F16))

        with (nc.sbuf_tensor("xf", [128, NC * T], F16) as xf,
              nc.sbuf_tensor("xot", [128, D], F16) as xot,
              nc.sbuf_tensor("w1s", [128, 16 * 128], F16) as w1s,
              nc.sbuf_tensor("w3s", [128, 16 * 128], F16) as w3s):
            emit_stream_to_fm(p, lambda t: x_own[t * 128:(t + 1) * 128, :],
                              xf, xot, PS_T, ident_16)
            for hc in range(64):
                p.wait(G, "eP")
                p.gdma(w1s[:].rearrange("p (c j) -> p c j", j=128),
                       w1t[:, hc * 128:(hc + 1) * 128].rearrange(
                           "(c p) j -> p c j", p=128))
                p.gdma(w3s[:].rearrange("p (c j) -> p c j", j=128),
                       w3t[:, hc * 128:(hc + 1) * 128].rearrange(
                           "(c p) j -> p c j", p=128))
                p.wait(PE, "dW", "eV", "eA")
                for ws, off in ((w1s, 0), (w3s, 1024)):
                    for ts in range(2):
                        dst = PS_BIG[:, off + ts * 512: off + (ts + 1) * 512]
                        for c in range(NC):
                            ins = PE.matmul(
                                dst, ws[:, c * 128:(c + 1) * 128],
                                xf[:, c * T + ts * 512: c * T + ts * 512 + 512],
                                start=(c == 0), stop=(c == NC - 1))
                p.inc(ins, "eP")
                p.wait(Sc, "eP")
                p.inc(Sc.activation(gsil[:], PS_BIG[:, 0:1024], AF.Silu), "eA")
                p.wait(V, "eA")
                p.inc(V.tensor_tensor(hid[:, hc * 1024:(hc + 1) * 1024], gsil[:],
                                      PS_BIG[:, 1024:2048], ALU.mult), "eV")

        for bit in (1, 2, 4, 8, 16, 32):
            p.wait(V, "eV")
            for c in range(64):
                if c & bit:
                    continue
                a = hid[:, c * 1024:(c + 1) * 1024]
                b = hid[:, (c + bit) * 1024:(c + bit + 1) * 1024]
                V.tensor_tensor(tmp16[:], a, b, ALU.subtract)
                V.tensor_tensor(a, a, b, ALU.add)
                V.tensor_copy(b, tmp16[:])
            p.inc(V.tensor_copy(tmp16[:, 0:1], tmp16[:, 0:1]), "eV")

        for hc in range(64):
            p.wait(PE, "eV", "eA")
            for ts in range(2):
                ps = [PS_A, PS_B][ts][:]
                ins = PE.matmul(ps, h128_16[:],
                                hid[:, hc * 1024 + ts * 512: hc * 1024 + ts * 512 + 512],
                                start=True, stop=True)
            p.inc(ins, "eP")
            p.wait(Sc, "eP")
            for ts in range(2):
                ins = Sc.activation(
                    hid[:, hc * 1024 + ts * 512: hc * 1024 + ts * 512 + 512],
                    [PS_A, PS_B][ts][:], AF.Copy, scale=fsc_col[:])
            p.inc(ins, "eA")

        proj_16 = stk.enter_context(nc.sbuf_tensor("proj_16", [128, NC * T], F16))
        with nc.sbuf_tensor("w2s", [128, 64 * 128], F16) as w2s:
            pp = [PS_A, PS_B]
            for dc in range(NC):
                p.wait(G, "eP")
                for qj in range(4):
                    p.gdma(w2s[:, qj * 16 * 128:(qj + 1) * 16 * 128].rearrange(
                               "p (c j) -> p c j", j=128),
                           w2t[:, dc * 128:(dc + 1) * 128].rearrange(
                               "(c p) j -> p c j", p=128)[:, qj * 16:(qj + 1) * 16, :])
                p.wait(PE, "dW", "eV", "eA")
                for ts in range(2):
                    ps = pp[ts][:]
                    for hc in range(64):
                        ins = PE.matmul(ps, w2s[:, hc * 128:(hc + 1) * 128],
                                        hid[:, hc * 1024 + ts * 512:
                                            hc * 1024 + ts * 512 + 512],
                                        start=(hc == 0), stop=(hc == 63))
                    p.inc(ins, "eP")
                p.wait(V, "eP")
                for ts in range(2):
                    ins = V.tensor_copy(
                        proj_16[:, dc * T + ts * 512: dc * T + (ts + 1) * 512],
                        pp[ts][:])
                p.inc(ins, "eV")

        with nc.sbuf_tensor("xproj", [128, D], F16) as xproj:
            for tt in range(NT):
                p.wait(SY, "eV", "eP")
                p.sdma(xres_sb[:], x_own[tt * 128:(tt + 1) * 128, :])
                for half in range(2):
                    p.wait(PE, "eV")
                    for j in range(8):
                        dc = half * 8 + j
                        ins = PE.transpose(
                            PS_T[:, j * 128:(j + 1) * 128],
                            proj_16[:, dc * T + tt * 128: dc * T + (tt + 1) * 128],
                            ident_16[:])
                        if j == 7:
                            p.inc(ins, "eP")
                    p.wait(V, "eP")
                    for j in range(8):
                        dc = half * 8 + j
                        ins = V.tensor_copy(xproj[:, dc * 128:(dc + 1) * 128],
                                            PS_T[:, j * 128:(j + 1) * 128])
                        if j == 7:
                            p.inc(ins, "eV")
                p.wait(V, "dX")
                emit_ln_tail(p, xproj[:], xres_sb[:], xsum_sb[:], xout_sb[:],
                             xres_sb[:], stats, tt, eps_col)
                p.wait(SY, "eV")
                p.sdma(x4_out[tt * 128:(tt + 1) * 128, :], xout_sb[:])
            p.wait(SY, "dX")
    return nc


def make_h128():
    h = np.array([[1.0]], dtype=np.float32)
    while h.shape[0] < 128:
        h = np.block([[h, h], [h, -h]])
    return h.astype(np.float32)


def run_l1f(x3_full, ins):
    nc = build_l1f()
    iden = np.eye(128, dtype=np.float32)
    owns = shard_rows(x3_full)
    w1tt = np.ascontiguousarray(ins["L1_ffn_W1"].T)
    w3tt = np.ascontiguousarray(ins["L1_ffn_W3"].T)
    w2tt = np.ascontiguousarray(ins["L1_ffn_W2"].T)
    in_maps = [{"x_own": owns[c], "w1t": w1tt, "w3t": w3tt, "w2t": w2tt,
                "h128": make_h128(), "ident": iden} for c in range(8)]
    res = run_bass_kernel_spmd(nc, in_maps, core_ids=list(range(8)))
    return np.concatenate([r["x4"] for r in res.results], axis=0)


def kernel(**inputs):
    ins = {k: np.ascontiguousarray(np.asarray(v)) for k, v in inputs.items()}
    x0 = ins["L0_attn_x"].reshape(8192, 2048)
    x1 = run_l0a(x0, ins)
    x2 = run_l0f(x1, ins)
    x3 = run_l1a(x2, ins)
    x4 = run_l1f(x3, ins)
    return x4.reshape(4, 2048, 2048)


# version 03
# speedup vs baseline: 1.0011x; 1.0011x over previous
import contextlib
import math
import numpy as np
import ml_dtypes
import concourse.bass as bass
import concourse.mybir as mybir
from concourse.bass_utils import run_bass_kernel_spmd

F32 = mybir.dt.float32
BF16 = mybir.dt.bfloat16
F16 = mybir.dt.float16
U32 = mybir.dt.uint32
AF = mybir.ActivationFunctionType
ALU = mybir.AluOpType
AX = mybir.AxisListType

T = 1024
D = 2048
NT = T // 128
NC = D // 128
NHEADS = 16
HS = 128
HGU = 16384
HH = 8192
MASKVAL = -30000.0
LN_EPS = 1e-5
RMS_EPS = float(np.finfo(np.float32).eps)
INV2048 = 1.0 / 2048.0
INV64 = 1.0 / 64.0
INV8192 = 1.0 / 8192.0
QSCALE = 1.0 / math.sqrt(128.0)
FWHT_SCALE = 1.0 / math.sqrt(8192.0)
THR = 0.5009765625


def bcast(ap, n):
    return bass.AP(ap.tensor, ap.offset, [list(x) for x in ap.ap] + [[0, n]])


class Ph:
    def __init__(self, nc):
        self.nc = nc
        self.sem = {}
        self.cnt = {}
        for name in ("dW", "dX", "eP", "eV", "eA", "eG"):
            self.sem[name] = nc.alloc_semaphore(name)
            self.cnt[name] = 0

    def inc(self, instr, name, n=1):
        instr.then_inc(self.sem[name], n)
        self.cnt[name] += n
        return instr

    def wait(self, eng, *names):
        for name in names:
            if self.cnt[name] > 0:
                eng.wait_ge(self.sem[name], self.cnt[name])

    def barrier(self):
        nc = self.nc
        for eng in (nc.tensor, nc.vector, nc.scalar, nc.gpsimd, nc.sync):
            self.wait(eng, "dW", "dX", "eP", "eV", "eA")

    def gdma(self, out, in_):
        return self.inc(self.nc.gpsimd.dma_start(out=out, in_=in_), "dW", 16)

    def sdma(self, out, in_):
        return self.inc(self.nc.sync.dma_start(out=out, in_=in_), "dX", 16)



def emit_ln_tail(p, ps_big, xres_sb, xsum_sb, xout_sb, trash_f32, stats, tt,
                 eps_col, rstd_extra=None):
    V, A = p.nc.vector, p.nc.scalar
    m, tmp = stats["m"], stats["tmp"]
    if rstd_extra is None:
        ins = V.scalar_tensor_tensor(xsum_sb, ps_big, 0.0, xres_sb,
                                     ALU.add, ALU.add)
    else:
        ins = V.scalar_tensor_tensor(xsum_sb, ps_big, rstd_extra, xres_sb,
                                     ALU.mult, ALU.add)
    p.inc(ins, "eV")
    p.inc(V.tensor_reduce(stats["ssum"][:, tt:tt + 1], xsum_sb, AX.XY, ALU.add),
          "eV")
    p.wait(A, "eV")
    p.inc(A.activation(trash_f32, xsum_sb, AF.Square,
                       accum_out=stats["ssq"][:, tt:tt + 1]), "eA")
    p.wait(V, "eA")
    V.tensor_scalar_mul(m[:], stats["ssum"][:, tt:tt + 1], INV2048)
    p.inc(V.tensor_scalar_mul(tmp[:, 0:1], stats["ssq"][:, tt:tt + 1], INV2048),
          "eV")
    p.wait(A, "eV")
    p.inc(A.activation(tmp[:, 1:2], m[:], AF.Square), "eA")
    p.wait(V, "eA")
    p.inc(V.scalar_tensor_tensor(tmp[:, 2:3], tmp[:, 1:2], -1.0, tmp[:, 0:1],
                                 ALU.mult, ALU.add), "eV")
    p.wait(A, "eV")
    p.inc(A.activation(tmp[:, 3:4], tmp[:, 2:3], AF.Sqrt, bias=eps_col[:]), "eA")
    p.wait(V, "eA")
    p.inc(V.reciprocal(tmp[:, 4:5], tmp[:, 3:4]), "eV")
    p.wait(A, "eV")
    p.inc(A.copy(tmp[:, 5:6], tmp[:, 4:5]), "eA")
    p.wait(V, "eA")
    p.inc(V.tensor_scalar(xout_sb, xsum_sb, m[:], tmp[:, 5:6],
                          ALU.subtract, ALU.mult), "eV")


def emit_stream_to_fm(p, x_dram_rows, xf_sb, xot_sb, ps_t, ident_h, ntiles=NT):
    nc = p.nc
    PE, V = nc.tensor, nc.vector
    Tp = ntiles * 128
    for t in range(ntiles):
        buf = xot_sb[:, 0:D]
        p.wait(nc.gpsimd, "eP")
        p.gdma(buf, x_dram_rows(t))
        p.wait(PE, "dW")
        for half in range(2):
            p.wait(PE, "eV")
            for cc in range(8):
                c = half * 8 + cc
                ins = PE.transpose(ps_t[:, cc * 128:(cc + 1) * 128],
                                   buf[:, c * 128:(c + 1) * 128], ident_h[:])
                if cc == 7:
                    p.inc(ins, "eP")
            p.wait(V, "eP")
            for cc in range(8):
                c = half * 8 + cc
                ins = V.tensor_copy(xf_sb[:, c * Tp + t * 128: c * Tp + (t + 1) * 128],
                                    ps_t[:, cc * 128:(cc + 1) * 128])
                if cc == 7:
                    p.inc(ins, "eV")


def emit_quant(p, wb, bufA, bufB, gsum, scale_bf, sc2_bf, thr_f32,
               inv64_col, thr_col, G):
    V, A = p.nc.vector, p.nc.scalar
    F = G * 64
    bv = lambda t: t[:, 0:F].rearrange("p (g e) -> p g e", e=64)
    p.inc(A.activation(bufA[:, 0:F], wb[:, 0:F], AF.Abs), "eA")
    p.inc(A.activation(bufB[:, 0:F], wb[:, 0:F], AF.Sign), "eA")
    p.inc(V.tensor_reduce(gsum[:, 0:G], bv(wb), AX.X, ALU.add,
                          apply_absolute_value=True), "eV")
    p.wait(A, "eV")
    p.inc(A.activation(scale_bf[:, 0:G], gsum[:, 0:G], AF.Copy,
                       scale=inv64_col[:]), "eA")
    p.wait(V, "eA")
    p.inc(V.tensor_scalar_max(sc2_bf[:, 0:G], scale_bf[:, 0:G], 1e-8), "eV")
    p.wait(A, "eV")
    p.inc(A.activation(thr_f32[:, 0:G], sc2_bf[:, 0:G], AF.Copy,
                       scale=thr_col[:]), "eA")
    p.wait(V, "eA")
    V.tensor_tensor(bv(bufA), bv(bufA), bcast(thr_f32[:, 0:G], 64), ALU.is_gt)
    V.tensor_tensor(bufB[:, 0:F], bufA[:, 0:F], bufB[:, 0:F], ALU.mult)
    V.tensor_tensor(bv(bufB), bv(bufB), bcast(sc2_bf[:, 0:G], 64), ALU.mult)
    V.tensor_tensor(bufB[:, 0:F], bufB[:, 0:F], wb[:, 0:F], ALU.subtract)
    p.inc(V.tensor_tensor(wb[:, 0:F], bufB[:, 0:F], wb[:, 0:F], ALU.add), "eV")



def emit_l0a(nc, p, io):
    x_own, x_ctx = io["x_in"], io["x_ctx"]
    qkv_w, qkv_b = io["qkv_w"], io["qkv_b"]
    out_w, out_b = io["out_w"], io["out_b"]
    maskB, tri, ident = io["maskB"], io["tri"], io["ident"]
    x1_out = io["x_out"]

    PE, V, Sc, G, SY = nc.tensor, nc.vector, nc.scalar, nc.gpsimd, nc.sync
    p.barrier()
    with contextlib.ExitStack() as stk:
        E = lambda n, *a, **k: stk.enter_context(nc.sbuf_tensor("a0a0" + n, *a, **k))
        EP = lambda n, *a, **k: stk.enter_context(nc.psum_tensor("a0" + n, *a, **k))
        PS_BIG = EP("PS_BIG", [128, 2048], F32)
        PS_A = EP("PS_A", [128, 512], F32)
        PS_B = EP("PS_B", [128, 512], F32)
        PS_C = EP("PS_C", [128, 512], F32)
        PS_T = EP("PS_T", [128, 1024], BF16)

        xot = E("xot", [128, D], BF16)
        xf_own = E("xf_own", [128, NC * T], BF16)
        xf_ctx = E("xf_ctx", [128, NC * T], BF16)
        K_sb = [E(f"K{i}", [128, 2048], BF16) for i in range(2)]
        V_sb = [E(f"V{i}", [128, 2048], BF16) for i in range(2)]
        Q_sb = [E(f"Q{i}", [128, 1024], BF16) for i in range(2)]
        ctx_sb = E("ctx", [128, NHEADS * T], BF16)
        maskB_sb = E("maskB_sb", [128, 2048], F16)
        tri_bf = E("tri_bf", [128, 128], BF16)
        s_tile = E("s_tile", [128, 2048], F32)
        p_bf = E("p_bf", [128, 2048], BF16)
        pT_sb = E("pT_sb", [128, 2048], BF16)
        wq_sb = [E(f"wq{i}", [128, 16 * 128], BF16) for i in range(2)]
        wk_sb = [E(f"wk{i}", [128, 16 * 128], BF16) for i in range(2)]
        wv_sb = [E(f"wv{i}", [128, 16 * 128], BF16) for i in range(2)]
        ow_sb = [E(f"ow{i}", [128, 2048], BF16) for i in range(2)]
        qb16 = E("qb16", [1, 3 * D], F16)
        ob16 = E("ob16", [1, D], F16)
        ones16 = E("ones16", [1, 512], F16)
        ident_bf = E("ident_bf", [128, 128], BF16)
        xres_sb = E("xres_sb", [128, D], F32)
        xsum_sb = E("xsum_sb", [128, D], F32)
        xout_sb = xres_sb
        negmax = E("negmax", [128, 1], F32)
        rowsum = E("rowsum", [128, 1], F32)
        rcol = E("rcol", [128, 1], F32)
        eps_col = E("eps_col", [128, 1], F32)
        stats = {"ssum": E("ssum", [128, NT], F32),
                 "ssq": E("ssq", [128, NT], F32),
                 "m": E("mcol", [128, 1], F32),
                 "tmp": E("tmpc", [128, 6], F32)}

        p.sdma(maskB_sb[:], maskB)
        p.gdma(tri_bf[:], tri)
        p.gdma(qb16[:], qkv_b.rearrange("(o n) -> o n", o=1))
        p.gdma(ob16[:], out_b.rearrange("(o n) -> o n", o=1))
        p.gdma(ident_bf[:], ident)
        V.memset(ones16[:], 1.0)
        p.inc(V.memset(eps_col[:], LN_EPS), "eV")

        emit_stream_to_fm(p, lambda t: x_own[t * 128:(t + 1) * 128, :],
                          xf_own, xot, PS_T, ident_bf)
        emit_stream_to_fm(p, lambda t: x_ctx[t * 128:(t + 1) * 128, :],
                          xf_ctx, xot, PS_T, ident_bf)

        def xf_tokslice(kt512):
            src = xf_ctx if kt512 < 2 else xf_own
            off = (kt512 % 2) * 512
            return src, off

        prod_ps = [PS_A, PS_B]
        for g in range(NHEADS // 2):
            for hh in range(2):
                h = 2 * g + hh
                p.wait(G, "eP")
                for wsb, col0 in ((wq_sb[hh], h * HS), (wk_sb[hh], D + h * HS),
                                  (wv_sb[hh], 2 * D + h * HS)):
                    p.gdma(wsb[:].rearrange("p (c j) -> p c j", j=128),
                           qkv_w[:, col0:col0 + HS].rearrange("(c p) j -> p c j", p=128))
                p.wait(PE, "dW", "eV")
                npd = 0
                for kt in range(4):
                    ps = prod_ps[npd % 2][:]
                    npd += 1
                    src, off = xf_tokslice(kt)
                    for c in range(NC):
                        PE.matmul(ps, wk_sb[hh][:, c * 128:(c + 1) * 128],
                                  src[:, c * T + off: c * T + off + 512],
                                  start=(c == 0), stop=False)
                    p.inc(PE.matmul(ps, qb16[0:1, D + h * HS:D + (h + 1) * HS],
                                    ones16[0:1, :], start=False, stop=True), "eP")
                    p.wait(V, "eP")
                    p.inc(V.tensor_copy(K_sb[hh][:, kt * 512:(kt + 1) * 512], ps), "eV")
                    p.wait(PE, "eV")
                for vt in range(16):
                    ps = prod_ps[npd % 2][:]
                    npd += 1
                    src = xf_ctx if vt < 8 else xf_own
                    toff = (vt % 8) * 128
                    for c in range(NC):
                        PE.matmul(ps[:, 0:128],
                                  src[:, c * T + toff: c * T + toff + 128],
                                  wv_sb[hh][:, c * 128:(c + 1) * 128],
                                  start=(c == 0), stop=False)
                    p.inc(PE.matmul(ps[:, 0:128], ones16[0:1, 0:128],
                                    qb16[0:1, 2 * D + h * HS:2 * D + (h + 1) * HS],
                                    start=False, stop=True), "eP")
                    p.wait(V, "eP")
                    p.inc(V.tensor_copy(V_sb[hh][:, vt * 128:(vt + 1) * 128],
                                        ps[:, 0:128]), "eV")
                    p.wait(PE, "eV")
                for qs in range(2):
                    ps = prod_ps[npd % 2][:]
                    npd += 1
                    for c in range(NC):
                        PE.matmul(ps, wq_sb[hh][:, c * 128:(c + 1) * 128],
                                  xf_own[:, c * T + qs * 512: c * T + qs * 512 + 512],
                                  start=(c == 0), stop=False)
                    p.inc(PE.matmul(ps, qb16[0:1, h * HS:(h + 1) * HS],
                                    ones16[0:1, :], start=False, stop=True), "eP")
                    p.wait(V, "eP")
                    p.inc(V.tensor_copy(Q_sb[hh][:, qs * 512:(qs + 1) * 512], ps), "eV")
                    p.wait(PE, "eV")

                slope = 1.0 / 2.0 ** (h / 2.0)
                for qt in range(NT):
                    nk = 3 if qt < 4 else 4
                    ve = 1024 + (qt + 1) * 128
                    kch = ve // 128
                    p.wait(PE, "eV")
                    dj = 1024 + qt * 128
                    w = nk * 512
                    for kt in range(nk):
                        dslc = PS_BIG[:, kt * 512:(kt + 1) * 512]
                        PE.matmul(dslc, Q_sb[hh][:, qt * 128:(qt + 1) * 128],
                                  K_sb[hh][:, kt * 512:(kt + 1) * 512],
                                  start=True, stop=(dj // 512 != kt))
                    p.inc(PE.matmul(PS_BIG[:, dj:dj + 128], ident_bf[:], tri_bf[:],
                                    start=False, stop=True), "eP")
                    p.wait(V, "eP", "eA")
                    if ve < w:
                        V.memset(s_tile[:, ve:w], -1e30)
                    V.scalar_tensor_tensor(s_tile[:, 0:ve], maskB_sb[:, 0:ve], slope,
                                           PS_BIG[:, 0:ve], ALU.mult, ALU.add)
                    p.inc(V.tensor_reduce(negmax[:], s_tile[:, 0:w], AX.XY,
                                          ALU.max, negate=True), "eV")
                    p.wait(Sc, "eV")
                    p.inc(Sc.activation(s_tile[:, 0:w], s_tile[:, 0:w], AF.Exp,
                                        bias=negmax[:], accum_out=rowsum[:]), "eA")
                    p.wait(V, "eA")
                    p.inc(V.reciprocal(rcol[:], rowsum[:]), "eV")
                    p.wait(Sc, "eV")
                    p.inc(Sc.activation(p_bf[:, 0:ve], s_tile[:, 0:ve], AF.Copy,
                                        scale=rcol[:]), "eA")
                    done = 0
                    while done < kch:
                        bn = min(8, kch - done)
                        p.wait(PE, "eV", "eA")
                        for j in range(bn):
                            kc = done + j
                            ins = PE.transpose(PS_T[:, j * 128:(j + 1) * 128],
                                               p_bf[:, kc * 128:(kc + 1) * 128],
                                               ident_bf[:])
                            if j == bn - 1:
                                p.inc(ins, "eP")
                        p.wait(V, "eP")
                        for j in range(bn):
                            kc = done + j
                            ins = V.tensor_copy(pT_sb[:, kc * 128:(kc + 1) * 128],
                                                PS_T[:, j * 128:(j + 1) * 128])
                            if j == bn - 1:
                                p.inc(ins, "eV")
                        done += bn
                    p.wait(PE, "eV", "eA")
                    for kc in range(kch):
                        ins = PE.matmul(PS_C[:, 0:128],
                                        V_sb[hh][:, kc * 128:(kc + 1) * 128],
                                        pT_sb[:, kc * 128:(kc + 1) * 128],
                                        start=(kc == 0), stop=(kc == kch - 1))
                    p.inc(ins, "eP")
                    p.wait(Sc, "eP")
                    p.inc(Sc.copy(ctx_sb[:, h * T + qt * 128: h * T + (qt + 1) * 128],
                                  PS_C[:, 0:128]), "eA")

        for tt in range(NT):
            p.wait(SY, "eV")
            p.sdma(xres_sb[:], x_own[tt * 128:(tt + 1) * 128, :])
            p.wait(PE, "eV")
            for hc in range(NC):
                p.wait(G, "eP")
                p.gdma(ow_sb[hc % 2][:], out_w[hc * 128:(hc + 1) * 128, :])
                p.wait(PE, "dW")
                for js in range(4):
                    ins = PE.matmul(PS_BIG[:, js * 512:(js + 1) * 512],
                                    ctx_sb[:, hc * T + tt * 128: hc * T + (tt + 1) * 128],
                                    ow_sb[hc % 2][:, js * 512:(js + 1) * 512],
                                    start=(hc == 0), stop=False)
                p.inc(ins, "eP")
            for js in range(4):
                ins = PE.matmul(PS_BIG[:, js * 512:(js + 1) * 512], ones16[0:1, 0:128],
                                ob16[0:1, js * 512:(js + 1) * 512], start=False,
                                stop=True)
            p.inc(ins, "eP")
            p.wait(V, "eP", "dX")
            emit_ln_tail(p, PS_BIG[:], xres_sb[:], xsum_sb[:], xout_sb[:],
                         xres_sb[:], stats, tt, eps_col)
            p.wait(SY, "eV")
            p.sdma(x1_out[tt * 128:(tt + 1) * 128, :], xout_sb[:])
        p.wait(SY, "dX")


def emit_l0f(nc, p, io):
    x_own, wgu, wpr, ident = io["x_in"], io["wgu"], io["wpr"], io["ident"]
    x2_out, wpt = io["x_out"], io["wpt"]

    PE, V, Sc, G, SY = nc.tensor, nc.vector, nc.scalar, nc.gpsimd, nc.sync
    p.barrier()
    with contextlib.ExitStack() as stk:
        E = lambda n, *a, **k: stk.enter_context(nc.sbuf_tensor("f0f0" + n, *a, **k))
        EP = lambda n, *a, **k: stk.enter_context(nc.psum_tensor("f0" + n, *a, **k))
        PS_BIG = EP("PS_BIG", [128, 2048], F32)
        PS_A = EP("PS_A", [128, 512], F32)
        PS_B = EP("PS_B", [128, 512], F32)
        PS_C = EP("PS_C", [128, 512], F32)
        PS_T = EP("PS_T", [128, 1024], BF16)

        ident_bf = E("ident_bf", [128, 128], BF16)
        ones_col16 = E("ones_col16", [128, 1], F16)
        onesf32 = E("onesf32", [1, 1], F32)
        eps_col = E("eps_col", [128, 1], F32)
        inv64_col = E("inv64_col", [128, 1], F32)
        thr_col = E("thr_col", [128, 1], F32)
        epsR = E("epsR", [1, 1], F32)
        inv8192c = E("inv8192c", [1, 1], F32)
        gsum = E("gsum", [128, 128], F32)
        scale_bf = E("scale_bf", [128, 128], BF16)
        sc2_bf = E("sc2_bf", [128, 128], BF16)
        thr_f32 = E("thr_f32", [128, 128], F32)
        msq_sb = E("msq_sb", [1, 1024], F32)
        rstd_row = msq_sb
        rstd_cols = E("rstd_cols", [128, 8], F32)
        stats = {"ssum": E("ssum", [128, NT], F32),
                 "ssq": E("ssq", [128, NT], F32),
                 "m": E("mcol", [128, 1], F32),
                 "tmp": E("tmpc", [128, 6], F32)}

        p.gdma(ident_bf[:], ident)
        V.memset(ones_col16[:], 1.0)
        V.memset(onesf32[:], 1.0)
        V.memset(eps_col[:], LN_EPS)
        V.memset(inv64_col[:], INV64)
        V.memset(thr_col[:], THR)
        V.memset(epsR[:], RMS_EPS)
        p.inc(V.memset(inv8192c[:], INV8192), "eV")

        hid = E("hid", [128, 64 * 1024], BF16)

        with (nc.sbuf_tensor("f0wbP", [128, HH], BF16) as wbP,
              nc.sbuf_tensor("f0bufA", [128, HH], BF16) as bufA,
              nc.sbuf_tensor("f0bufB", [128, HH], BF16) as bufB,
              nc.sbuf_tensor("f0stage", [128, 1024], BF16) as stage):
            for dr in range(16):
                p.wait(G, "eV", "eA", "eP")
                p.gdma(wbP[:], wpr[dr * 128:(dr + 1) * 128, :])
                p.wait(V, "dW")
                p.wait(Sc, "dW")
                emit_quant(p, wbP, bufA, bufB, gsum, scale_bf, sc2_bf,
                           thr_f32, inv64_col, thr_col, HH // 64)
                for half in range(8):
                    p.wait(PE, "eV")
                    for j in range(8):
                        hc = half * 8 + j
                        ins = PE.transpose(PS_T[:, j * 128:(j + 1) * 128],
                                           wbP[:, hc * 128:(hc + 1) * 128],
                                           ident_bf[:])
                        if j == 7:
                            p.inc(ins, "eP")
                    p.wait(V, "eP", "dW")
                    for j in range(8):
                        ins = V.tensor_copy(stage[:, j * 128:(j + 1) * 128],
                                            PS_T[:, j * 128:(j + 1) * 128])
                        if j == 7:
                            p.inc(ins, "eV")
                    p.wait(G, "eV")
                    p.gdma(wpt[half * 8:half * 8 + 8, dr].rearrange(
                               "j p f -> p j f"),
                           stage[:].rearrange("p (j f) -> p j f", f=128))

        with (nc.sbuf_tensor("f0xf", [128, NC * T], BF16) as xf,
              nc.sbuf_tensor("f0xot", [128, D], BF16) as xot):
            emit_stream_to_fm(p, lambda t: x_own[t * 128:(t + 1) * 128, :],
                              xf, xot, PS_T, ident_bf)
            with (nc.sbuf_tensor("f0wbG", [128, 2 * D], BF16) as wbG,
                  nc.sbuf_tensor("f0bufAg", [128, 2 * D], BF16) as bufAg,
                  nc.sbuf_tensor("f0bufBg", [128, 2 * D], BF16) as bufBg,
                  nc.sbuf_tensor("f0wgT", [128, 2 * NC * 128], BF16) as wgT,
                  nc.sbuf_tensor("f0gsil", [128, 1024], BF16) as gsil):
                h2 = bufAg
                for hc in range(64):
                    p.wait(G, "eV", "eA", "eP")
                    p.gdma(wbG[:, 0:D], wgu[hc * 128:(hc + 1) * 128, :])
                    p.gdma(wbG[:, D:2 * D], wgu[(64 + hc) * 128:(65 + hc) * 128, :])
                    p.wait(V, "dW")
                    p.wait(Sc, "dW")
                    emit_quant(p, wbG, bufAg, bufBg, gsum, scale_bf, sc2_bf,
                               thr_f32, inv64_col, thr_col, 2 * D // 64)
                    for half in range(4):
                        p.wait(PE, "eV")
                        for j in range(8):
                            c = half * 8 + j
                            ins = PE.transpose(PS_T[:, j * 128:(j + 1) * 128],
                                               wbG[:, c * 128:(c + 1) * 128],
                                               ident_bf[:])
                            if j == 7:
                                p.inc(ins, "eP")
                        p.wait(V, "eP")
                        for j in range(8):
                            c = half * 8 + j
                            ins = V.tensor_copy(wgT[:, c * 128:(c + 1) * 128],
                                                PS_T[:, j * 128:(j + 1) * 128])
                            if j == 7:
                                p.inc(ins, "eV")
                    p.wait(PE, "eV", "eA")
                    for gu_i in range(2):
                        for ts in range(2):
                            dst = PS_BIG[:, gu_i * 1024 + ts * 512:
                                         gu_i * 1024 + (ts + 1) * 512]
                            for c in range(NC):
                                ins = PE.matmul(
                                    dst, wgT[:, (gu_i * NC + c) * 128:
                                             (gu_i * NC + c + 1) * 128],
                                    xf[:, c * T + ts * 512: c * T + ts * 512 + 512],
                                    start=(c == 0), stop=(c == NC - 1))
                    p.inc(ins, "eP")
                    p.wait(Sc, "eP")
                    p.inc(Sc.activation(gsil[:], PS_BIG[:, 0:1024], AF.Silu), "eA")
                    p.wait(V, "eA")
                    p.inc(V.tensor_tensor(hid[:, hc * 1024:(hc + 1) * 1024], gsil[:],
                                          PS_BIG[:, 1024:2048], ALU.mult), "eV")
                    p.wait(Sc, "eV")
                    p.inc(Sc.activation(h2[:, 0:1024],
                                        hid[:, hc * 1024:(hc + 1) * 1024],
                                        AF.Square), "eA")
                    p.wait(PE, "eA")
                    PE.matmul(PS_A[0:1, :], ones_col16[:], h2[:, 0:512],
                              start=(hc == 0), stop=(hc == 63),
                              skip_group_check=True)
                    ins = PE.matmul(PS_B[0:1, :], ones_col16[:], h2[:, 512:1024],
                                    start=(hc == 0), stop=(hc == 63),
                                    skip_group_check=True)
                    p.inc(ins, "eP")

                p.wait(V, "eP")
                V.tensor_copy(msq_sb[0:1, 0:512], PS_A[0:1, :])
                p.inc(V.tensor_copy(msq_sb[0:1, 512:1024], PS_B[0:1, :]), "eV")
                p.wait(Sc, "eV")
                p.inc(Sc.activation(rstd_row[0:1, :], msq_sb[0:1, :], AF.Sqrt,
                                    bias=epsR[:], scale=inv8192c[:]), "eA")
                p.wait(V, "eA")
                p.inc(V.reciprocal(rstd_row[0:1, :], rstd_row[0:1, :]), "eV")
                p.wait(PE, "eV")
                for j in range(NT):
                    ins = PE.matmul(PS_C[:, j:j + 1],
                                    rstd_row[0:1, j * 128:(j + 1) * 128],
                                    onesf32[0:1, 0:1], start=True, stop=True)
                p.inc(ins, "eP")
                p.wait(V, "eP")
                p.inc(V.tensor_copy(rstd_cols[:, 0:NT], PS_C[:, 0:NT]), "eV")

        proj_bf = E("proj_bf", [128, NC * T], BF16)
        with nc.sbuf_tensor("f0w2s", [128, 64 * 128], BF16) as w2s:
            pp = [PS_A, PS_B]
            for dc in range(NC):
                p.wait(G, "eP")
                for qj in range(4):
                    p.gdma(w2s[:, qj * 16 * 128:(qj + 1) * 16 * 128].rearrange(
                               "p (j f) -> p j f", f=128),
                           wpt[qj * 16:(qj + 1) * 16, dc].rearrange(
                               "j p f -> p j f"))
                p.wait(PE, "dW", "eV")
                for ts in range(2):
                    ps = pp[ts][:]
                    for hc in range(64):
                        ins = PE.matmul(ps, w2s[:, hc * 128:(hc + 1) * 128],
                                        hid[:, hc * 1024 + ts * 512:
                                            hc * 1024 + ts * 512 + 512],
                                        start=(hc == 0), stop=(hc == 63))
                    p.inc(ins, "eP")
                p.wait(V, "eP")
                for ts in range(2):
                    ins = V.tensor_copy(
                        proj_bf[:, dc * T + ts * 512: dc * T + (ts + 1) * 512],
                        pp[ts][:])
                p.inc(ins, "eV")

        with (nc.sbuf_tensor("f0xproj", [128, D], BF16) as xproj,
              nc.sbuf_tensor("f0xres_g", [128, D], F32) as xres_t,
              nc.sbuf_tensor("f0xsum_g", [128, D], F32) as xsum_t):
            xres = xres_t[:]
            xsum = xsum_t[:]
            for tt in range(NT):
                p.wait(SY, "eV", "eP")
                p.sdma(xres, x_own[tt * 128:(tt + 1) * 128, :])
                for half in range(2):
                    p.wait(PE, "eV")
                    for j in range(8):
                        dc = half * 8 + j
                        ins = PE.transpose(
                            PS_T[:, j * 128:(j + 1) * 128],
                            proj_bf[:, dc * T + tt * 128: dc * T + (tt + 1) * 128],
                            ident_bf[:])
                        if j == 7:
                            p.inc(ins, "eP")
                    p.wait(V, "eP")
                    for j in range(8):
                        dc = half * 8 + j
                        ins = V.tensor_copy(xproj[:, dc * 128:(dc + 1) * 128],
                                            PS_T[:, j * 128:(j + 1) * 128])
                        if j == 7:
                            p.inc(ins, "eV")
                p.wait(V, "dX")
                emit_ln_tail(p, xproj[:], xres, xsum, xres, xres, stats, tt,
                             eps_col, rstd_extra=rstd_cols[:, tt:tt + 1])
                p.wait(SY, "eV")
                p.sdma(x2_out[tt * 128:(tt + 1) * 128, :], xres)
            p.wait(SY, "dX")


def emit_l1a(nc, p, io):
    x_own, x_kv = io["x_in"], io["x_kv"]
    Wq, bq, Wk, bk, Wv, bv, Wo, bo = (io[k] for k in
                                      ("Wq", "bq", "Wk", "bk", "Wv", "bv",
                                       "Wo", "bo"))
    ident, x3_out = io["ident"], io["x_out"]
    PE, V, Sc, G, SY = nc.tensor, nc.vector, nc.scalar, nc.gpsimd, nc.sync
    p.barrier()
    with contextlib.ExitStack() as stk:
        E = lambda n, *a, **k: stk.enter_context(nc.sbuf_tensor("a1a1" + n, *a, **k))
        EP = lambda n, *a, **k: stk.enter_context(nc.psum_tensor("a1" + n, *a, **k))
        PS_BIG = EP("PS_BIG", [128, 2048], F32)
        PS_A = EP("PS_A", [128, 512], F32)
        PS_B = EP("PS_B", [128, 512], F32)
        PS_C = EP("PS_C", [128, 512], F32)
        PS_T = EP("PS_T", [128, 1024], BF16)

        xf_own = E("xf_own", [128, NC * T], BF16)
        xf_kv = E("xf_kv", [128, NC * 2 * T], BF16)
        K_sb = E("K_sb", [128, 2048], BF16)
        V_sb = E("V_sb", [128, 2048], BF16)
        Q_sb = E("Q_sb", [128, 1024], BF16)
        ctx_sb = E("ctx", [128, NHEADS * T], BF16)
        s_tile = E("s_tile", [128, 2048], F32)
        p_bf = E("p_bf", [128, 2048], BF16)
        pT_sb = E("pT_sb", [128, 2048], BF16)
        wq_sb = E("wq_sb", [128, 16 * 128], BF16)
        wk_sb = E("wk_sb", [128, 16 * 128], BF16)
        wv_sb = E("wv_sb", [128, 16 * 128], BF16)
        ow_sb = [E(f"ow{i}", [128, 2048], BF16) for i in range(2)]
        bq16 = E("bq16", [1, 128], F16)
        bk16 = E("bk16", [1, 128], F16)
        bv16 = E("bv16", [1, 128], F16)
        ob16 = E("ob16", [1, D], F16)
        ones16 = E("ones16", [1, 512], F16)
        ident_bf = E("ident_bf", [128, 128], BF16)
        xot = E("xot", [128, D], BF16)
        xres_sb = E("xres_sb", [128, D], F32)
        xsum_sb = E("xsum_sb", [128, D], F32)
        xout_sb = xres_sb
        negmax = E("negmax", [128, 1], F32)
        rowsum = E("rowsum", [128, 1], F32)
        rcol = E("rcol", [128, 1], F32)
        eps_col = E("eps_col", [128, 1], F32)
        stats = {"ssum": E("ssum", [128, NT], F32),
                 "ssq": E("ssq", [128, NT], F32),
                 "m": E("mcol", [128, 1], F32),
                 "tmp": E("tmpc", [128, 6], F32)}

        p.gdma(ident_bf[:], ident)
        p.gdma(ob16[:], bo.rearrange("(o n) -> o n", o=1))
        V.memset(ones16[:], 1.0)
        p.inc(V.memset(eps_col[:], LN_EPS), "eV")

        emit_stream_to_fm(p, lambda t: x_own[t * 128:(t + 1) * 128, :],
                          xf_own, xot, PS_T, ident_bf)
        emit_stream_to_fm(p, lambda t: x_kv[t * 128:(t + 1) * 128, :],
                          xf_kv, xot, PS_T, ident_bf, ntiles=2 * NT)

        prod_ps = [PS_A, PS_B]
        npd = 0
        for g in range(4):
            p.wait(G, "eP")
            p.gdma(wk_sb[:].rearrange("p (c j) -> p c j", j=128),
                   Wk[:, g * 128:(g + 1) * 128].rearrange("(c p) j -> p c j", p=128))
            p.gdma(wv_sb[:].rearrange("p (c j) -> p c j", j=128),
                   Wv[:, g * 128:(g + 1) * 128].rearrange("(c p) j -> p c j", p=128))
            p.gdma(bk16[:], bk.rearrange("(o n) -> o n", o=1)[0:1, g * 128:(g + 1) * 128])
            p.gdma(bv16[:], bv.rearrange("(o n) -> o n", o=1)[0:1, g * 128:(g + 1) * 128])
            p.wait(PE, "dW", "eV")
            for kt in range(4):
                ps = prod_ps[npd % 2][:]
                npd += 1
                for c in range(NC):
                    PE.matmul(ps, wk_sb[:, c * 128:(c + 1) * 128],
                              xf_kv[:, c * 2 * T + kt * 512: c * 2 * T + (kt + 1) * 512],
                              start=(c == 0), stop=False)
                p.inc(PE.matmul(ps, bk16[0:1, :], ones16[0:1, :],
                                start=False, stop=True), "eP")
                p.wait(V, "eP")
                p.inc(V.tensor_copy(K_sb[:, kt * 512:(kt + 1) * 512], ps), "eV")
                p.wait(PE, "eV")
            for vt in range(16):
                ps = prod_ps[npd % 2][:]
                npd += 1
                for c in range(NC):
                    PE.matmul(ps[:, 0:128],
                              xf_kv[:, c * 2 * T + vt * 128: c * 2 * T + vt * 128 + 128],
                              wv_sb[:, c * 128:(c + 1) * 128],
                              start=(c == 0), stop=False)
                p.inc(PE.matmul(ps[:, 0:128], ones16[0:1, 0:128], bv16[0:1, :],
                                start=False, stop=True), "eP")
                p.wait(V, "eP")
                p.inc(V.tensor_copy(V_sb[:, vt * 128:(vt + 1) * 128], ps[:, 0:128]),
                      "eV")
                p.wait(PE, "eV")
            for hh in range(4):
                h = 4 * g + hh
                p.wait(G, "eP")
                p.gdma(wq_sb[:].rearrange("p (c j) -> p c j", j=128),
                       Wq[:, h * 128:(h + 1) * 128].rearrange("(c p) j -> p c j", p=128))
                p.gdma(bq16[:], bq.rearrange("(o n) -> o n", o=1)[0:1, h * 128:(h + 1) * 128])
                p.wait(PE, "dW", "eV")
                for qs in range(2):
                    ps = prod_ps[npd % 2][:]
                    npd += 1
                    for c in range(NC):
                        PE.matmul(ps, wq_sb[:, c * 128:(c + 1) * 128],
                                  xf_own[:, c * T + qs * 512: c * T + qs * 512 + 512],
                                  start=(c == 0), stop=False)
                    p.inc(PE.matmul(ps, bq16[0:1, :], ones16[0:1, :],
                                    start=False, stop=True), "eP")
                    p.wait(V, "eP")
                    p.inc(V.tensor_scalar_mul(Q_sb[:, qs * 512:(qs + 1) * 512], ps,
                                              QSCALE), "eV")
                    p.wait(PE, "eV")
                for qt in range(NT):
                    p.wait(PE, "eV", "eA")
                    for kt in range(4):
                        ins = PE.matmul(PS_BIG[:, kt * 512:(kt + 1) * 512],
                                        Q_sb[:, qt * 128:(qt + 1) * 128],
                                        K_sb[:, kt * 512:(kt + 1) * 512],
                                        start=True, stop=True)
                    p.inc(ins, "eP")
                    p.wait(V, "eP")
                    p.inc(V.tensor_reduce(negmax[:], PS_BIG[:], AX.XY,
                                          ALU.max, negate=True), "eV")
                    p.wait(Sc, "eV")
                    p.inc(Sc.activation(s_tile[:], PS_BIG[:], AF.Exp,
                                        bias=negmax[:], accum_out=rowsum[:]), "eA")
                    p.wait(V, "eA")
                    p.inc(V.reciprocal(rcol[:], rowsum[:]), "eV")
                    p.wait(Sc, "eV")
                    p.inc(Sc.activation(p_bf[:], s_tile[:], AF.Copy,
                                        scale=rcol[:]), "eA")
                    done = 0
                    while done < 16:
                        p.wait(PE, "eV", "eA")
                        for j in range(8):
                            kc = done + j
                            ins = PE.transpose(PS_T[:, j * 128:(j + 1) * 128],
                                               p_bf[:, kc * 128:(kc + 1) * 128],
                                               ident_bf[:])
                        p.inc(ins, "eP")
                        p.wait(V, "eP")
                        for j in range(8):
                            kc = done + j
                            ins = V.tensor_copy(pT_sb[:, kc * 128:(kc + 1) * 128],
                                                PS_T[:, j * 128:(j + 1) * 128])
                        p.inc(ins, "eV")
                        done += 8
                    p.wait(PE, "eV", "eA")
                    for kc in range(16):
                        ins = PE.matmul(PS_C[:, 0:128], V_sb[:, kc * 128:(kc + 1) * 128],
                                        pT_sb[:, kc * 128:(kc + 1) * 128],
                                        start=(kc == 0), stop=(kc == 15))
                    p.inc(ins, "eP")
                    p.wait(Sc, "eP")
                    p.inc(Sc.copy(ctx_sb[:, h * T + qt * 128: h * T + (qt + 1) * 128],
                                  PS_C[:, 0:128]), "eA")

        for tt in range(NT):
            p.wait(SY, "eV")
            p.sdma(xres_sb[:], x_own[tt * 128:(tt + 1) * 128, :])
            p.wait(PE, "eV")
            for hc in range(NC):
                p.wait(G, "eP")
                p.gdma(ow_sb[hc % 2][:], Wo[hc * 128:(hc + 1) * 128, :])
                p.wait(PE, "dW")
                for js in range(4):
                    ins = PE.matmul(PS_BIG[:, js * 512:(js + 1) * 512],
                                    ctx_sb[:, hc * T + tt * 128: hc * T + (tt + 1) * 128],
                                    ow_sb[hc % 2][:, js * 512:(js + 1) * 512],
                                    start=(hc == 0), stop=False)
                p.inc(ins, "eP")
            for js in range(4):
                ins = PE.matmul(PS_BIG[:, js * 512:(js + 1) * 512], ones16[0:1, 0:128],
                                ob16[0:1, js * 512:(js + 1) * 512], start=False,
                                stop=True)
            p.inc(ins, "eP")
            p.wait(V, "eP", "dX")
            emit_ln_tail(p, PS_BIG[:], xres_sb[:], xsum_sb[:], xout_sb[:],
                         xres_sb[:], stats, tt, eps_col)
            p.wait(SY, "eV")
            p.sdma(x3_out[tt * 128:(tt + 1) * 128, :], xout_sb[:])
        p.wait(SY, "dX")


def emit_l1f(nc, p, io):
    x_own, w1t, w3t, w2t = io["x_in"], io["w1t"], io["w3t"], io["w2t"]
    h128, ident, x4_out = io["h128"], io["ident"], io["x_out"]
    PE, V, Sc, G, SY = nc.tensor, nc.vector, nc.scalar, nc.gpsimd, nc.sync
    p.barrier()
    with contextlib.ExitStack() as stk:
        E = lambda n, *a, **k: stk.enter_context(nc.sbuf_tensor("f1f1" + n, *a, **k))
        EP = lambda n, *a, **k: stk.enter_context(nc.psum_tensor("f1" + n, *a, **k))
        PS_BIG = EP("PS_BIG", [128, 2048], F32)
        PS_A = EP("PS_A", [128, 512], F32)
        PS_B = EP("PS_B", [128, 512], F32)
        PS_C = EP("PS_C", [128, 512], F32)
        PS_T = EP("PS_T", [128, 1024], F16)

        h128_16 = E("h128_16", [128, 128], F16)
        ident_16 = E("ident_16", [128, 128], F16)
        fsc_col = E("fsc_col", [128, 1], F32)
        eps_col = E("eps_col", [128, 1], F32)
        gsil = E("gsil", [128, 1024], F16)
        tmp16 = E("tmp16", [128, 1024], F16)
        xres_sb = E("xres_sb", [128, D], F32)
        xsum_sb = E("xsum_sb", [128, D], F32)
        xout_sb = xres_sb
        stats = {"ssum": E("ssum", [128, NT], F32),
                 "ssq": E("ssq", [128, NT], F32),
                 "m": E("mcol", [128, 1], F32),
                 "tmp": E("tmpc", [128, 6], F32)}

        p.gdma(h128_16[:], h128)
        p.gdma(ident_16[:], ident)
        V.memset(fsc_col[:], FWHT_SCALE)
        p.inc(V.memset(eps_col[:], LN_EPS), "eV")

        hid = E("hid", [128, 64 * 1024], F16)

        with (nc.sbuf_tensor("f1xf", [128, NC * T], F16) as xf,
              nc.sbuf_tensor("f1xot", [128, D], F16) as xot,
              nc.sbuf_tensor("f1w1s", [128, 16 * 128], F16) as w1s,
              nc.sbuf_tensor("f1w3s", [128, 16 * 128], F16) as w3s):
            emit_stream_to_fm(p, lambda t: x_own[t * 128:(t + 1) * 128, :],
                              xf, xot, PS_T, ident_16)
            for hc in range(64):
                p.wait(G, "eP")
                p.gdma(w1s[:].rearrange("p (c j) -> p c j", j=128),
                       w1t[:, hc * 128:(hc + 1) * 128].rearrange(
                           "(c p) j -> p c j", p=128))
                p.gdma(w3s[:].rearrange("p (c j) -> p c j", j=128),
                       w3t[:, hc * 128:(hc + 1) * 128].rearrange(
                           "(c p) j -> p c j", p=128))
                p.wait(PE, "dW", "eV", "eA")
                for ws, off in ((w1s, 0), (w3s, 1024)):
                    for ts in range(2):
                        dst = PS_BIG[:, off + ts * 512: off + (ts + 1) * 512]
                        for c in range(NC):
                            ins = PE.matmul(
                                dst, ws[:, c * 128:(c + 1) * 128],
                                xf[:, c * T + ts * 512: c * T + ts * 512 + 512],
                                start=(c == 0), stop=(c == NC - 1))
                p.inc(ins, "eP")
                p.wait(Sc, "eP")
                p.inc(Sc.activation(gsil[:], PS_BIG[:, 0:1024], AF.Silu), "eA")
                p.wait(V, "eA")
                p.inc(V.tensor_tensor(hid[:, hc * 1024:(hc + 1) * 1024], gsil[:],
                                      PS_BIG[:, 1024:2048], ALU.mult), "eV")

        for bit in (1, 2, 4, 8, 16, 32):
            p.wait(V, "eV")
            for c in range(64):
                if c & bit:
                    continue
                a = hid[:, c * 1024:(c + 1) * 1024]
                b = hid[:, (c + bit) * 1024:(c + bit + 1) * 1024]
                V.tensor_tensor(tmp16[:], a, b, ALU.subtract)
                V.tensor_tensor(a, a, b, ALU.add)
                V.tensor_copy(b, tmp16[:])
            p.inc(V.tensor_copy(tmp16[:, 0:1], tmp16[:, 0:1]), "eV")

        for hc in range(64):
            p.wait(PE, "eV", "eA")
            for ts in range(2):
                ps = [PS_A, PS_B][ts][:]
                ins = PE.matmul(ps, h128_16[:],
                                hid[:, hc * 1024 + ts * 512: hc * 1024 + ts * 512 + 512],
                                start=True, stop=True)
            p.inc(ins, "eP")
            p.wait(Sc, "eP")
            for ts in range(2):
                ins = Sc.activation(
                    hid[:, hc * 1024 + ts * 512: hc * 1024 + ts * 512 + 512],
                    [PS_A, PS_B][ts][:], AF.Copy, scale=fsc_col[:])
            p.inc(ins, "eA")

        proj_16 = E("proj_16", [128, NC * T], F16)
        with nc.sbuf_tensor("f1w2s", [128, 64 * 128], F16) as w2s:
            pp = [PS_A, PS_B]
            for dc in range(NC):
                p.wait(G, "eP")
                for qj in range(4):
                    p.gdma(w2s[:, qj * 16 * 128:(qj + 1) * 16 * 128].rearrange(
                               "p (c j) -> p c j", j=128),
                           w2t[:, dc * 128:(dc + 1) * 128].rearrange(
                               "(c p) j -> p c j", p=128)[:, qj * 16:(qj + 1) * 16, :])
                p.wait(PE, "dW", "eV", "eA")
                for ts in range(2):
                    ps = pp[ts][:]
                    for hc in range(64):
                        ins = PE.matmul(ps, w2s[:, hc * 128:(hc + 1) * 128],
                                        hid[:, hc * 1024 + ts * 512:
                                            hc * 1024 + ts * 512 + 512],
                                        start=(hc == 0), stop=(hc == 63))
                    p.inc(ins, "eP")
                p.wait(V, "eP")
                for ts in range(2):
                    ins = V.tensor_copy(
                        proj_16[:, dc * T + ts * 512: dc * T + (ts + 1) * 512],
                        pp[ts][:])
                p.inc(ins, "eV")

        with nc.sbuf_tensor("f1xproj", [128, D], F16) as xproj:
            for tt in range(NT):
                p.wait(SY, "eV", "eP")
                p.sdma(xres_sb[:], x_own[tt * 128:(tt + 1) * 128, :])
                for half in range(2):
                    p.wait(PE, "eV")
                    for j in range(8):
                        dc = half * 8 + j
                        ins = PE.transpose(
                            PS_T[:, j * 128:(j + 1) * 128],
                            proj_16[:, dc * T + tt * 128: dc * T + (tt + 1) * 128],
                            ident_16[:])
                        if j == 7:
                            p.inc(ins, "eP")
                    p.wait(V, "eP")
                    for j in range(8):
                        dc = half * 8 + j
                        ins = V.tensor_copy(xproj[:, dc * 128:(dc + 1) * 128],
                                            PS_T[:, j * 128:(j + 1) * 128])
                        if j == 7:
                            p.inc(ins, "eV")
                p.wait(V, "dX")
                emit_ln_tail(p, xproj[:], xres_sb[:], xsum_sb[:], xout_sb[:],
                             xres_sb[:], stats, tt, eps_col)
                p.wait(SY, "eV")
                p.sdma(x4_out[tt * 128:(tt + 1) * 128, :], xout_sb[:])
            p.wait(SY, "dX")


def emit_exchange(nc, p, x2_d, kv_sh, fl_sh, magic):
    G = nc.gpsimd
    p.barrier()
    with nc.sbuf_tensor("xm_sb", [1, 64], U32) as m_sb, \
         nc.sbuf_tensor("xf_sb", [1, 64], U32) as f_sb:
        p.sdma(m_sb[:], magic)
        p.wait(G, "dX", "dW")
        pid = G.partition_id()
        par = pid % 2
        sib = 1 - par
        p.gdma(kv_sh[bass.ds(par * T, T), :], x2_d)
        p.wait(G, "dW")
        p.gdma(fl_sh[bass.ds(par, 1), :], m_sb[:])
        p.wait(G, "dW")
        psem = nc.alloc_semaphore("psem")
        with G.register("pv") as pv, G.register("sv") as sv, \
             G.register("it") as it, G.register("mg") as mg, \
             G.register("cond") as cond:
            G.reg_load(mg, m_sb[0:1, 0:1])
            G.reg_mov(pv, 0)
            G.reg_mov(it, 0)
            G.reg_mov(sv, 0)
            G.reg_mov(cond, 1)
            with G.While(cond):
                G.reg_add(sv, sv, 16)
                G.dma_start(out=f_sb[:], in_=fl_sh[bass.ds(sib, 1), :]).then_inc(
                    psem, 16)
                G.wait_ge(psem, sv)
                G.reg_load(pv, f_sb[0:1, 0:1])
                G.reg_add(it, it, 1)
                G.reg_alu(cond, pv, mg, ALU.not_equal)
                with G.register("lim") as lim:
                    G.reg_alu(lim, it, 40000, ALU.is_lt)
                    G.reg_alu(cond, cond, lim, ALU.mult)



def build_fused():
    nc = bass.Bass("TRN2", target_bir_lowering=False)
    p = Ph(nc)

    def ein(name, shape):
        return nc.dram_tensor(name, shape, F32, kind="ExternalInput").ap()

    x_own = ein("x_own", [T, D])
    x_ctx = ein("x_ctx", [T, D])
    qkv_w = ein("qkv_w", [D, 3 * D])
    qkv_b = ein("qkv_b", [3 * D])
    out_w = ein("out_w", [D, D])
    out_b = ein("out_b", [D])
    maskB = nc.dram_tensor("maskB", [128, 2048], F16, kind="ExternalInput").ap()
    tri = ein("tri", [128, 128])
    ident = ein("ident", [128, 128])
    wgu = ein("wgu", [HGU, D])
    wpr = ein("wpr", [D, HH])
    Wq = ein("Wq", [D, D]); bq = ein("bq", [D])
    Wk = ein("Wk", [D, 512]); bk = ein("bk", [512])
    Wv = ein("Wv", [D, 512]); bv = ein("bv", [512])
    Wo = ein("Wo", [D, D]); bo = ein("bo", [D])
    w1t = ein("w1t", [D, HH])
    w3t = ein("w3t", [D, HH])
    w2t = ein("w2t", [HH, D])
    h128 = ein("h128", [128, 128])
    magic = nc.dram_tensor("magic", [1, 64], U32, kind="ExternalInput").ap()
    x4_out = nc.dram_tensor("x4", [T, D], F32, kind="ExternalOutput").ap()

    x1_d = nc.dram_tensor("x1_d", [T, D], F32, kind="Internal").ap()
    x2_d = nc.dram_tensor("x2_d", [T, D], F32, kind="Internal").ap()
    x3_d = nc.dram_tensor("x3_d", [T, D], F32, kind="Internal").ap()
    wpt_d = nc.dram_tensor("wpt_d", [64, 16, 128, 128], BF16, kind="Internal").ap()
    kv_sh = nc.dram_tensor("kv_sh", [2 * T, D], F32, kind="Internal",
                           addr_space="Shared").ap()
    fl_sh = nc.dram_tensor("fl_sh", [2, 64], U32, kind="Internal",
                           addr_space="Shared").ap()

    with nc.body("main"):
        emit_l0a(nc, p, dict(x_in=x_own, x_ctx=x_ctx, qkv_w=qkv_w, qkv_b=qkv_b,
                             out_w=out_w, out_b=out_b, maskB=maskB, tri=tri,
                             ident=ident, x_out=x1_d))
        emit_l0f(nc, p, dict(x_in=x1_d, wgu=wgu, wpr=wpr, ident=ident,
                             x_out=x2_d, wpt=wpt_d))
        emit_exchange(nc, p, x2_d, kv_sh, fl_sh, magic)
        emit_l1a(nc, p, dict(x_in=x2_d, x_kv=kv_sh, Wq=Wq, bq=bq, Wk=Wk, bk=bk,
                             Wv=Wv, bv=bv, Wo=Wo, bo=bo, ident=ident,
                             x_out=x3_d))
        emit_l1f(nc, p, dict(x_in=x3_d, w1t=w1t, w3t=w3t, w2t=w2t, h128=h128,
                             ident=ident, x_out=x4_out))
    return nc



def make_l0a_consts():
    iden = np.eye(128, dtype=np.float32)
    trim = np.where(np.arange(128)[None, :] <= np.arange(128)[:, None],
                    0.0, MASKVAL).astype(np.float32)
    masks = []
    for c in range(8):
        half = c % 2
        B = np.zeros((128, 2048), dtype=np.float16)
        if half == 0:
            B[:, 0:1024] = np.float16(MASKVAL)
            B[:, 1024:2048] = np.arange(0, 1024, dtype=np.float16)[None, :]
        else:
            B[:, :] = np.arange(0, 2048, dtype=np.float16)[None, :]
        masks.append(B)
    return masks, trim, iden


def make_h128():
    h = np.array([[1.0]], dtype=np.float32)
    while h.shape[0] < 128:
        h = np.block([[h, h], [h, -h]])
    return h.astype(np.float32)


def shard_rows(x):
    return [np.ascontiguousarray(x[c * T:(c + 1) * T]) for c in range(8)]


def ctx_rows(x):
    out = []
    for c in range(8):
        b, half = c // 2, c % 2
        o = (b * 2 + (1 - half)) * T
        out.append(np.ascontiguousarray(x[o:o + T]))
    return out


_BUILT = None


def kernel(**inputs):
    global _BUILT
    ins = {k: np.ascontiguousarray(np.asarray(v)) for k, v in inputs.items()}
    x0 = ins["L0_attn_x"].reshape(8192, 2048)
    masks, trim, iden = make_l0a_consts()
    owns = shard_rows(x0)
    ctxs = ctx_rows(x0)
    w1tt = np.ascontiguousarray(ins["L1_ffn_W1"].T)
    w3tt = np.ascontiguousarray(ins["L1_ffn_W3"].T)
    w2tt = np.ascontiguousarray(ins["L1_ffn_W2"].T)
    magic = np.full((1, 64), np.random.randint(1, 2**31, dtype=np.int64),
                    dtype=np.uint32)
    if _BUILT is None:
        _BUILT = build_fused()
    nc = _BUILT
    in_maps = []
    for c in range(8):
        in_maps.append({
            "x_own": owns[c], "x_ctx": ctxs[c],
            "qkv_w": ins["L0_attn_qkv_w"], "qkv_b": ins["L0_attn_qkv_b"],
            "out_w": ins["L0_attn_out_w"], "out_b": ins["L0_attn_out_b"],
            "maskB": masks[c], "tri": trim, "ident": iden,
            "wgu": ins["L0_ffn_w_gate_up"], "wpr": ins["L0_ffn_w_proj"],
            "Wq": ins["L1_attn_Wq"], "bq": ins["L1_attn_bq"],
            "Wk": ins["L1_attn_Wk"], "bk": ins["L1_attn_bk"],
            "Wv": ins["L1_attn_Wv"], "bv": ins["L1_attn_bv"],
            "Wo": ins["L1_attn_Wo"], "bo": ins["L1_attn_bo"],
            "w1t": w1tt, "w3t": w3tt, "w2t": w2tt,
            "h128": make_h128(), "magic": magic,
        })
    res = run_bass_kernel_spmd(nc, in_maps, core_ids=list(range(8)))
    x4 = np.concatenate([r["x4"] for r in res.results], axis=0)
    return x4.reshape(4, 2048, 2048)


# version 07
# speedup vs baseline: 1.3860x; 1.3845x over previous
import contextlib
import math
import numpy as np
import ml_dtypes
import concourse.bass as bass
import concourse.mybir as mybir
from concourse.bass_utils import run_bass_kernel_spmd

F32 = mybir.dt.float32
BF16 = mybir.dt.bfloat16
F16 = mybir.dt.float16
U32 = mybir.dt.uint32
AF = mybir.ActivationFunctionType
ALU = mybir.AluOpType
AX = mybir.AxisListType

T = 1024
D = 2048
NT = T // 128
NC = D // 128
NHEADS = 16
HS = 128
HGU = 16384
HH = 8192
MASKVAL = -30000.0
LN_EPS = 1e-5
RMS_EPS = float(np.finfo(np.float32).eps)
INV2048 = 1.0 / 2048.0
INV64 = 1.0 / 64.0
INV8192 = 1.0 / 8192.0
QSCALE = 1.0 / math.sqrt(128.0)
FWHT_SCALE = 1.0 / math.sqrt(8192.0)
THR = 0.5009765625


def bcast(ap, n):
    return bass.AP(ap.tensor, ap.offset, [list(x) for x in ap.ap] + [[0, n]])


class Ph:
    def __init__(self, nc):
        self.nc = nc
        self.sem = {}
        self.cnt = {}
        for name in ("dW", "dX", "eP", "eV", "eA", "eG"):
            self.sem[name] = nc.alloc_semaphore(name)
            self.cnt[name] = 0

    def inc(self, instr, name, n=1):
        instr.then_inc(self.sem[name], n)
        self.cnt[name] += n
        return instr

    def wait(self, eng, *names):
        for name in names:
            if self.cnt[name] > 0:
                eng.wait_ge(self.sem[name], self.cnt[name])

    def snap(self, name):
        return self.cnt[name]

    def wait_val(self, eng, name, val):
        if val > 0:
            eng.wait_ge(self.sem[name], val)

    def barrier(self):
        nc = self.nc
        for eng in (nc.tensor, nc.vector, nc.scalar, nc.gpsimd, nc.sync):
            self.wait(eng, "dW", "dX", "eP", "eV", "eA")

    def gdma(self, out, in_):
        return self.inc(self.nc.gpsimd.dma_start(out=out, in_=in_), "dW", 16)

    def sdma(self, out, in_):
        return self.inc(self.nc.sync.dma_start(out=out, in_=in_), "dX", 16)



def emit_ln_tail(p, ps_big, xres_sb, xsum_sb, xout_sb, trash_f32, stats, tt,
                 eps_col, rstd_extra=None):
    V, A = p.nc.vector, p.nc.scalar
    m, tmp = stats["m"], stats["tmp"]
    if rstd_extra is None:
        ins = V.scalar_tensor_tensor(xsum_sb, ps_big, 0.0, xres_sb,
                                     ALU.add, ALU.add)
    else:
        ins = V.scalar_tensor_tensor(xsum_sb, ps_big, rstd_extra, xres_sb,
                                     ALU.mult, ALU.add)
    p.inc(ins, "eV")
    psum_free_snap = p.snap("eV")
    p.inc(V.tensor_reduce(stats["ssum"][:, tt:tt + 1], xsum_sb, AX.XY, ALU.add),
          "eV")
    p.wait(A, "eV")
    p.inc(A.activation(trash_f32, xsum_sb, AF.Square,
                       accum_out=stats["ssq"][:, tt:tt + 1]), "eA")
    p.wait(V, "eA")
    V.tensor_scalar_mul(m[:], stats["ssum"][:, tt:tt + 1], INV2048)
    p.inc(V.tensor_scalar_mul(tmp[:, 0:1], stats["ssq"][:, tt:tt + 1], INV2048),
          "eV")
    p.wait(A, "eV")
    p.inc(A.activation(tmp[:, 1:2], m[:], AF.Square), "eA")
    p.wait(V, "eA")
    p.inc(V.scalar_tensor_tensor(tmp[:, 2:3], tmp[:, 1:2], -1.0, tmp[:, 0:1],
                                 ALU.mult, ALU.add), "eV")
    p.wait(A, "eV")
    p.inc(A.activation(tmp[:, 3:4], tmp[:, 2:3], AF.Sqrt, bias=eps_col[:]), "eA")
    p.wait(V, "eA")
    p.inc(V.reciprocal(tmp[:, 4:5], tmp[:, 3:4]), "eV")
    p.wait(A, "eV")
    p.inc(A.copy(tmp[:, 5:6], tmp[:, 4:5]), "eA")
    p.wait(V, "eA")
    p.inc(V.tensor_scalar(xout_sb, xsum_sb, m[:], tmp[:, 5:6],
                          ALU.subtract, ALU.mult), "eV")
    return psum_free_snap


def emit_stream_to_fm(p, x_dram_rows, xf_sb, xot_sb, ps_t, ident_h, ntiles=NT):
    nc = p.nc
    PE, V = nc.tensor, nc.vector
    Tp = ntiles * 128
    for t in range(ntiles):
        for half in range(2):
            buf = xot_sb[:, 0:1024]
            p.wait(nc.gpsimd, "eP")
            p.gdma(buf, x_dram_rows(t)[:, half * 1024:(half + 1) * 1024])
            p.wait(PE, "dW", "eV")
            for cc in range(8):
                c = half * 8 + cc
                ins = PE.transpose(ps_t[:, cc * 128:(cc + 1) * 128],
                                   buf[:, cc * 128:(cc + 1) * 128], ident_h[:])
                if cc == 7:
                    p.inc(ins, "eP")
            p.wait(V, "eP")
            for cc in range(8):
                c = half * 8 + cc
                ins = V.tensor_copy(xf_sb[:, c * Tp + t * 128: c * Tp + (t + 1) * 128],
                                    ps_t[:, cc * 128:(cc + 1) * 128])
                if cc == 7:
                    p.inc(ins, "eV")


def emit_quant(p, wb, bufA, bufB, gsum, scale_bf, sc2_bf, thr_f32,
               inv64_col, thr_col, G):
    V, A = p.nc.vector, p.nc.scalar
    F = G * 64
    bv = lambda t: t[:, 0:F].rearrange("p (g e) -> p g e", e=64)
    p.wait(A, "eV")
    p.inc(A.activation(bufA[:, 0:F], wb[:, 0:F], AF.Abs), "eA")
    p.inc(A.activation(bufB[:, 0:F], wb[:, 0:F], AF.Sign), "eA")
    p.inc(V.tensor_reduce(gsum[:, 0:G], bv(wb), AX.X, ALU.add,
                          apply_absolute_value=True), "eV")
    p.wait(A, "eV")
    p.inc(A.activation(scale_bf[:, 0:G], gsum[:, 0:G], AF.Copy,
                       scale=inv64_col[:]), "eA")
    p.wait(V, "eA")
    p.inc(V.tensor_scalar_max(sc2_bf[:, 0:G], scale_bf[:, 0:G], 1e-8), "eV")
    p.wait(A, "eV")
    p.inc(A.activation(thr_f32[:, 0:G], sc2_bf[:, 0:G], AF.Copy,
                       scale=thr_col[:]), "eA")
    p.wait(V, "eA")
    V.tensor_tensor(bv(bufA), bv(bufA), bcast(thr_f32[:, 0:G], 64), ALU.is_gt)
    V.tensor_tensor(bufB[:, 0:F], bufA[:, 0:F], bufB[:, 0:F], ALU.mult)
    V.tensor_tensor(bv(bufB), bv(bufB), bcast(sc2_bf[:, 0:G], 64), ALU.mult)
    V.tensor_tensor(bufB[:, 0:F], bufB[:, 0:F], wb[:, 0:F], ALU.subtract)
    p.inc(V.tensor_tensor(wb[:, 0:F], bufB[:, 0:F], wb[:, 0:F], ALU.add), "eV")



def emit_l0a(nc, p, io):
    x_own, x_ctx = io["x_in"], io["x_ctx"]
    qkv_w, qkv_b = io["qkv_w"], io["qkv_b"]
    out_w, out_b = io["out_w"], io["out_b"]
    maskB, tri, ident = io["maskB"], io["tri"], io["ident"]
    x1_out = io["x_out"]

    PE, V, Sc, G, SY = nc.tensor, nc.vector, nc.scalar, nc.gpsimd, nc.sync
    p.barrier()
    with contextlib.ExitStack() as stk:
        E = lambda n, *a, **k: stk.enter_context(nc.sbuf_tensor("a0a0" + n, *a, **k))
        EP = lambda n, *a, **k: stk.enter_context(nc.psum_tensor("a0" + n, *a, **k))
        PS_BIG = EP("PS_BIG", [128, 2048], F32)
        PS_A = EP("PS_A", [128, 512], F32)
        PS_B = EP("PS_B", [128, 512], F32)
        PS_C = EP("PS_C", [128, 512], F32)
        PS_T = EP("PS_T", [128, 1024], BF16)

        xot = E("xot", [128, 1024], BF16)
        xf_own = E("xf_own", [128, NC * T], BF16)
        xf_ctx = E("xf_ctx", [128, NC * T], BF16)
        K_sb = [E(f"K{i}", [128, 2048], BF16) for i in range(2)]
        V_sb = [E(f"V{i}", [128, 2048], BF16) for i in range(2)]
        Q_sb = [E(f"Q{i}", [128, 1024], BF16) for i in range(2)]
        ctx_sb = E("ctx", [128, NHEADS * T], BF16)
        maskB_sb = E("maskB_sb", [128, 2048], F16)
        tri_bf = E("tri_bf", [128, 128], BF16)
        s_tile = E("s_tile", [128, 2048], F32)
        p_bf = E("p_bf", [128, 2048], BF16)
        pT_sb = E("pT_sb", [128, 2048], BF16)
        wq_sb = [E(f"wq{i}", [128, 16 * 128], BF16) for i in range(2)]
        wk_sb = [E(f"wk{i}", [128, 16 * 128], BF16) for i in range(2)]
        wv_sb = [E(f"wv{i}", [128, 16 * 128], BF16) for i in range(2)]
        ow_sb = [E(f"ow{i}", [128, 2048], BF16) for i in range(2)]
        qb16 = E("qb16", [1, 3 * D], F16)
        ob16 = E("ob16", [1, D], F16)
        ones16 = E("ones16", [1, 512], F16)
        ident_bf = E("ident_bf", [128, 128], BF16)
        xres_sb = E("xres_sb", [128, D], F32)
        xsum_sb = E("xsum_sb", [128, D], F32)
        xout_sb = xres_sb
        negmax = E("negmax", [128, 1], F32)
        rowsum = E("rowsum", [128, 1], F32)
        rcol = E("rcol", [128, 1], F32)
        eps_col = E("eps_col", [128, 1], F32)
        stats = {"ssum": E("ssum", [128, NT], F32),
                 "ssq": E("ssq", [128, NT], F32),
                 "m": E("mcol", [128, 1], F32),
                 "tmp": E("tmpc", [128, 6], F32)}

        p.sdma(maskB_sb[:], maskB)
        p.gdma(tri_bf[:], tri)
        p.gdma(qb16[:], qkv_b.rearrange("(o n) -> o n", o=1))
        p.gdma(ob16[:], out_b.rearrange("(o n) -> o n", o=1))
        p.gdma(ident_bf[:], ident)
        V.memset(ones16[:], 1.0)
        p.inc(V.memset(eps_col[:], LN_EPS), "eV")

        emit_stream_to_fm(p, lambda t: x_own[t * 128:(t + 1) * 128, :],
                          xf_own, xot, PS_T, ident_bf)
        emit_stream_to_fm(p, lambda t: x_ctx[t * 128:(t + 1) * 128, :],
                          xf_ctx, xot, PS_T, ident_bf)

        def xf_tokslice(kt512):
            src = xf_ctx if kt512 < 2 else xf_own
            off = (kt512 % 2) * 512
            return src, off

        prod_ps = [PS_A, PS_B]
        for g in range(NHEADS // 2):
            for hh in range(2):
                h = 2 * g + hh
                p.wait(G, "eP")
                for wsb, col0 in ((wq_sb[hh], h * HS), (wk_sb[hh], D + h * HS),
                                  (wv_sb[hh], 2 * D + h * HS)):
                    p.gdma(wsb[:].rearrange("p (c j) -> p c j", j=128),
                           qkv_w[:, col0:col0 + HS].rearrange("(c p) j -> p c j", p=128))
                p.wait(PE, "dW", "eV")
                npd = 0
                for kt in range(4):
                    ps = prod_ps[npd % 2][:]
                    npd += 1
                    src, off = xf_tokslice(kt)
                    for c in range(NC):
                        PE.matmul(ps, wk_sb[hh][:, c * 128:(c + 1) * 128],
                                  src[:, c * T + off: c * T + off + 512],
                                  start=(c == 0), stop=False)
                    p.inc(PE.matmul(ps, qb16[0:1, D + h * HS:D + (h + 1) * HS],
                                    ones16[0:1, :], start=False, stop=True), "eP")
                    p.wait(V, "eP")
                    p.inc(V.tensor_copy(K_sb[hh][:, kt * 512:(kt + 1) * 512], ps), "eV")
                    p.wait(PE, "eV")
                for vt in range(16):
                    ps = prod_ps[npd % 2][:]
                    npd += 1
                    src = xf_ctx if vt < 8 else xf_own
                    toff = (vt % 8) * 128
                    for c in range(NC):
                        PE.matmul(ps[:, 0:128],
                                  src[:, c * T + toff: c * T + toff + 128],
                                  wv_sb[hh][:, c * 128:(c + 1) * 128],
                                  start=(c == 0), stop=False)
                    p.inc(PE.matmul(ps[:, 0:128], ones16[0:1, 0:128],
                                    qb16[0:1, 2 * D + h * HS:2 * D + (h + 1) * HS],
                                    start=False, stop=True), "eP")
                    p.wait(V, "eP")
                    p.inc(V.tensor_copy(V_sb[hh][:, vt * 128:(vt + 1) * 128],
                                        ps[:, 0:128]), "eV")
                    p.wait(PE, "eV")
                for qs in range(2):
                    ps = prod_ps[npd % 2][:]
                    npd += 1
                    for c in range(NC):
                        PE.matmul(ps, wq_sb[hh][:, c * 128:(c + 1) * 128],
                                  xf_own[:, c * T + qs * 512: c * T + qs * 512 + 512],
                                  start=(c == 0), stop=False)
                    p.inc(PE.matmul(ps, qb16[0:1, h * HS:(h + 1) * HS],
                                    ones16[0:1, :], start=False, stop=True), "eP")
                    p.wait(V, "eP")
                    p.inc(V.tensor_copy(Q_sb[hh][:, qs * 512:(qs + 1) * 512], ps), "eV")
                    p.wait(PE, "eV")

                slope = 1.0 / 2.0 ** (h / 2.0)
                for qt in range(NT):
                    nk = 3 if qt < 4 else 4
                    ve = 1024 + (qt + 1) * 128
                    kch = ve // 128
                    p.wait(PE, "eV")
                    dj = 1024 + qt * 128
                    w = nk * 512
                    for kt in range(nk):
                        dslc = PS_BIG[:, kt * 512:(kt + 1) * 512]
                        PE.matmul(dslc, Q_sb[hh][:, qt * 128:(qt + 1) * 128],
                                  K_sb[hh][:, kt * 512:(kt + 1) * 512],
                                  start=True, stop=(dj // 512 != kt))
                    p.inc(PE.matmul(PS_BIG[:, dj:dj + 128], ident_bf[:], tri_bf[:],
                                    start=False, stop=True), "eP")
                    p.wait(V, "eP", "eA")
                    if ve < w:
                        V.memset(s_tile[:, ve:w], -1e30)
                    V.scalar_tensor_tensor(s_tile[:, 0:ve], maskB_sb[:, 0:ve], slope,
                                           PS_BIG[:, 0:ve], ALU.mult, ALU.add)
                    p.inc(V.tensor_reduce(negmax[:], s_tile[:, 0:w], AX.XY,
                                          ALU.max, negate=True), "eV")
                    p.wait(Sc, "eV")
                    p.inc(Sc.activation(s_tile[:, 0:w], s_tile[:, 0:w], AF.Exp,
                                        bias=negmax[:], accum_out=rowsum[:]), "eA")
                    p.wait(V, "eA")
                    p.inc(V.reciprocal(rcol[:], rowsum[:]), "eV")
                    p.wait(Sc, "eV")
                    p.inc(Sc.activation(p_bf[:, 0:ve], s_tile[:, 0:ve], AF.Copy,
                                        scale=rcol[:]), "eA")
                    done = 0
                    while done < kch:
                        bn = min(8, kch - done)
                        p.wait(PE, "eV", "eA")
                        for j in range(bn):
                            kc = done + j
                            ins = PE.transpose(PS_T[:, j * 128:(j + 1) * 128],
                                               p_bf[:, kc * 128:(kc + 1) * 128],
                                               ident_bf[:])
                            if j == bn - 1:
                                p.inc(ins, "eP")
                        p.wait(V, "eP")
                        for j in range(bn):
                            kc = done + j
                            ins = V.tensor_copy(pT_sb[:, kc * 128:(kc + 1) * 128],
                                                PS_T[:, j * 128:(j + 1) * 128])
                            if j == bn - 1:
                                p.inc(ins, "eV")
                        done += bn
                    p.wait(PE, "eV", "eA")
                    for kc in range(kch):
                        ins = PE.matmul(PS_C[:, 0:128],
                                        V_sb[hh][:, kc * 128:(kc + 1) * 128],
                                        pT_sb[:, kc * 128:(kc + 1) * 128],
                                        start=(kc == 0), stop=(kc == kch - 1))
                    p.inc(ins, "eP")
                    p.wait(Sc, "eP")
                    p.inc(Sc.copy(ctx_sb[:, h * T + qt * 128: h * T + (qt + 1) * 128],
                                  PS_C[:, 0:128]), "eA")

        ln_snap = 0
        p.wait(PE, "eA", "eV")
        for tt in range(NT):
            p.wait(SY, "eV")
            p.sdma(xres_sb[:], x_own[tt * 128:(tt + 1) * 128, :])
            p.wait_val(PE, "eV", ln_snap)
            for hc in range(NC):
                p.wait(G, "eP")
                p.gdma(ow_sb[hc % 2][:], out_w[hc * 128:(hc + 1) * 128, :])
                p.wait(PE, "dW")
                for js in range(4):
                    ins = PE.matmul(PS_BIG[:, js * 512:(js + 1) * 512],
                                    ctx_sb[:, hc * T + tt * 128: hc * T + (tt + 1) * 128],
                                    ow_sb[hc % 2][:, js * 512:(js + 1) * 512],
                                    start=(hc == 0), stop=False)
                p.inc(ins, "eP")
            for js in range(4):
                ins = PE.matmul(PS_BIG[:, js * 512:(js + 1) * 512], ones16[0:1, 0:128],
                                ob16[0:1, js * 512:(js + 1) * 512], start=False,
                                stop=True)
            p.inc(ins, "eP")
            p.wait(V, "eP", "dX")
            ln_snap = emit_ln_tail(p, PS_BIG[:], xres_sb[:], xsum_sb[:], xout_sb[:],
                                   xres_sb[:], stats, tt, eps_col)
            p.wait(SY, "eV")
            p.sdma(x1_out[tt * 128:(tt + 1) * 128, :], xout_sb[:])
        p.wait(SY, "dX")


def emit_l0f(nc, p, io):
    x_own, wgu, wpr, ident = io["x_in"], io["wgu"], io["wpr"], io["ident"]
    x2_out, wpt = io["x_out"], io["wpt"]
    dbg = io.get("dbg")

    PE, V, Sc, G, SY = nc.tensor, nc.vector, nc.scalar, nc.gpsimd, nc.sync
    p.barrier()
    with contextlib.ExitStack() as stk:
        E = lambda n, *a, **k: stk.enter_context(nc.sbuf_tensor("f0f0" + n, *a, **k))
        EP = lambda n, *a, **k: stk.enter_context(nc.psum_tensor("f0" + n, *a, **k))
        PS_BIG = EP("PS_BIG", [128, 2048], F32)
        PS_A = EP("PS_A", [128, 512], F32)
        PS_B = EP("PS_B", [128, 512], F32)
        PS_C = EP("PS_C", [128, 512], F32)
        PS_T = EP("PS_T", [128, 1024], BF16)

        ident_bf = E("ident_bf", [128, 128], BF16)
        ones_col16 = E("ones_col16", [128, 1], F16)
        onesf32 = E("onesf32", [1, 1], F32)
        eps_col = E("eps_col", [128, 1], F32)
        inv64_col = E("inv64_col", [128, 1], F32)
        thr_col = E("thr_col", [128, 1], F32)
        epsR = E("epsR", [1, 1], F32)
        inv8192c = E("inv8192c", [1, 1], F32)
        gsum = E("gsum", [128, 128], F32)
        scale_bf = E("scale_bf", [128, 128], BF16)
        sc2_bf = E("sc2_bf", [128, 128], BF16)
        thr_f32 = E("thr_f32", [128, 128], F32)
        rstd_cols = E("rstd_cols", [128, 8], F32)
        stats = {"ssum": E("ssum", [128, NT], F32),
                 "ssq": E("ssq", [128, NT], F32),
                 "m": E("mcol", [128, 1], F32),
                 "tmp": E("tmpc", [128, 6], F32)}

        p.gdma(ident_bf[:], ident)
        V.memset(ones_col16[:], 1.0)
        V.memset(onesf32[:], 1.0)
        V.memset(eps_col[:], LN_EPS)
        V.memset(inv64_col[:], INV64)
        V.memset(thr_col[:], THR)
        V.memset(epsR[:], RMS_EPS)
        p.inc(V.memset(inv8192c[:], INV8192), "eV")

        hid = E("hid", [128, 64 * 1024], BF16)

        with (nc.sbuf_tensor("f0wbP0", [128, HH], BF16) as wbP0,
              nc.sbuf_tensor("f0wbP1", [128, HH], BF16) as wbP1,
              nc.sbuf_tensor("f0bufA", [128, HH], BF16) as bufA,
              nc.sbuf_tensor("f0bufB", [128, HH], BF16) as bufB,
              nc.sbuf_tensor("f0stage", [128, 1024], BF16) as stage):
            wbP = [wbP0, wbP1]
            ldp = [0] * 18
            qp = [0] * 18
            tp = [0] * 18

            def b_load(i):
                if i < 0 or i > 15:
                    return
                p.wait_val(G, "eP", tp[i - 2] if i >= 2 else 0)
                p.gdma(wbP[i % 2][:], wpr[i * 128:(i + 1) * 128, :])
                ldp[i] = p.snap("dW")

            def b_qnt(i):
                if i < 0 or i > 15:
                    return
                p.wait_val(V, "dW", ldp[i])
                p.wait_val(Sc, "dW", ldp[i])
                p.wait_val(V, "eP", tp[i - 1] if i >= 1 else 0)
                emit_quant(p, wbP[i % 2], bufA, bufB, gsum, scale_bf, sc2_bf,
                           thr_f32, inv64_col, thr_col, HH // 64)
                qp[i] = p.snap("eV")

            b_load(0)
            b_qnt(0)
            b_load(1)
            for dr in range(16):
                p.wait_val(PE, "eV", qp[dr])
                for half in range(8):
                    p.wait(PE, "eV")
                    for j in range(8):
                        hc = half * 8 + j
                        ins = PE.transpose(PS_T[:, j * 128:(j + 1) * 128],
                                           wbP[dr % 2][:, hc * 128:(hc + 1) * 128],
                                           ident_bf[:])
                        if j == 7:
                            p.inc(ins, "eP")
                    p.wait(V, "eP")
                    p.wait(V, "dW")
                    for j in range(8):
                        ins = V.tensor_copy(stage[:, j * 128:(j + 1) * 128],
                                            PS_T[:, j * 128:(j + 1) * 128])
                        if j == 7:
                            p.inc(ins, "eV")
                    p.wait(G, "eV")
                    p.gdma(wpt[half * 8:half * 8 + 8, dr].rearrange(
                               "j p f -> p j f"),
                           stage[:].rearrange("p (j f) -> p j f", f=128))
                tp[dr] = p.snap("eP")
                b_qnt(dr + 1)
                b_load(dr + 2)

        with (nc.sbuf_tensor("f0xf", [128, NC * T], BF16) as xf,
              nc.sbuf_tensor("f0xot", [128, 1024], BF16) as xot):
            emit_stream_to_fm(p, lambda t: x_own[t * 128:(t + 1) * 128, :],
                              xf, xot, PS_T, ident_bf)
            with (nc.sbuf_tensor("f0wbG0", [128, 2 * D], BF16) as wbG0,
                  nc.sbuf_tensor("f0wbG1", [128, 2 * D], BF16) as wbG1,
                  nc.sbuf_tensor("f0bufAg", [128, 2 * D], BF16) as bufAg,
                  nc.sbuf_tensor("f0bufBg", [128, 2 * D], BF16) as bufBg,
                  nc.sbuf_tensor("f0wgT", [128, 2 * NC * 128], BF16) as wgT,
                  nc.sbuf_tensor("f0gsil", [128, 1024], BF16) as gsil):
                wbG = [wbG0, wbG1]
                h2 = gsil
                ld_snap = [0] * 66
                q_snap = [0] * 66
                t_snap = [0] * 66
                sil_snap = [0] * 66
                hid_snap = [0] * 66
                sq_snap = [0] * 66
                mm_snap = [0] * 66

                def emit_load(i):
                    if i > 64 - 1 or i < 0:
                        return
                    p.wait_val(G, "eP", t_snap[i - 2] if i >= 2 else 0)
                    p.gdma(wbG[i % 2][:, 0:D], wgu[i * 128:(i + 1) * 128, :])
                    p.gdma(wbG[i % 2][:, D:2 * D],
                           wgu[(64 + i) * 128:(65 + i) * 128, :])
                    ld_snap[i] = p.snap("dW")

                def emit_qnt(i):
                    if i > 63 or i < 0:
                        return
                    p.wait_val(V, "dW", ld_snap[i])
                    p.wait_val(Sc, "dW", ld_snap[i])
                    p.wait_val(V, "eP", t_snap[i - 1] if i >= 1 else 0)
                    emit_quant(p, wbG[i % 2], bufAg, bufBg, gsum, scale_bf,
                               sc2_bf, thr_f32, inv64_col, thr_col, 2 * D // 64)
                    q_snap[i] = p.snap("eV")

                emit_load(0)
                emit_qnt(0)
                emit_load(1)
                for hc in range(64):
                    p.wait_val(PE, "eV", q_snap[hc])
                    for half in range(4):
                        p.wait(PE, "eV")
                        for j in range(8):
                            c = half * 8 + j
                            ins = PE.transpose(PS_T[:, j * 128:(j + 1) * 128],
                                               wbG[hc % 2][:, c * 128:(c + 1) * 128],
                                               ident_bf[:])
                        p.inc(ins, "eP")
                        p.wait(V, "eP")
                        for j in range(8):
                            c = half * 8 + j
                            ins = V.tensor_copy(wgT[:, c * 128:(c + 1) * 128],
                                                PS_T[:, j * 128:(j + 1) * 128])
                        p.inc(ins, "eV")
                    t_snap[hc] = p.snap("eP")
                    if hc >= 1:
                        p.wait_val(PE, "eA", sil_snap[hc - 1])
                        p.wait_val(PE, "eV", hid_snap[hc - 1])
                    p.wait(PE, "eV")
                    for gu_i in range(2):
                        for ts in range(2):
                            dst = PS_BIG[:, gu_i * 1024 + ts * 512:
                                         gu_i * 1024 + (ts + 1) * 512]
                            for c in range(NC):
                                ins = PE.matmul(
                                    dst, wgT[:, (gu_i * NC + c) * 128:
                                             (gu_i * NC + c + 1) * 128],
                                    xf[:, c * T + ts * 512: c * T + ts * 512 + 512],
                                    start=(c == 0), stop=(c == NC - 1))
                    p.inc(ins, "eP")
                    mm_snap[hc] = p.snap("eP")
                    if hc >= 1:
                        p.wait_val(PE, "eA", sq_snap[hc - 1])
                        PE.matmul(PS_A[0:1, :], ones_col16[:], h2[:, 0:512],
                                  start=(hc - 1 == 0), stop=False,
                                  skip_group_check=True)
                        ins = PE.matmul(PS_B[0:1, :], ones_col16[:],
                                        h2[:, 512:1024],
                                        start=(hc - 1 == 0), stop=False,
                                        skip_group_check=True)
                        p.inc(ins, "eP")
                    emit_qnt(hc + 1)
                    emit_load(hc + 2)
                    p.wait(Sc, "eP")
                    p.inc(Sc.activation(gsil[:], PS_BIG[:, 0:1024], AF.Silu), "eA")
                    sil_snap[hc] = p.snap("eA")
                    p.wait_val(V, "eA", sil_snap[hc])
                    p.inc(V.tensor_tensor(hid[:, hc * 1024:(hc + 1) * 1024],
                                          gsil[:], PS_BIG[:, 1024:2048],
                                          ALU.mult), "eV")
                    hid_snap[hc] = p.snap("eV")
                    p.wait_val(Sc, "eV", hid_snap[hc])
                    p.inc(Sc.activation(h2[:, 0:1024],
                                        hid[:, hc * 1024:(hc + 1) * 1024],
                                        AF.Square), "eA")
                    sq_snap[hc] = p.snap("eA")
                p.wait_val(PE, "eA", sq_snap[63])
                PE.matmul(PS_A[0:1, :], ones_col16[:], h2[:, 0:512],
                          start=False, stop=True, skip_group_check=True)
                ins = PE.matmul(PS_B[0:1, :], ones_col16[:], h2[:, 512:1024],
                                start=False, stop=True, skip_group_check=True)
                p.inc(ins, "eP")

        with nc.sbuf_tensor("f0msq", [1, 1024], F32) as msq_sb:
            rstd_row = msq_sb
            p.wait(V, "eP")
            V.tensor_copy(msq_sb[0:1, 0:512], PS_A[0:1, :])
            p.inc(V.tensor_copy(msq_sb[0:1, 512:1024], PS_B[0:1, :]), "eV")
            p.wait(Sc, "eV")
            p.inc(Sc.activation(rstd_row[0:1, :], msq_sb[0:1, :], AF.Sqrt,
                                bias=epsR[:], scale=inv8192c[:]), "eA")
            p.wait(V, "eA")
            p.inc(V.reciprocal(rstd_row[0:1, :], rstd_row[0:1, :]), "eV")
            p.wait(PE, "eV")
            for j in range(NT):
                ins = PE.matmul(PS_C[:, j:j + 1],
                                rstd_row[0:1, j * 128:(j + 1) * 128],
                                onesf32[0:1, 0:1], start=True, stop=True)
            p.inc(ins, "eP")
            p.wait(V, "eP")
            p.inc(V.tensor_copy(rstd_cols[:, 0:NT], PS_C[:, 0:NT]), "eV")

        proj_bf = E("proj_bf", [128, NC * T], BF16)
        with (nc.sbuf_tensor("f0w2s0", [128, 64 * 128], BF16) as w2s0,
              nc.sbuf_tensor("f0w2s1", [128, 64 * 128], BF16) as w2s1):
            w2s = [w2s0, w2s1]
            pp = [PS_A, PS_B]
            ldw2 = [0] * 18
            mmw2 = [0] * 18
            evw2 = [0] * 18

            def w2_load(i):
                if i < 0 or i > 15:
                    return
                p.wait_val(G, "eP", mmw2[i - 2] if i >= 2 else 0)
                for qj in range(4):
                    p.gdma(w2s[i % 2][:, qj * 16 * 128:(qj + 1) * 16 * 128].rearrange(
                               "p (j f) -> p j f", f=128),
                           wpt[qj * 16:(qj + 1) * 16, i].rearrange(
                               "j p f -> p j f"))
                ldw2[i] = p.snap("dW")

            p.wait(G, "eP", "eV", "eA")
            w2_load(0)
            w2_load(1)
            p.wait(PE, "eV", "eA")
            for dc in range(NC):
                p.wait_val(PE, "dW", ldw2[dc])
                p.wait_val(PE, "eV", evw2[dc - 1] if dc >= 1 else 0)
                for ts in range(2):
                    ps = pp[ts][:]
                    for hc in range(64):
                        ins = PE.matmul(ps, w2s[dc % 2][:, hc * 128:(hc + 1) * 128],
                                        hid[:, hc * 1024 + ts * 512:
                                            hc * 1024 + ts * 512 + 512],
                                        start=(hc == 0), stop=(hc == 63))
                    p.inc(ins, "eP")
                mmw2[dc] = p.snap("eP")
                w2_load(dc + 2)
                p.wait_val(V, "eP", mmw2[dc])
                for ts in range(2):
                    ins = V.tensor_copy(
                        proj_bf[:, dc * T + ts * 512: dc * T + (ts + 1) * 512],
                        pp[ts][:])
                p.inc(ins, "eV")
                evw2[dc] = p.snap("eV")

        with (nc.sbuf_tensor("f0xproj", [128, D], BF16) as xproj,
              nc.sbuf_tensor("f0xres_g", [128, D], F32) as xres_t,
              nc.sbuf_tensor("f0xsum_g", [128, D], F32) as xsum_t):
            xres = xres_t[:]
            xsum = xsum_t[:]
            for tt in range(NT):
                p.wait(SY, "eV", "eP")
                p.sdma(xres, x_own[tt * 128:(tt + 1) * 128, :])
                for half in range(2):
                    p.wait(PE, "eV")
                    for j in range(8):
                        dc = half * 8 + j
                        ins = PE.transpose(
                            PS_T[:, j * 128:(j + 1) * 128],
                            proj_bf[:, dc * T + tt * 128: dc * T + (tt + 1) * 128],
                            ident_bf[:])
                        if j == 7:
                            p.inc(ins, "eP")
                    p.wait(V, "eP")
                    for j in range(8):
                        dc = half * 8 + j
                        ins = V.tensor_copy(xproj[:, dc * 128:(dc + 1) * 128],
                                            PS_T[:, j * 128:(j + 1) * 128])
                        if j == 7:
                            p.inc(ins, "eV")
                p.wait(V, "dX")
                emit_ln_tail(p, xproj[:], xres, xsum, xres, xres, stats, tt,
                             eps_col, rstd_extra=rstd_cols[:, tt:tt + 1])
                p.wait(SY, "eV")
                p.sdma(x2_out[tt * 128:(tt + 1) * 128, :], xres)
            p.wait(SY, "dX")


def emit_l1a(nc, p, io):
    x_own, x_kv = io["x_in"], io["x_kv"]
    Wq, bq, Wk, bk, Wv, bv, Wo, bo = (io[k] for k in
                                      ("Wq", "bq", "Wk", "bk", "Wv", "bv",
                                       "Wo", "bo"))
    ident, x3_out = io["ident"], io["x_out"]
    PE, V, Sc, G, SY = nc.tensor, nc.vector, nc.scalar, nc.gpsimd, nc.sync
    p.barrier()
    with contextlib.ExitStack() as stk:
        E = lambda n, *a, **k: stk.enter_context(nc.sbuf_tensor("a1a1" + n, *a, **k))
        EP = lambda n, *a, **k: stk.enter_context(nc.psum_tensor("a1" + n, *a, **k))
        PS_BIG = EP("PS_BIG", [128, 2048], F32)
        PS_A = EP("PS_A", [128, 512], F32)
        PS_B = EP("PS_B", [128, 512], F32)
        PS_C = EP("PS_C", [128, 512], F32)
        PS_T = EP("PS_T", [128, 1024], BF16)

        xf_own = E("xf_own", [128, NC * T], BF16)
        xf_kv = E("xf_kv", [128, NC * 2 * T], BF16)
        K_sb = E("K_sb", [128, 2048], BF16)
        V_sb = E("V_sb", [128, 2048], BF16)
        Q_sb = E("Q_sb", [128, 1024], BF16)
        ctx_sb = E("ctx", [128, NHEADS * T], BF16)
        s_tile = E("s_tile", [128, 2048], F32)
        p_bf = E("p_bf", [128, 2048], BF16)
        pT_sb = E("pT_sb", [128, 2048], BF16)
        wq_sb = E("wq_sb", [128, 16 * 128], BF16)
        wk_sb = E("wk_sb", [128, 16 * 128], BF16)
        wv_sb = E("wv_sb", [128, 16 * 128], BF16)
        ow_sb = [E(f"ow{i}", [128, 2048], BF16) for i in range(2)]
        bq16 = E("bq16", [1, 128], F16)
        bk16 = E("bk16", [1, 128], F16)
        bv16 = E("bv16", [1, 128], F16)
        ob16 = E("ob16", [1, D], F16)
        ones16 = E("ones16", [1, 512], F16)
        ident_bf = E("ident_bf", [128, 128], BF16)
        xot = E("xot", [128, 1024], BF16)
        xres_sb = E("xres_sb", [128, D], F32)
        xsum_sb = E("xsum_sb", [128, D], F32)
        xout_sb = xres_sb
        negmax = E("negmax", [128, 1], F32)
        rowsum = E("rowsum", [128, 1], F32)
        rcol = E("rcol", [128, 1], F32)
        eps_col = E("eps_col", [128, 1], F32)
        stats = {"ssum": E("ssum", [128, NT], F32),
                 "ssq": E("ssq", [128, NT], F32),
                 "m": E("mcol", [128, 1], F32),
                 "tmp": E("tmpc", [128, 6], F32)}

        p.gdma(ident_bf[:], ident)
        p.gdma(ob16[:], bo.rearrange("(o n) -> o n", o=1))
        V.memset(ones16[:], 1.0)
        p.inc(V.memset(eps_col[:], LN_EPS), "eV")

        emit_stream_to_fm(p, lambda t: x_own[t * 128:(t + 1) * 128, :],
                          xf_own, xot, PS_T, ident_bf)
        emit_stream_to_fm(p, lambda t: x_kv[t * 128:(t + 1) * 128, :],
                          xf_kv, xot, PS_T, ident_bf, ntiles=2 * NT)

        prod_ps = [PS_A, PS_B]
        npd = 0
        for g in range(4):
            p.wait(G, "eP")
            p.gdma(wk_sb[:].rearrange("p (c j) -> p c j", j=128),
                   Wk[:, g * 128:(g + 1) * 128].rearrange("(c p) j -> p c j", p=128))
            p.gdma(wv_sb[:].rearrange("p (c j) -> p c j", j=128),
                   Wv[:, g * 128:(g + 1) * 128].rearrange("(c p) j -> p c j", p=128))
            p.gdma(bk16[:], bk.rearrange("(o n) -> o n", o=1)[0:1, g * 128:(g + 1) * 128])
            p.gdma(bv16[:], bv.rearrange("(o n) -> o n", o=1)[0:1, g * 128:(g + 1) * 128])
            p.wait(PE, "dW", "eV")
            for kt in range(4):
                ps = prod_ps[npd % 2][:]
                npd += 1
                for c in range(NC):
                    PE.matmul(ps, wk_sb[:, c * 128:(c + 1) * 128],
                              xf_kv[:, c * 2 * T + kt * 512: c * 2 * T + (kt + 1) * 512],
                              start=(c == 0), stop=False)
                p.inc(PE.matmul(ps, bk16[0:1, :], ones16[0:1, :],
                                start=False, stop=True), "eP")
                p.wait(V, "eP")
                p.inc(V.tensor_copy(K_sb[:, kt * 512:(kt + 1) * 512], ps), "eV")
                p.wait(PE, "eV")
            for vt in range(16):
                ps = prod_ps[npd % 2][:]
                npd += 1
                for c in range(NC):
                    PE.matmul(ps[:, 0:128],
                              xf_kv[:, c * 2 * T + vt * 128: c * 2 * T + vt * 128 + 128],
                              wv_sb[:, c * 128:(c + 1) * 128],
                              start=(c == 0), stop=False)
                p.inc(PE.matmul(ps[:, 0:128], ones16[0:1, 0:128], bv16[0:1, :],
                                start=False, stop=True), "eP")
                p.wait(V, "eP")
                p.inc(V.tensor_copy(V_sb[:, vt * 128:(vt + 1) * 128], ps[:, 0:128]),
                      "eV")
                p.wait(PE, "eV")
            for hh in range(4):
                h = 4 * g + hh
                p.wait(G, "eP")
                p.gdma(wq_sb[:].rearrange("p (c j) -> p c j", j=128),
                       Wq[:, h * 128:(h + 1) * 128].rearrange("(c p) j -> p c j", p=128))
                p.gdma(bq16[:], bq.rearrange("(o n) -> o n", o=1)[0:1, h * 128:(h + 1) * 128])
                p.wait(PE, "dW", "eV")
                for qs in range(2):
                    ps = prod_ps[npd % 2][:]
                    npd += 1
                    for c in range(NC):
                        PE.matmul(ps, wq_sb[:, c * 128:(c + 1) * 128],
                                  xf_own[:, c * T + qs * 512: c * T + qs * 512 + 512],
                                  start=(c == 0), stop=False)
                    p.inc(PE.matmul(ps, bq16[0:1, :], ones16[0:1, :],
                                    start=False, stop=True), "eP")
                    p.wait(V, "eP")
                    p.inc(V.tensor_scalar_mul(Q_sb[:, qs * 512:(qs + 1) * 512], ps,
                                              QSCALE), "eV")
                    p.wait(PE, "eV")
                for qt in range(NT):
                    p.wait(PE, "eV", "eA")
                    for kt in range(4):
                        ins = PE.matmul(PS_BIG[:, kt * 512:(kt + 1) * 512],
                                        Q_sb[:, qt * 128:(qt + 1) * 128],
                                        K_sb[:, kt * 512:(kt + 1) * 512],
                                        start=True, stop=True)
                    p.inc(ins, "eP")
                    p.wait(V, "eP")
                    p.inc(V.tensor_reduce(negmax[:], PS_BIG[:], AX.XY,
                                          ALU.max, negate=True), "eV")
                    p.wait(Sc, "eV")
                    p.inc(Sc.activation(s_tile[:], PS_BIG[:], AF.Exp,
                                        bias=negmax[:], accum_out=rowsum[:]), "eA")
                    p.wait(V, "eA")
                    p.inc(V.reciprocal(rcol[:], rowsum[:]), "eV")
                    p.wait(Sc, "eV")
                    p.inc(Sc.activation(p_bf[:], s_tile[:], AF.Copy,
                                        scale=rcol[:]), "eA")
                    done = 0
                    while done < 16:
                        p.wait(PE, "eV", "eA")
                        for j in range(8):
                            kc = done + j
                            ins = PE.transpose(PS_T[:, j * 128:(j + 1) * 128],
                                               p_bf[:, kc * 128:(kc + 1) * 128],
                                               ident_bf[:])
                        p.inc(ins, "eP")
                        p.wait(V, "eP")
                        for j in range(8):
                            kc = done + j
                            ins = V.tensor_copy(pT_sb[:, kc * 128:(kc + 1) * 128],
                                                PS_T[:, j * 128:(j + 1) * 128])
                        p.inc(ins, "eV")
                        done += 8
                    p.wait(PE, "eV", "eA")
                    for kc in range(16):
                        ins = PE.matmul(PS_C[:, 0:128], V_sb[:, kc * 128:(kc + 1) * 128],
                                        pT_sb[:, kc * 128:(kc + 1) * 128],
                                        start=(kc == 0), stop=(kc == 15))
                    p.inc(ins, "eP")
                    p.wait(Sc, "eP")
                    p.inc(Sc.copy(ctx_sb[:, h * T + qt * 128: h * T + (qt + 1) * 128],
                                  PS_C[:, 0:128]), "eA")

        ln_snap = 0
        p.wait(PE, "eA", "eV")
        for tt in range(NT):
            p.wait(SY, "eV")
            p.sdma(xres_sb[:], x_own[tt * 128:(tt + 1) * 128, :])
            p.wait_val(PE, "eV", ln_snap)
            for hc in range(NC):
                p.wait(G, "eP")
                p.gdma(ow_sb[hc % 2][:], Wo[hc * 128:(hc + 1) * 128, :])
                p.wait(PE, "dW")
                for js in range(4):
                    ins = PE.matmul(PS_BIG[:, js * 512:(js + 1) * 512],
                                    ctx_sb[:, hc * T + tt * 128: hc * T + (tt + 1) * 128],
                                    ow_sb[hc % 2][:, js * 512:(js + 1) * 512],
                                    start=(hc == 0), stop=False)
                p.inc(ins, "eP")
            for js in range(4):
                ins = PE.matmul(PS_BIG[:, js * 512:(js + 1) * 512], ones16[0:1, 0:128],
                                ob16[0:1, js * 512:(js + 1) * 512], start=False,
                                stop=True)
            p.inc(ins, "eP")
            p.wait(V, "eP", "dX")
            ln_snap = emit_ln_tail(p, PS_BIG[:], xres_sb[:], xsum_sb[:], xout_sb[:],
                                   xres_sb[:], stats, tt, eps_col)
            p.wait(SY, "eV")
            p.sdma(x3_out[tt * 128:(tt + 1) * 128, :], xout_sb[:])
        p.wait(SY, "dX")


def emit_l1f(nc, p, io):
    x_own, w1t, w3t, w2t = io["x_in"], io["w1t"], io["w3t"], io["w2t"]
    h128, ident, x4_out = io["h128"], io["ident"], io["x_out"]
    PE, V, Sc, G, SY = nc.tensor, nc.vector, nc.scalar, nc.gpsimd, nc.sync
    p.barrier()
    with contextlib.ExitStack() as stk:
        E = lambda n, *a, **k: stk.enter_context(nc.sbuf_tensor("f1f1" + n, *a, **k))
        EP = lambda n, *a, **k: stk.enter_context(nc.psum_tensor("f1" + n, *a, **k))
        PS_BIG = EP("PS_BIG", [128, 2048], F32)
        PS_A = EP("PS_A", [128, 512], F32)
        PS_B = EP("PS_B", [128, 512], F32)
        PS_C = EP("PS_C", [128, 512], F32)
        PS_T = EP("PS_T", [128, 1024], F16)

        h128_16 = E("h128_16", [128, 128], F16)
        ident_16 = E("ident_16", [128, 128], F16)
        fsc_col = E("fsc_col", [128, 1], F32)
        eps_col = E("eps_col", [128, 1], F32)
        stats = {"ssum": E("ssum", [128, NT], F32),
                 "ssq": E("ssq", [128, NT], F32),
                 "m": E("mcol", [128, 1], F32),
                 "tmp": E("tmpc", [128, 6], F32)}

        p.gdma(h128_16[:], h128)
        p.gdma(ident_16[:], ident)
        V.memset(fsc_col[:], FWHT_SCALE)
        p.inc(V.memset(eps_col[:], LN_EPS), "eV")

        hid = E("hid", [128, 64 * 1024], F16)

        with (nc.sbuf_tensor("f1xf", [128, NC * T], F16) as xf,
              nc.sbuf_tensor("f1xot", [128, 1024], F16) as xot,
              nc.sbuf_tensor("f1w1s0", [128, 16 * 128], F16) as w1s0,
              nc.sbuf_tensor("f1w1s1", [128, 16 * 128], F16) as w1s1,
              nc.sbuf_tensor("f1w3s0", [128, 16 * 128], F16) as w3s0,
              nc.sbuf_tensor("f1w3s1", [128, 16 * 128], F16) as w3s1,
              nc.sbuf_tensor("f1gsil", [128, 1024], F16) as gsil):
            w1s = [w1s0, w1s1]
            w3s = [w3s0, w3s1]
            emit_stream_to_fm(p, lambda t: x_own[t * 128:(t + 1) * 128, :],
                              xf, xot, PS_T, ident_16)
            ldu = [0] * 66
            mmu = [0] * 66
            silu_s = [0] * 66
            hidu = [0] * 66

            def u_load(i):
                if i < 0 or i > 63:
                    return
                p.wait_val(G, "eP", mmu[i - 2] if i >= 2 else 0)
                p.gdma(w1s[i % 2][:].rearrange("p (c j) -> p c j", j=128),
                       w1t[:, i * 128:(i + 1) * 128].rearrange(
                           "(c p) j -> p c j", p=128))
                p.gdma(w3s[i % 2][:].rearrange("p (c j) -> p c j", j=128),
                       w3t[:, i * 128:(i + 1) * 128].rearrange(
                           "(c p) j -> p c j", p=128))
                ldu[i] = p.snap("dW")

            u_load(0)
            u_load(1)
            for hc in range(64):
                p.wait_val(PE, "dW", ldu[hc])
                if hc >= 1:
                    p.wait_val(PE, "eA", silu_s[hc - 1])
                    p.wait_val(PE, "eV", hidu[hc - 1])
                else:
                    p.wait(PE, "eV")
                for ws, off in ((w1s[hc % 2], 0), (w3s[hc % 2], 1024)):
                    for ts in range(2):
                        dst = PS_BIG[:, off + ts * 512: off + (ts + 1) * 512]
                        for c in range(NC):
                            ins = PE.matmul(
                                dst, ws[:, c * 128:(c + 1) * 128],
                                xf[:, c * T + ts * 512: c * T + ts * 512 + 512],
                                start=(c == 0), stop=(c == NC - 1))
                p.inc(ins, "eP")
                mmu[hc] = p.snap("eP")
                u_load(hc + 2)
                p.wait_val(Sc, "eP", mmu[hc])
                p.inc(Sc.activation(gsil[:], PS_BIG[:, 0:1024], AF.Silu), "eA")
                silu_s[hc] = p.snap("eA")
                p.wait_val(V, "eA", silu_s[hc])
                p.inc(V.tensor_tensor(hid[:, hc * 1024:(hc + 1) * 1024], gsil[:],
                                      PS_BIG[:, 1024:2048], ALU.mult), "eV")
                hidu[hc] = p.snap("eV")

        with nc.sbuf_tensor("f1tmp16", [128, 1024], F16) as tmp16:
            for bit in (1, 2, 4, 8, 16, 32):
                p.wait(V, "eV")
                for c in range(64):
                    if c & bit:
                        continue
                    a = hid[:, c * 1024:(c + 1) * 1024]
                    b = hid[:, (c + bit) * 1024:(c + bit + 1) * 1024]
                    V.tensor_tensor(tmp16[:], a, b, ALU.subtract)
                    V.tensor_tensor(a, a, b, ALU.add)
                    V.tensor_copy(b, tmp16[:])
                p.inc(V.tensor_copy(tmp16[:, 0:1], tmp16[:, 0:1]), "eV")

        for hc in range(64):
            p.wait(PE, "eV", "eA")
            for ts in range(2):
                ps = [PS_A, PS_B][ts][:]
                ins = PE.matmul(ps, h128_16[:],
                                hid[:, hc * 1024 + ts * 512: hc * 1024 + ts * 512 + 512],
                                start=True, stop=True)
            p.inc(ins, "eP")
            p.wait(Sc, "eP")
            for ts in range(2):
                ins = Sc.activation(
                    hid[:, hc * 1024 + ts * 512: hc * 1024 + ts * 512 + 512],
                    [PS_A, PS_B][ts][:], AF.Copy, scale=fsc_col[:])
            p.inc(ins, "eA")

        proj_16 = E("proj_16", [128, NC * T], F16)
        with (nc.sbuf_tensor("f1w2s0", [128, 64 * 128], F16) as w2sA,
              nc.sbuf_tensor("f1w2s1", [128, 64 * 128], F16) as w2sB):
            w2s = [w2sA, w2sB]
            pp = [PS_A, PS_B]
            ld2 = [0] * 18
            mm2 = [0] * 18
            ev2 = [0] * 18

            def p2_load(i):
                if i < 0 or i > 15:
                    return
                p.wait_val(G, "eP", mm2[i - 2] if i >= 2 else 0)
                for qj in range(4):
                    p.gdma(w2s[i % 2][:, qj * 16 * 128:(qj + 1) * 16 * 128].rearrange(
                               "p (c j) -> p c j", j=128),
                           w2t[:, i * 128:(i + 1) * 128].rearrange(
                               "(c p) j -> p c j", p=128)[:, qj * 16:(qj + 1) * 16, :])
                ld2[i] = p.snap("dW")

            p.wait(G, "eP", "eV", "eA")
            p2_load(0)
            p2_load(1)
            for dc in range(NC):
                p.wait_val(PE, "dW", ld2[dc])
                p.wait_val(PE, "eV", ev2[dc - 1] if dc >= 1 else 0)
                p.wait(PE, "eA")
                for ts in range(2):
                    ps = pp[ts][:]
                    for hc in range(64):
                        ins = PE.matmul(ps, w2s[dc % 2][:, hc * 128:(hc + 1) * 128],
                                        hid[:, hc * 1024 + ts * 512:
                                            hc * 1024 + ts * 512 + 512],
                                        start=(hc == 0), stop=(hc == 63))
                    p.inc(ins, "eP")
                mm2[dc] = p.snap("eP")
                p2_load(dc + 2)
                p.wait_val(V, "eP", mm2[dc])
                for ts in range(2):
                    ins = V.tensor_copy(
                        proj_16[:, dc * T + ts * 512: dc * T + (ts + 1) * 512],
                        pp[ts][:])
                p.inc(ins, "eV")
                ev2[dc] = p.snap("eV")

        with (nc.sbuf_tensor("f1xproj", [128, D], F16) as xproj,
              nc.sbuf_tensor("f1xres", [128, D], F32) as xres_t,
              nc.sbuf_tensor("f1xsum", [128, D], F32) as xsum_t):
            xres_sb = xres_t
            xsum_sb = xsum_t
            xout_sb = xres_sb
            for tt in range(NT):
                p.wait(SY, "eV", "eP")
                p.sdma(xres_sb[:], x_own[tt * 128:(tt + 1) * 128, :])
                for half in range(2):
                    p.wait(PE, "eV")
                    for j in range(8):
                        dc = half * 8 + j
                        ins = PE.transpose(
                            PS_T[:, j * 128:(j + 1) * 128],
                            proj_16[:, dc * T + tt * 128: dc * T + (tt + 1) * 128],
                            ident_16[:])
                        if j == 7:
                            p.inc(ins, "eP")
                    p.wait(V, "eP")
                    for j in range(8):
                        dc = half * 8 + j
                        ins = V.tensor_copy(xproj[:, dc * 128:(dc + 1) * 128],
                                            PS_T[:, j * 128:(j + 1) * 128])
                        if j == 7:
                            p.inc(ins, "eV")
                p.wait(V, "dX")
                emit_ln_tail(p, xproj[:], xres_sb[:], xsum_sb[:], xout_sb[:],
                             xres_sb[:], stats, tt, eps_col)
                p.wait(SY, "eV")
                p.sdma(x4_out[tt * 128:(tt + 1) * 128, :], xout_sb[:])
            p.wait(SY, "dX")


def emit_exchange(nc, p, x2_d, kv_sh, fl_sh, magic):
    G = nc.gpsimd
    p.barrier()
    with nc.sbuf_tensor("xm_sb", [1, 64], U32) as m_sb, \
         nc.sbuf_tensor("xf_sb", [1, 64], U32) as f_sb:
        p.sdma(m_sb[:], magic)
        p.wait(G, "dX", "dW")
        pid = G.partition_id()
        par = pid % 2
        sib = 1 - par
        p.gdma(kv_sh[bass.ds(par * T, T), :], x2_d)
        p.wait(G, "dW")
        p.gdma(fl_sh[bass.ds(par, 1), :], m_sb[:])
        p.wait(G, "dW")
        psem = nc.alloc_semaphore("psem")
        with G.register("pv") as pv, G.register("sv") as sv, \
             G.register("it") as it, G.register("mg") as mg, \
             G.register("cond") as cond:
            G.reg_load(mg, m_sb[0:1, 0:1])
            G.reg_mov(pv, 0)
            G.reg_mov(it, 0)
            G.reg_mov(sv, 0)
            G.reg_mov(cond, 1)
            with G.While(cond):
                G.reg_add(sv, sv, 16)
                G.dma_start(out=f_sb[:], in_=fl_sh[bass.ds(sib, 1), :]).then_inc(
                    psem, 16)
                G.wait_ge(psem, sv)
                G.reg_load(pv, f_sb[0:1, 0:1])
                G.reg_add(it, it, 1)
                G.reg_alu(cond, pv, mg, ALU.not_equal)
                with G.register("lim") as lim:
                    G.reg_alu(lim, it, 40000, ALU.is_lt)
                    G.reg_alu(cond, cond, lim, ALU.mult)



def build_fused():
    nc = bass.Bass("TRN2", target_bir_lowering=False)
    p = Ph(nc)

    def ein(name, shape):
        return nc.dram_tensor(name, shape, F32, kind="ExternalInput").ap()

    x_own = ein("x_own", [T, D])
    x_ctx = ein("x_ctx", [T, D])
    qkv_w = ein("qkv_w", [D, 3 * D])
    qkv_b = ein("qkv_b", [3 * D])
    out_w = ein("out_w", [D, D])
    out_b = ein("out_b", [D])
    maskB = nc.dram_tensor("maskB", [128, 2048], F16, kind="ExternalInput").ap()
    tri = ein("tri", [128, 128])
    ident = ein("ident", [128, 128])
    wgu = ein("wgu", [HGU, D])
    wpr = ein("wpr", [D, HH])
    Wq = ein("Wq", [D, D]); bq = ein("bq", [D])
    Wk = ein("Wk", [D, 512]); bk = ein("bk", [512])
    Wv = ein("Wv", [D, 512]); bv = ein("bv", [512])
    Wo = ein("Wo", [D, D]); bo = ein("bo", [D])
    w1t = ein("w1t", [D, HH])
    w3t = ein("w3t", [D, HH])
    w2t = ein("w2t", [HH, D])
    h128 = ein("h128", [128, 128])
    magic = nc.dram_tensor("magic", [1, 64], U32, kind="ExternalInput").ap()
    x4_out = nc.dram_tensor("x4", [T, D], F32, kind="ExternalOutput").ap()

    x1_d = nc.dram_tensor("x1_d", [T, D], F32, kind="Internal").ap()
    x2_d = nc.dram_tensor("x2_d", [T, D], F32, kind="Internal").ap()
    x3_d = nc.dram_tensor("x3_d", [T, D], F32, kind="Internal").ap()
    wpt_d = nc.dram_tensor("wpt_d", [64, 16, 128, 128], BF16, kind="Internal").ap()
    kv_sh = nc.dram_tensor("kv_sh", [2 * T, D], F32, kind="Internal",
                           addr_space="Shared").ap()
    fl_sh = nc.dram_tensor("fl_sh", [2, 64], U32, kind="Internal",
                           addr_space="Shared").ap()

    DEBUG = globals().get("FUSED_DEBUG", False)
    dbg2 = None
    if DEBUG:
        dx1 = nc.dram_tensor("dx1", [T, D], F32, kind="ExternalOutput").ap()
        dx2 = nc.dram_tensor("dx2", [T, D], F32, kind="ExternalOutput").ap()
        dx3 = nc.dram_tensor("dx3", [T, D], F32, kind="ExternalOutput").ap()
        dbg2 = {n: nc.dram_tensor(n, s, F32, kind="ExternalOutput").ap()
                for n, s in [("d_weffg", [128, 4096]), ("d_hid", [128, 2048]),
                             ("d_rms", [128, 8]), ("d_proj", [128, 1024])]}
    with nc.body("main"):
        emit_l0a(nc, p, dict(x_in=x_own, x_ctx=x_ctx, qkv_w=qkv_w, qkv_b=qkv_b,
                             out_w=out_w, out_b=out_b, maskB=maskB, tri=tri,
                             ident=ident, x_out=x1_d))
        emit_l0f(nc, p, dict(x_in=x1_d, wgu=wgu, wpr=wpr, ident=ident,
                             x_out=x2_d, wpt=wpt_d, dbg=dbg2))
        emit_exchange(nc, p, x2_d, kv_sh, fl_sh, magic)
        emit_l1a(nc, p, dict(x_in=x2_d, x_kv=kv_sh, Wq=Wq, bq=bq, Wk=Wk, bk=bk,
                             Wv=Wv, bv=bv, Wo=Wo, bo=bo, ident=ident,
                             x_out=x3_d))
        emit_l1f(nc, p, dict(x_in=x3_d, w1t=w1t, w3t=w3t, w2t=w2t, h128=h128,
                             ident=ident, x_out=x4_out))
        if DEBUG:
            p.barrier()
            p.gdma(dx1, x1_d)
            p.gdma(dx2, x2_d)
            p.gdma(dx3, x3_d)
            nc.gpsimd.wait_ge(p.sem["dW"], p.cnt["dW"])
    return nc



def make_l0a_consts():
    iden = np.eye(128, dtype=np.float32)
    trim = np.where(np.arange(128)[None, :] <= np.arange(128)[:, None],
                    0.0, MASKVAL).astype(np.float32)
    masks = []
    for c in range(8):
        half = c % 2
        B = np.zeros((128, 2048), dtype=np.float16)
        if half == 0:
            B[:, 0:1024] = np.float16(MASKVAL)
            B[:, 1024:2048] = np.arange(0, 1024, dtype=np.float16)[None, :]
        else:
            B[:, :] = np.arange(0, 2048, dtype=np.float16)[None, :]
        masks.append(B)
    return masks, trim, iden


def make_h128():
    h = np.array([[1.0]], dtype=np.float32)
    while h.shape[0] < 128:
        h = np.block([[h, h], [h, -h]])
    return h.astype(np.float32)


def shard_rows(x):
    return [np.ascontiguousarray(x[c * T:(c + 1) * T]) for c in range(8)]


def ctx_rows(x):
    out = []
    for c in range(8):
        b, half = c // 2, c % 2
        o = (b * 2 + (1 - half)) * T
        out.append(np.ascontiguousarray(x[o:o + T]))
    return out


_BUILT = None


def kernel(**inputs):
    global _BUILT
    ins = {k: np.ascontiguousarray(np.asarray(v)) for k, v in inputs.items()}
    x0 = ins["L0_attn_x"].reshape(8192, 2048)
    masks, trim, iden = make_l0a_consts()
    owns = shard_rows(x0)
    ctxs = ctx_rows(x0)
    w1tt = np.ascontiguousarray(ins["L1_ffn_W1"].T)
    w3tt = np.ascontiguousarray(ins["L1_ffn_W3"].T)
    w2tt = np.ascontiguousarray(ins["L1_ffn_W2"].T)
    magic = np.full((1, 64), np.random.randint(1, 2**31, dtype=np.int64),
                    dtype=np.uint32)
    if _BUILT is None:
        _BUILT = build_fused()
    nc = _BUILT
    in_maps = []
    for c in range(8):
        in_maps.append({
            "x_own": owns[c], "x_ctx": ctxs[c],
            "qkv_w": ins["L0_attn_qkv_w"], "qkv_b": ins["L0_attn_qkv_b"],
            "out_w": ins["L0_attn_out_w"], "out_b": ins["L0_attn_out_b"],
            "maskB": masks[c], "tri": trim, "ident": iden,
            "wgu": ins["L0_ffn_w_gate_up"], "wpr": ins["L0_ffn_w_proj"],
            "Wq": ins["L1_attn_Wq"], "bq": ins["L1_attn_bq"],
            "Wk": ins["L1_attn_Wk"], "bk": ins["L1_attn_bk"],
            "Wv": ins["L1_attn_Wv"], "bv": ins["L1_attn_bv"],
            "Wo": ins["L1_attn_Wo"], "bo": ins["L1_attn_bo"],
            "w1t": w1tt, "w3t": w3tt, "w2t": w2tt,
            "h128": make_h128(), "magic": magic,
        })
    res = run_bass_kernel_spmd(nc, in_maps, core_ids=list(range(8)))
    x4 = np.concatenate([r["x4"] for r in res.results], axis=0)
    return x4.reshape(4, 2048, 2048)


# version 08
# speedup vs baseline: 1.5296x; 1.1037x over previous
import contextlib
import math
import numpy as np
import ml_dtypes
import concourse.bass as bass
import concourse.mybir as mybir
from concourse.bass_utils import run_bass_kernel_spmd

F32 = mybir.dt.float32
BF16 = mybir.dt.bfloat16
F16 = mybir.dt.float16
U32 = mybir.dt.uint32
AF = mybir.ActivationFunctionType
ALU = mybir.AluOpType
AX = mybir.AxisListType

T = 1024
D = 2048
NT = T // 128
NC = D // 128
NHEADS = 16
HS = 128
HGU = 16384
HH = 8192
MASKVAL = -30000.0
LN_EPS = 1e-5
RMS_EPS = float(np.finfo(np.float32).eps)
INV2048 = 1.0 / 2048.0
INV64 = 1.0 / 64.0
INV8192 = 1.0 / 8192.0
QSCALE = 1.0 / math.sqrt(128.0)
FWHT_SCALE = 1.0 / math.sqrt(8192.0)
THR = 0.5009765625


def bcast(ap, n):
    return bass.AP(ap.tensor, ap.offset, [list(x) for x in ap.ap] + [[0, n]])


class Ph:
    def __init__(self, nc):
        self.nc = nc
        self.sem = {}
        self.cnt = {}
        for name in ("dW", "dX", "eP", "eV", "eA", "eG"):
            self.sem[name] = nc.alloc_semaphore(name)
            self.cnt[name] = 0

    def inc(self, instr, name, n=1):
        instr.then_inc(self.sem[name], n)
        self.cnt[name] += n
        return instr

    def wait(self, eng, *names):
        for name in names:
            if self.cnt[name] > 0:
                eng.wait_ge(self.sem[name], self.cnt[name])

    def snap(self, name):
        return self.cnt[name]

    def wait_val(self, eng, name, val):
        if val > 0:
            eng.wait_ge(self.sem[name], val)

    def barrier(self):
        nc = self.nc
        for eng in (nc.tensor, nc.vector, nc.scalar, nc.gpsimd, nc.sync):
            self.wait(eng, "dW", "dX", "eP", "eV", "eA")

    def gdma(self, out, in_):
        return self.inc(self.nc.gpsimd.dma_start(out=out, in_=in_), "dW", 16)

    def sdma(self, out, in_):
        return self.inc(self.nc.sync.dma_start(out=out, in_=in_), "dX", 16)



def emit_ln_tail(p, ps_big, xres_sb, xsum_sb, xout_sb, trash_f32, stats, tt,
                 eps_col, rstd_extra=None):
    V, A = p.nc.vector, p.nc.scalar
    m, tmp = stats["m"], stats["tmp"]
    if rstd_extra is None:
        ins = V.scalar_tensor_tensor(xsum_sb, ps_big, 0.0, xres_sb,
                                     ALU.add, ALU.add)
    else:
        ins = V.scalar_tensor_tensor(xsum_sb, ps_big, rstd_extra, xres_sb,
                                     ALU.mult, ALU.add)
    p.inc(ins, "eV")
    psum_free_snap = p.snap("eV")
    p.inc(V.tensor_reduce(stats["ssum"][:, tt:tt + 1], xsum_sb, AX.XY, ALU.add),
          "eV")
    p.wait(A, "eV")
    p.inc(A.activation(trash_f32, xsum_sb, AF.Square,
                       accum_out=stats["ssq"][:, tt:tt + 1]), "eA")
    p.wait(V, "eA")
    V.tensor_scalar_mul(m[:], stats["ssum"][:, tt:tt + 1], INV2048)
    p.inc(V.tensor_scalar_mul(tmp[:, 0:1], stats["ssq"][:, tt:tt + 1], INV2048),
          "eV")
    p.wait(A, "eV")
    p.inc(A.activation(tmp[:, 1:2], m[:], AF.Square), "eA")
    p.wait(V, "eA")
    p.inc(V.scalar_tensor_tensor(tmp[:, 2:3], tmp[:, 1:2], -1.0, tmp[:, 0:1],
                                 ALU.mult, ALU.add), "eV")
    p.wait(A, "eV")
    p.inc(A.activation(tmp[:, 3:4], tmp[:, 2:3], AF.Sqrt, bias=eps_col[:]), "eA")
    p.wait(V, "eA")
    p.inc(V.reciprocal(tmp[:, 4:5], tmp[:, 3:4]), "eV")
    p.wait(A, "eV")
    p.inc(A.copy(tmp[:, 5:6], tmp[:, 4:5]), "eA")
    p.wait(V, "eA")
    p.inc(V.tensor_scalar(xout_sb, xsum_sb, m[:], tmp[:, 5:6],
                          ALU.subtract, ALU.mult), "eV")
    return psum_free_snap


def emit_stream_to_fm(p, x_dram_rows, xf_sb, xot_sb, ps_t, ident_h, ntiles=NT):
    nc = p.nc
    PE, V = nc.tensor, nc.vector
    Tp = ntiles * 128
    for t in range(ntiles):
        for half in range(2):
            buf = xot_sb[:, 0:1024]
            p.wait(nc.gpsimd, "eP")
            p.gdma(buf, x_dram_rows(t)[:, half * 1024:(half + 1) * 1024])
            p.wait(PE, "dW", "eV")
            for cc in range(8):
                c = half * 8 + cc
                ins = PE.transpose(ps_t[:, cc * 128:(cc + 1) * 128],
                                   buf[:, cc * 128:(cc + 1) * 128], ident_h[:])
                if cc == 7:
                    p.inc(ins, "eP")
            p.wait(V, "eP")
            for cc in range(8):
                c = half * 8 + cc
                ins = V.tensor_copy(xf_sb[:, c * Tp + t * 128: c * Tp + (t + 1) * 128],
                                    ps_t[:, cc * 128:(cc + 1) * 128])
                if cc == 7:
                    p.inc(ins, "eV")


def emit_quant(p, wb, bufA, bufB, gsum, scale_bf, sc2_bf, thr_f32,
               inv64_col, thr_col, G):
    V, A = p.nc.vector, p.nc.scalar
    F = G * 64
    bv = lambda t: t[:, 0:F].rearrange("p (g e) -> p g e", e=64)
    p.wait(A, "eV")
    p.inc(A.activation(bufA[:, 0:F], wb[:, 0:F], AF.Abs), "eA")
    p.inc(A.activation(bufB[:, 0:F], wb[:, 0:F], AF.Sign), "eA")
    p.inc(V.tensor_reduce(gsum[:, 0:G], bv(wb), AX.X, ALU.add,
                          apply_absolute_value=True), "eV")
    p.wait(A, "eV")
    p.inc(A.activation(scale_bf[:, 0:G], gsum[:, 0:G], AF.Copy,
                       scale=inv64_col[:]), "eA")
    p.wait(V, "eA")
    p.inc(V.tensor_scalar_max(sc2_bf[:, 0:G], scale_bf[:, 0:G], 1e-8), "eV")
    p.wait(A, "eV")
    p.inc(A.activation(thr_f32[:, 0:G], sc2_bf[:, 0:G], AF.Copy,
                       scale=thr_col[:]), "eA")
    p.wait(V, "eA")
    V.tensor_tensor(bv(bufA), bv(bufA), bcast(thr_f32[:, 0:G], 64), ALU.is_gt)
    V.tensor_tensor(bufB[:, 0:F], bufA[:, 0:F], bufB[:, 0:F], ALU.mult)
    V.tensor_tensor(bv(bufB), bv(bufB), bcast(sc2_bf[:, 0:G], 64), ALU.mult)
    V.tensor_tensor(bufB[:, 0:F], bufB[:, 0:F], wb[:, 0:F], ALU.subtract)
    p.inc(V.tensor_tensor(wb[:, 0:F], bufB[:, 0:F], wb[:, 0:F], ALU.add), "eV")



def emit_l0a(nc, p, io):
    x_own, x_ctx = io["x_in"], io["x_ctx"]
    qkv_w, qkv_b = io["qkv_w"], io["qkv_b"]
    out_w, out_b = io["out_w"], io["out_b"]
    maskB, tri, ident = io["maskB"], io["tri"], io["ident"]
    x1_out = io["x_out"]

    PE, V, Sc, G, SY = nc.tensor, nc.vector, nc.scalar, nc.gpsimd, nc.sync
    p.barrier()
    with contextlib.ExitStack() as stk:
        E = lambda n, *a, **k: stk.enter_context(nc.sbuf_tensor("a0a0" + n, *a, **k))
        EP = lambda n, *a, **k: stk.enter_context(nc.psum_tensor("a0" + n, *a, **k))
        PS_BIG = EP("PS_BIG", [128, 2048], F32)
        PS_A = EP("PS_A", [128, 512], F32)
        PS_B = EP("PS_B", [128, 512], F32)
        PS_C = EP("PS_C", [128, 512], F32)
        PS_T = EP("PS_T", [128, 1024], BF16)

        ctx_sb = E("ctx", [128, NHEADS * T], BF16)
        ob16 = E("ob16", [1, D], F16)
        ones16 = E("ones16", [1, 512], F16)
        ident_bf = E("ident_bf", [128, 128], BF16)
        xres_sb = E("xres_sb", [128, D], F32)
        xsum_sb = E("xsum_sb", [128, D], F32)
        xout_sb = xres_sb
        negmax = E("negmax", [128, 1], F32)
        rowsum = E("rowsum", [128, 1], F32)
        rcol = E("rcol", [128, 1], F32)
        eps_col = E("eps_col", [128, 1], F32)
        stats = {"ssum": E("ssum", [128, NT], F32),
                 "ssq": E("ssq", [128, NT], F32),
                 "m": E("mcol", [128, 1], F32),
                 "tmp": E("tmpc", [128, 6], F32)}
        stk1 = contextlib.ExitStack()
        E1 = lambda n, *a, **k: stk1.enter_context(nc.sbuf_tensor("a0a0" + n, *a, **k))
        xot = E1("xot", [128, 1024], BF16)
        xf_own = E1("xf_own", [128, NC * T], BF16)
        xf_ctx = E1("xf_ctx", [128, NC * T], BF16)
        K_sb = [E1(f"K{i}", [128, 2048], BF16) for i in range(2)]
        V_sb = [E1(f"V{i}", [128, 2048], BF16) for i in range(2)]
        Q_sb = [E1(f"Q{i}", [128, 1024], BF16) for i in range(2)]
        maskB_sb = E1("maskB_sb", [128, 2048], F16)
        tri_bf = E1("tri_bf", [128, 128], BF16)
        s_tile = E1("s_tile", [128, 2048], F32)
        p_bf = E1("p_bf", [128, 2048], BF16)
        pT_sb = E1("pT_sb", [128, 2048], BF16)
        wq_sb = [E1(f"wq{i}", [128, 16 * 128], BF16) for i in range(2)]
        wk_sb = [E1(f"wk{i}", [128, 16 * 128], BF16) for i in range(2)]
        wv_sb = [E1(f"wv{i}", [128, 16 * 128], BF16) for i in range(2)]
        qb16 = E1("qb16", [1, 3 * D], F16)

        p.sdma(maskB_sb[:], maskB)
        p.gdma(tri_bf[:], tri)
        p.gdma(qb16[:], qkv_b.rearrange("(o n) -> o n", o=1))
        p.gdma(ob16[:], out_b.rearrange("(o n) -> o n", o=1))
        p.gdma(ident_bf[:], ident)
        V.memset(ones16[:], 1.0)
        p.inc(V.memset(eps_col[:], LN_EPS), "eV")

        emit_stream_to_fm(p, lambda t: x_own[t * 128:(t + 1) * 128, :],
                          xf_own, xot, PS_T, ident_bf)
        emit_stream_to_fm(p, lambda t: x_ctx[t * 128:(t + 1) * 128, :],
                          xf_ctx, xot, PS_T, ident_bf)

        def xf_tokslice(kt512):
            src = xf_ctx if kt512 < 2 else xf_own
            off = (kt512 % 2) * 512
            return src, off

        prod_ps = [PS_A, PS_B]
        for g in range(NHEADS // 2):
            for hh in range(2):
                h = 2 * g + hh
                p.wait(G, "eP")
                for wsb, col0 in ((wq_sb[hh], h * HS), (wk_sb[hh], D + h * HS),
                                  (wv_sb[hh], 2 * D + h * HS)):
                    p.gdma(wsb[:].rearrange("p (c j) -> p c j", j=128),
                           qkv_w[:, col0:col0 + HS].rearrange("(c p) j -> p c j", p=128))
                p.wait(PE, "dW", "eV")
                npd = 0
                for kt in range(4):
                    ps = prod_ps[npd % 2][:]
                    npd += 1
                    src, off = xf_tokslice(kt)
                    for c in range(NC):
                        PE.matmul(ps, wk_sb[hh][:, c * 128:(c + 1) * 128],
                                  src[:, c * T + off: c * T + off + 512],
                                  start=(c == 0), stop=False)
                    p.inc(PE.matmul(ps, qb16[0:1, D + h * HS:D + (h + 1) * HS],
                                    ones16[0:1, :], start=False, stop=True), "eP")
                    p.wait(V, "eP")
                    p.inc(V.tensor_copy(K_sb[hh][:, kt * 512:(kt + 1) * 512], ps), "eV")
                    p.wait(PE, "eV")
                for vt in range(16):
                    ps = prod_ps[npd % 2][:]
                    npd += 1
                    src = xf_ctx if vt < 8 else xf_own
                    toff = (vt % 8) * 128
                    for c in range(NC):
                        PE.matmul(ps[:, 0:128],
                                  src[:, c * T + toff: c * T + toff + 128],
                                  wv_sb[hh][:, c * 128:(c + 1) * 128],
                                  start=(c == 0), stop=False)
                    p.inc(PE.matmul(ps[:, 0:128], ones16[0:1, 0:128],
                                    qb16[0:1, 2 * D + h * HS:2 * D + (h + 1) * HS],
                                    start=False, stop=True), "eP")
                    p.wait(V, "eP")
                    p.inc(V.tensor_copy(V_sb[hh][:, vt * 128:(vt + 1) * 128],
                                        ps[:, 0:128]), "eV")
                    p.wait(PE, "eV")
                for qs in range(2):
                    ps = prod_ps[npd % 2][:]
                    npd += 1
                    for c in range(NC):
                        PE.matmul(ps, wq_sb[hh][:, c * 128:(c + 1) * 128],
                                  xf_own[:, c * T + qs * 512: c * T + qs * 512 + 512],
                                  start=(c == 0), stop=False)
                    p.inc(PE.matmul(ps, qb16[0:1, h * HS:(h + 1) * HS],
                                    ones16[0:1, :], start=False, stop=True), "eP")
                    p.wait(V, "eP")
                    p.inc(V.tensor_copy(Q_sb[hh][:, qs * 512:(qs + 1) * 512], ps), "eV")
                    p.wait(PE, "eV")

                slope = 1.0 / 2.0 ** (h / 2.0)
                for qt in range(NT):
                    nk = 3 if qt < 4 else 4
                    ve = 1024 + (qt + 1) * 128
                    kch = ve // 128
                    p.wait(PE, "eV")
                    dj = 1024 + qt * 128
                    w = nk * 512
                    for kt in range(nk):
                        dslc = PS_BIG[:, kt * 512:(kt + 1) * 512]
                        PE.matmul(dslc, Q_sb[hh][:, qt * 128:(qt + 1) * 128],
                                  K_sb[hh][:, kt * 512:(kt + 1) * 512],
                                  start=True, stop=(dj // 512 != kt))
                    p.inc(PE.matmul(PS_BIG[:, dj:dj + 128], ident_bf[:], tri_bf[:],
                                    start=False, stop=True), "eP")
                    p.wait(V, "eP", "eA")
                    V.scalar_tensor_tensor(s_tile[:, 0:ve], maskB_sb[:, 0:ve], slope,
                                           PS_BIG[:, 0:ve], ALU.mult, ALU.add)
                    p.inc(V.tensor_reduce(negmax[:], s_tile[:, 0:ve], AX.XY,
                                          ALU.max, negate=True), "eV")
                    p.wait(Sc, "eV")
                    p.inc(Sc.activation(s_tile[:, 0:ve], s_tile[:, 0:ve], AF.Exp,
                                        bias=negmax[:], accum_out=rowsum[:]), "eA")
                    p.wait(V, "eA")
                    p.inc(V.reciprocal(rcol[:], rowsum[:]), "eV")
                    p.wait(Sc, "eV")
                    p.inc(Sc.activation(p_bf[:, 0:ve], s_tile[:, 0:ve], AF.Copy,
                                        scale=rcol[:]), "eA")
                    done = 0
                    while done < kch:
                        bn = min(8, kch - done)
                        p.wait(PE, "eV", "eA")
                        for j in range(bn):
                            kc = done + j
                            ins = PE.transpose(PS_T[:, j * 128:(j + 1) * 128],
                                               p_bf[:, kc * 128:(kc + 1) * 128],
                                               ident_bf[:])
                            if j == bn - 1:
                                p.inc(ins, "eP")
                        p.wait(V, "eP")
                        for j in range(bn):
                            kc = done + j
                            ins = V.tensor_copy(pT_sb[:, kc * 128:(kc + 1) * 128],
                                                PS_T[:, j * 128:(j + 1) * 128])
                            if j == bn - 1:
                                p.inc(ins, "eV")
                        done += bn
                    p.wait(PE, "eV", "eA")
                    for kc in range(kch):
                        ins = PE.matmul(PS_C[:, 0:128],
                                        V_sb[hh][:, kc * 128:(kc + 1) * 128],
                                        pT_sb[:, kc * 128:(kc + 1) * 128],
                                        start=(kc == 0), stop=(kc == kch - 1))
                    p.inc(ins, "eP")
                    p.wait(Sc, "eP")
                    p.inc(Sc.copy(ctx_sb[:, h * T + qt * 128: h * T + (qt + 1) * 128],
                                  PS_C[:, 0:128]), "eA")

        stk1.close()
        proj_bf = E("proj_bf", [128, NC * T], BF16)
        with (nc.sbuf_tensor("a0ows0", [128, 16 * 128], BF16) as ows0,
              nc.sbuf_tensor("a0ows1", [128, 16 * 128], BF16) as ows1,
              nc.sbuf_tensor("a0xproj", [128, D], BF16) as xproj):
            ows = [ows0, ows1]
            pp = [PS_A, PS_B]
            ldo = [0] * 18
            mmo = [0] * 18
            evo = [0] * 18

            def wo_load(i):
                if i < 0 or i > 15:
                    return
                p.wait_val(G, "eP", mmo[i - 2] if i >= 2 else 0)
                p.gdma(ows[i % 2][:].rearrange("p (c j) -> p c j", j=128),
                       out_w[:, i * 128:(i + 1) * 128].rearrange(
                           "(c p) j -> p c j", p=128))
                ldo[i] = p.snap("dW")

            p.wait(G, "eP", "eV", "eA")
            wo_load(0)
            wo_load(1)
            p.wait(PE, "eV", "eA")
            for dc in range(NC):
                p.wait_val(PE, "dW", ldo[dc])
                p.wait_val(PE, "eV", evo[dc - 1] if dc >= 1 else 0)
                for ts in range(2):
                    ps = pp[ts][:]
                    for c in range(NC):
                        PE.matmul(ps, ows[dc % 2][:, c * 128:(c + 1) * 128],
                                  ctx_sb[:, c * T + ts * 512: c * T + ts * 512 + 512],
                                  start=(c == 0), stop=False)
                    ins = PE.matmul(ps, ob16[0:1, dc * 128:(dc + 1) * 128],
                                    ones16[0:1, 0:512], start=False, stop=True)
                    p.inc(ins, "eP")
                mmo[dc] = p.snap("eP")
                wo_load(dc + 2)
                p.wait_val(V, "eP", mmo[dc])
                for ts in range(2):
                    ins = V.tensor_copy(
                        proj_bf[:, dc * T + ts * 512: dc * T + (ts + 1) * 512],
                        pp[ts][:])
                p.inc(ins, "eV")
                evo[dc] = p.snap("eV")

            for tt in range(NT):
                p.wait(SY, "eV", "eP")
                p.sdma(xres_sb[:], x_own[tt * 128:(tt + 1) * 128, :])
                for half in range(2):
                    p.wait(PE, "eV")
                    for j in range(8):
                        dc = half * 8 + j
                        ins = PE.transpose(
                            PS_T[:, j * 128:(j + 1) * 128],
                            proj_bf[:, dc * T + tt * 128: dc * T + (tt + 1) * 128],
                            ident_bf[:])
                        if j == 7:
                            p.inc(ins, "eP")
                    p.wait(V, "eP")
                    for j in range(8):
                        dc = half * 8 + j
                        ins = V.tensor_copy(xproj[:, dc * 128:(dc + 1) * 128],
                                            PS_T[:, j * 128:(j + 1) * 128])
                        if j == 7:
                            p.inc(ins, "eV")
                p.wait(V, "dX")
                emit_ln_tail(p, xproj[:], xres_sb[:], xsum_sb[:], xout_sb[:],
                             xres_sb[:], stats, tt, eps_col)
                p.wait(SY, "eV")
                p.sdma(x1_out[tt * 128:(tt + 1) * 128, :], xout_sb[:])
            p.wait(SY, "dX")


def emit_l0f(nc, p, io):
    x_own, wgu, wpr, ident = io["x_in"], io["wgu"], io["wpr"], io["ident"]
    x2_out, wpt = io["x_out"], io["wpt"]
    dbg = io.get("dbg")

    PE, V, Sc, G, SY = nc.tensor, nc.vector, nc.scalar, nc.gpsimd, nc.sync
    p.barrier()
    with contextlib.ExitStack() as stk:
        E = lambda n, *a, **k: stk.enter_context(nc.sbuf_tensor("f0f0" + n, *a, **k))
        EP = lambda n, *a, **k: stk.enter_context(nc.psum_tensor("f0" + n, *a, **k))
        PS_BIG = EP("PS_BIG", [128, 2048], F32)
        PS_A = EP("PS_A", [128, 512], F32)
        PS_B = EP("PS_B", [128, 512], F32)
        PS_C = EP("PS_C", [128, 512], F32)
        PS_T = EP("PS_T", [128, 1024], BF16)

        ident_bf = E("ident_bf", [128, 128], BF16)
        ones_col16 = E("ones_col16", [128, 1], F16)
        onesf32 = E("onesf32", [1, 1], F32)
        eps_col = E("eps_col", [128, 1], F32)
        inv64_col = E("inv64_col", [128, 1], F32)
        thr_col = E("thr_col", [128, 1], F32)
        epsR = E("epsR", [1, 1], F32)
        inv8192c = E("inv8192c", [1, 1], F32)
        gsum = E("gsum", [128, 128], F32)
        scale_bf = E("scale_bf", [128, 128], BF16)
        sc2_bf = E("sc2_bf", [128, 128], BF16)
        thr_f32 = E("thr_f32", [128, 128], F32)
        rstd_cols = E("rstd_cols", [128, 8], F32)
        stats = {"ssum": E("ssum", [128, NT], F32),
                 "ssq": E("ssq", [128, NT], F32),
                 "m": E("mcol", [128, 1], F32),
                 "tmp": E("tmpc", [128, 6], F32)}

        p.gdma(ident_bf[:], ident)
        V.memset(ones_col16[:], 1.0)
        V.memset(onesf32[:], 1.0)
        V.memset(eps_col[:], LN_EPS)
        V.memset(inv64_col[:], INV64)
        V.memset(thr_col[:], THR)
        V.memset(epsR[:], RMS_EPS)
        p.inc(V.memset(inv8192c[:], INV8192), "eV")

        hid = E("hid", [128, 64 * 1024], BF16)

        with (nc.sbuf_tensor("f0wbP0", [128, HH], BF16) as wbP0,
              nc.sbuf_tensor("f0wbP1", [128, HH], BF16) as wbP1,
              nc.sbuf_tensor("f0bufA", [128, HH], BF16) as bufA,
              nc.sbuf_tensor("f0bufB", [128, HH], BF16) as bufB,
              nc.sbuf_tensor("f0stage", [128, 1024], BF16) as stage):
            wbP = [wbP0, wbP1]
            ldp = [0] * 18
            qp = [0] * 18
            tp = [0] * 18

            def b_load(i):
                if i < 0 or i > 15:
                    return
                p.wait_val(G, "eP", tp[i - 2] if i >= 2 else 0)
                p.gdma(wbP[i % 2][:], wpr[i * 128:(i + 1) * 128, :])
                ldp[i] = p.snap("dW")

            def b_qnt(i):
                if i < 0 or i > 15:
                    return
                p.wait_val(V, "dW", ldp[i])
                p.wait_val(Sc, "dW", ldp[i])
                p.wait_val(V, "eP", tp[i - 1] if i >= 1 else 0)
                emit_quant(p, wbP[i % 2], bufA, bufB, gsum, scale_bf, sc2_bf,
                           thr_f32, inv64_col, thr_col, HH // 64)
                qp[i] = p.snap("eV")

            b_load(0)
            b_qnt(0)
            b_load(1)
            for dr in range(16):
                p.wait_val(PE, "eV", qp[dr])
                for half in range(8):
                    p.wait(PE, "eV")
                    for j in range(8):
                        hc = half * 8 + j
                        ins = PE.transpose(PS_T[:, j * 128:(j + 1) * 128],
                                           wbP[dr % 2][:, hc * 128:(hc + 1) * 128],
                                           ident_bf[:])
                        if j == 7:
                            p.inc(ins, "eP")
                    p.wait(V, "eP")
                    p.wait(V, "dW")
                    for j in range(8):
                        ins = V.tensor_copy(stage[:, j * 128:(j + 1) * 128],
                                            PS_T[:, j * 128:(j + 1) * 128])
                        if j == 7:
                            p.inc(ins, "eV")
                    p.wait(G, "eV")
                    p.gdma(wpt[half * 8:half * 8 + 8, dr].rearrange(
                               "j p f -> p j f"),
                           stage[:].rearrange("p (j f) -> p j f", f=128))
                tp[dr] = p.snap("eP")
                b_qnt(dr + 1)
                b_load(dr + 2)

        with (nc.sbuf_tensor("f0xf", [128, NC * T], BF16) as xf,
              nc.sbuf_tensor("f0xot", [128, 1024], BF16) as xot):
            emit_stream_to_fm(p, lambda t: x_own[t * 128:(t + 1) * 128, :],
                              xf, xot, PS_T, ident_bf)
            with (nc.sbuf_tensor("f0wbG0", [128, 2 * D], BF16) as wbG0,
                  nc.sbuf_tensor("f0wbG1", [128, 2 * D], BF16) as wbG1,
                  nc.sbuf_tensor("f0bufAg", [128, 2 * D], BF16) as bufAg,
                  nc.sbuf_tensor("f0bufBg", [128, 2 * D], BF16) as bufBg,
                  nc.sbuf_tensor("f0wgT", [128, 2 * NC * 128], BF16) as wgT,
                  nc.sbuf_tensor("f0gsil", [128, 1024], BF16) as gsil):
                wbG = [wbG0, wbG1]
                h2 = gsil
                ld_snap = [0] * 66
                q_snap = [0] * 66
                t_snap = [0] * 66
                sil_snap = [0] * 66
                hid_snap = [0] * 66
                sq_snap = [0] * 66
                mm_snap = [0] * 66

                def emit_load(i):
                    if i > 64 - 1 or i < 0:
                        return
                    p.wait_val(G, "eP", t_snap[i - 2] if i >= 2 else 0)
                    p.gdma(wbG[i % 2][:, 0:D], wgu[i * 128:(i + 1) * 128, :])
                    p.gdma(wbG[i % 2][:, D:2 * D],
                           wgu[(64 + i) * 128:(65 + i) * 128, :])
                    ld_snap[i] = p.snap("dW")

                def emit_qnt(i):
                    if i > 63 or i < 0:
                        return
                    p.wait_val(V, "dW", ld_snap[i])
                    p.wait_val(Sc, "dW", ld_snap[i])
                    p.wait_val(V, "eP", t_snap[i - 1] if i >= 1 else 0)
                    emit_quant(p, wbG[i % 2], bufAg, bufBg, gsum, scale_bf,
                               sc2_bf, thr_f32, inv64_col, thr_col, 2 * D // 64)
                    q_snap[i] = p.snap("eV")

                emit_load(0)
                emit_qnt(0)
                emit_load(1)
                for hc in range(64):
                    p.wait_val(PE, "eV", q_snap[hc])
                    for half in range(4):
                        p.wait(PE, "eV")
                        for j in range(8):
                            c = half * 8 + j
                            ins = PE.transpose(PS_T[:, j * 128:(j + 1) * 128],
                                               wbG[hc % 2][:, c * 128:(c + 1) * 128],
                                               ident_bf[:])
                        p.inc(ins, "eP")
                        p.wait(V, "eP")
                        for j in range(8):
                            c = half * 8 + j
                            ins = V.tensor_copy(wgT[:, c * 128:(c + 1) * 128],
                                                PS_T[:, j * 128:(j + 1) * 128])
                        p.inc(ins, "eV")
                    t_snap[hc] = p.snap("eP")
                    if hc >= 1:
                        p.wait_val(PE, "eA", sil_snap[hc - 1])
                        p.wait_val(PE, "eV", hid_snap[hc - 1])
                    p.wait(PE, "eV")
                    for gu_i in range(2):
                        for ts in range(2):
                            dst = PS_BIG[:, gu_i * 1024 + ts * 512:
                                         gu_i * 1024 + (ts + 1) * 512]
                            for c in range(NC):
                                ins = PE.matmul(
                                    dst, wgT[:, (gu_i * NC + c) * 128:
                                             (gu_i * NC + c + 1) * 128],
                                    xf[:, c * T + ts * 512: c * T + ts * 512 + 512],
                                    start=(c == 0), stop=(c == NC - 1))
                    p.inc(ins, "eP")
                    mm_snap[hc] = p.snap("eP")
                    if hc >= 1:
                        p.wait_val(PE, "eA", sq_snap[hc - 1])
                        PE.matmul(PS_A[0:1, :], ones_col16[:], h2[:, 0:512],
                                  start=(hc - 1 == 0), stop=False,
                                  skip_group_check=True)
                        ins = PE.matmul(PS_B[0:1, :], ones_col16[:],
                                        h2[:, 512:1024],
                                        start=(hc - 1 == 0), stop=False,
                                        skip_group_check=True)
                        p.inc(ins, "eP")
                    emit_qnt(hc + 1)
                    emit_load(hc + 2)
                    p.wait(Sc, "eP")
                    p.inc(Sc.activation(gsil[:], PS_BIG[:, 0:1024], AF.Silu), "eA")
                    sil_snap[hc] = p.snap("eA")
                    p.wait_val(V, "eA", sil_snap[hc])
                    p.inc(V.tensor_tensor(hid[:, hc * 1024:(hc + 1) * 1024],
                                          gsil[:], PS_BIG[:, 1024:2048],
                                          ALU.mult), "eV")
                    hid_snap[hc] = p.snap("eV")
                    p.wait_val(Sc, "eV", hid_snap[hc])
                    p.inc(Sc.activation(h2[:, 0:1024],
                                        hid[:, hc * 1024:(hc + 1) * 1024],
                                        AF.Square), "eA")
                    sq_snap[hc] = p.snap("eA")
                p.wait_val(PE, "eA", sq_snap[63])
                PE.matmul(PS_A[0:1, :], ones_col16[:], h2[:, 0:512],
                          start=False, stop=True, skip_group_check=True)
                ins = PE.matmul(PS_B[0:1, :], ones_col16[:], h2[:, 512:1024],
                                start=False, stop=True, skip_group_check=True)
                p.inc(ins, "eP")

        with nc.sbuf_tensor("f0msq", [1, 1024], F32) as msq_sb:
            rstd_row = msq_sb
            p.wait(V, "eP")
            V.tensor_copy(msq_sb[0:1, 0:512], PS_A[0:1, :])
            p.inc(V.tensor_copy(msq_sb[0:1, 512:1024], PS_B[0:1, :]), "eV")
            p.wait(Sc, "eV")
            p.inc(Sc.activation(rstd_row[0:1, :], msq_sb[0:1, :], AF.Sqrt,
                                bias=epsR[:], scale=inv8192c[:]), "eA")
            p.wait(V, "eA")
            p.inc(V.reciprocal(rstd_row[0:1, :], rstd_row[0:1, :]), "eV")
            p.wait(PE, "eV")
            for j in range(NT):
                ins = PE.matmul(PS_C[:, j:j + 1],
                                rstd_row[0:1, j * 128:(j + 1) * 128],
                                onesf32[0:1, 0:1], start=True, stop=True)
            p.inc(ins, "eP")
            p.wait(V, "eP")
            p.inc(V.tensor_copy(rstd_cols[:, 0:NT], PS_C[:, 0:NT]), "eV")

        proj_bf = E("proj_bf", [128, NC * T], BF16)
        with (nc.sbuf_tensor("f0w2s0", [128, 64 * 128], BF16) as w2s0,
              nc.sbuf_tensor("f0w2s1", [128, 64 * 128], BF16) as w2s1):
            w2s = [w2s0, w2s1]
            pp = [PS_A, PS_B]
            ldw2 = [0] * 18
            mmw2 = [0] * 18
            evw2 = [0] * 18

            def w2_load(i):
                if i < 0 or i > 15:
                    return
                p.wait_val(G, "eP", mmw2[i - 2] if i >= 2 else 0)
                for qj in range(4):
                    p.gdma(w2s[i % 2][:, qj * 16 * 128:(qj + 1) * 16 * 128].rearrange(
                               "p (j f) -> p j f", f=128),
                           wpt[qj * 16:(qj + 1) * 16, i].rearrange(
                               "j p f -> p j f"))
                ldw2[i] = p.snap("dW")

            p.wait(G, "eP", "eV", "eA")
            w2_load(0)
            w2_load(1)
            p.wait(PE, "eV", "eA")
            for dc in range(NC):
                p.wait_val(PE, "dW", ldw2[dc])
                p.wait_val(PE, "eV", evw2[dc - 1] if dc >= 1 else 0)
                for ts in range(2):
                    ps = pp[ts][:]
                    for hc in range(64):
                        ins = PE.matmul(ps, w2s[dc % 2][:, hc * 128:(hc + 1) * 128],
                                        hid[:, hc * 1024 + ts * 512:
                                            hc * 1024 + ts * 512 + 512],
                                        start=(hc == 0), stop=(hc == 63))
                    p.inc(ins, "eP")
                mmw2[dc] = p.snap("eP")
                w2_load(dc + 2)
                p.wait_val(V, "eP", mmw2[dc])
                for ts in range(2):
                    ins = V.tensor_copy(
                        proj_bf[:, dc * T + ts * 512: dc * T + (ts + 1) * 512],
                        pp[ts][:])
                p.inc(ins, "eV")
                evw2[dc] = p.snap("eV")

        with (nc.sbuf_tensor("f0xproj", [128, D], BF16) as xproj,
              nc.sbuf_tensor("f0xres_g", [128, D], F32) as xres_t,
              nc.sbuf_tensor("f0xsum_g", [128, D], F32) as xsum_t):
            xres = xres_t[:]
            xsum = xsum_t[:]
            for tt in range(NT):
                p.wait(SY, "eV", "eP")
                p.sdma(xres, x_own[tt * 128:(tt + 1) * 128, :])
                for half in range(2):
                    p.wait(PE, "eV")
                    for j in range(8):
                        dc = half * 8 + j
                        ins = PE.transpose(
                            PS_T[:, j * 128:(j + 1) * 128],
                            proj_bf[:, dc * T + tt * 128: dc * T + (tt + 1) * 128],
                            ident_bf[:])
                        if j == 7:
                            p.inc(ins, "eP")
                    p.wait(V, "eP")
                    for j in range(8):
                        dc = half * 8 + j
                        ins = V.tensor_copy(xproj[:, dc * 128:(dc + 1) * 128],
                                            PS_T[:, j * 128:(j + 1) * 128])
                        if j == 7:
                            p.inc(ins, "eV")
                p.wait(V, "dX")
                emit_ln_tail(p, xproj[:], xres, xsum, xres, xres, stats, tt,
                             eps_col, rstd_extra=rstd_cols[:, tt:tt + 1])
                p.wait(SY, "eV")
                p.sdma(x2_out[tt * 128:(tt + 1) * 128, :], xres)
            p.wait(SY, "dX")


def emit_l1a(nc, p, io):
    x_own, x_kv = io["x_in"], io["x_kv"]
    Wq, bq, Wk, bk, Wv, bv, Wo, bo = (io[k] for k in
                                      ("Wq", "bq", "Wk", "bk", "Wv", "bv",
                                       "Wo", "bo"))
    ident, x3_out = io["ident"], io["x_out"]
    PE, V, Sc, G, SY = nc.tensor, nc.vector, nc.scalar, nc.gpsimd, nc.sync
    p.barrier()
    with contextlib.ExitStack() as stk:
        E = lambda n, *a, **k: stk.enter_context(nc.sbuf_tensor("a1a1" + n, *a, **k))
        EP = lambda n, *a, **k: stk.enter_context(nc.psum_tensor("a1" + n, *a, **k))
        PS_BIG = EP("PS_BIG", [128, 2048], F32)
        PS_A = EP("PS_A", [128, 512], F32)
        PS_B = EP("PS_B", [128, 512], F32)
        PS_C = EP("PS_C", [128, 512], F32)
        PS_T = EP("PS_T", [128, 1024], BF16)

        ctx_sb = E("ctx", [128, NHEADS * T], BF16)
        ob16 = E("ob16", [1, D], F16)
        ones16 = E("ones16", [1, 512], F16)
        ident_bf = E("ident_bf", [128, 128], BF16)
        xres_sb = E("xres_sb", [128, D], F32)
        xsum_sb = E("xsum_sb", [128, D], F32)
        xout_sb = xres_sb
        negmax = E("negmax", [128, 1], F32)
        rowsum = E("rowsum", [128, 1], F32)
        rcol = E("rcol", [128, 1], F32)
        eps_col = E("eps_col", [128, 1], F32)
        stats = {"ssum": E("ssum", [128, NT], F32),
                 "ssq": E("ssq", [128, NT], F32),
                 "m": E("mcol", [128, 1], F32),
                 "tmp": E("tmpc", [128, 6], F32)}
        stk1 = contextlib.ExitStack()
        E1 = lambda n, *a, **k: stk1.enter_context(nc.sbuf_tensor("a1a1" + n, *a, **k))
        xf_own = E1("xf_own", [128, NC * T], BF16)
        xf_kv = E1("xf_kv", [128, NC * 2 * T], BF16)
        K_sb = E1("K_sb", [128, 2048], BF16)
        V_sb = E1("V_sb", [128, 2048], BF16)
        Q_sb = E1("Q_sb", [128, 1024], BF16)
        s_tile = E1("s_tile", [128, 2048], F32)
        p_bf = E1("p_bf", [128, 2048], BF16)
        pT_sb = E1("pT_sb", [128, 2048], BF16)
        wq_sb = E1("wq_sb", [128, 16 * 128], BF16)
        wk_sb = E1("wk_sb", [128, 16 * 128], BF16)
        wv_sb = E1("wv_sb", [128, 16 * 128], BF16)
        bq16 = E1("bq16", [1, 128], F16)
        bk16 = E1("bk16", [1, 128], F16)
        bv16 = E1("bv16", [1, 128], F16)
        xot = E1("xot", [128, 1024], BF16)

        p.gdma(ident_bf[:], ident)
        p.gdma(ob16[:], bo.rearrange("(o n) -> o n", o=1))
        V.memset(ones16[:], 1.0)
        p.inc(V.memset(eps_col[:], LN_EPS), "eV")

        emit_stream_to_fm(p, lambda t: x_own[t * 128:(t + 1) * 128, :],
                          xf_own, xot, PS_T, ident_bf)
        emit_stream_to_fm(p, lambda t: x_kv[t * 128:(t + 1) * 128, :],
                          xf_kv, xot, PS_T, ident_bf, ntiles=2 * NT)

        prod_ps = [PS_A, PS_B]
        npd = 0
        for g in range(4):
            p.wait(G, "eP")
            p.gdma(wk_sb[:].rearrange("p (c j) -> p c j", j=128),
                   Wk[:, g * 128:(g + 1) * 128].rearrange("(c p) j -> p c j", p=128))
            p.gdma(wv_sb[:].rearrange("p (c j) -> p c j", j=128),
                   Wv[:, g * 128:(g + 1) * 128].rearrange("(c p) j -> p c j", p=128))
            p.gdma(bk16[:], bk.rearrange("(o n) -> o n", o=1)[0:1, g * 128:(g + 1) * 128])
            p.gdma(bv16[:], bv.rearrange("(o n) -> o n", o=1)[0:1, g * 128:(g + 1) * 128])
            p.wait(PE, "dW", "eV")
            for kt in range(4):
                ps = prod_ps[npd % 2][:]
                npd += 1
                for c in range(NC):
                    PE.matmul(ps, wk_sb[:, c * 128:(c + 1) * 128],
                              xf_kv[:, c * 2 * T + kt * 512: c * 2 * T + (kt + 1) * 512],
                              start=(c == 0), stop=False)
                p.inc(PE.matmul(ps, bk16[0:1, :], ones16[0:1, :],
                                start=False, stop=True), "eP")
                p.wait(V, "eP")
                p.inc(V.tensor_copy(K_sb[:, kt * 512:(kt + 1) * 512], ps), "eV")
                p.wait(PE, "eV")
            for vt in range(16):
                ps = prod_ps[npd % 2][:]
                npd += 1
                for c in range(NC):
                    PE.matmul(ps[:, 0:128],
                              xf_kv[:, c * 2 * T + vt * 128: c * 2 * T + vt * 128 + 128],
                              wv_sb[:, c * 128:(c + 1) * 128],
                              start=(c == 0), stop=False)
                p.inc(PE.matmul(ps[:, 0:128], ones16[0:1, 0:128], bv16[0:1, :],
                                start=False, stop=True), "eP")
                p.wait(V, "eP")
                p.inc(V.tensor_copy(V_sb[:, vt * 128:(vt + 1) * 128], ps[:, 0:128]),
                      "eV")
                p.wait(PE, "eV")
            for hh in range(4):
                h = 4 * g + hh
                p.wait(G, "eP")
                p.gdma(wq_sb[:].rearrange("p (c j) -> p c j", j=128),
                       Wq[:, h * 128:(h + 1) * 128].rearrange("(c p) j -> p c j", p=128))
                p.gdma(bq16[:], bq.rearrange("(o n) -> o n", o=1)[0:1, h * 128:(h + 1) * 128])
                p.wait(PE, "dW", "eV")
                for qs in range(2):
                    ps = prod_ps[npd % 2][:]
                    npd += 1
                    for c in range(NC):
                        PE.matmul(ps, wq_sb[:, c * 128:(c + 1) * 128],
                                  xf_own[:, c * T + qs * 512: c * T + qs * 512 + 512],
                                  start=(c == 0), stop=False)
                    p.inc(PE.matmul(ps, bq16[0:1, :], ones16[0:1, :],
                                    start=False, stop=True), "eP")
                    p.wait(V, "eP")
                    p.inc(V.tensor_scalar_mul(Q_sb[:, qs * 512:(qs + 1) * 512], ps,
                                              QSCALE), "eV")
                    p.wait(PE, "eV")
                for qt in range(NT):
                    p.wait(PE, "eV", "eA")
                    for kt in range(4):
                        ins = PE.matmul(PS_BIG[:, kt * 512:(kt + 1) * 512],
                                        Q_sb[:, qt * 128:(qt + 1) * 128],
                                        K_sb[:, kt * 512:(kt + 1) * 512],
                                        start=True, stop=True)
                    p.inc(ins, "eP")
                    p.wait(V, "eP")
                    p.inc(V.tensor_reduce(negmax[:], PS_BIG[:], AX.XY,
                                          ALU.max, negate=True), "eV")
                    p.wait(Sc, "eV")
                    p.inc(Sc.activation(s_tile[:], PS_BIG[:], AF.Exp,
                                        bias=negmax[:], accum_out=rowsum[:]), "eA")
                    p.wait(V, "eA")
                    p.inc(V.reciprocal(rcol[:], rowsum[:]), "eV")
                    p.wait(Sc, "eV")
                    p.inc(Sc.activation(p_bf[:], s_tile[:], AF.Copy,
                                        scale=rcol[:]), "eA")
                    done = 0
                    while done < 16:
                        p.wait(PE, "eV", "eA")
                        for j in range(8):
                            kc = done + j
                            ins = PE.transpose(PS_T[:, j * 128:(j + 1) * 128],
                                               p_bf[:, kc * 128:(kc + 1) * 128],
                                               ident_bf[:])
                        p.inc(ins, "eP")
                        p.wait(V, "eP")
                        for j in range(8):
                            kc = done + j
                            ins = V.tensor_copy(pT_sb[:, kc * 128:(kc + 1) * 128],
                                                PS_T[:, j * 128:(j + 1) * 128])
                        p.inc(ins, "eV")
                        done += 8
                    p.wait(PE, "eV", "eA")
                    for kc in range(16):
                        ins = PE.matmul(PS_C[:, 0:128], V_sb[:, kc * 128:(kc + 1) * 128],
                                        pT_sb[:, kc * 128:(kc + 1) * 128],
                                        start=(kc == 0), stop=(kc == 15))
                    p.inc(ins, "eP")
                    p.wait(Sc, "eP")
                    p.inc(Sc.copy(ctx_sb[:, h * T + qt * 128: h * T + (qt + 1) * 128],
                                  PS_C[:, 0:128]), "eA")

        stk1.close()
        proj_bf = E("proj_bf", [128, NC * T], BF16)
        with (nc.sbuf_tensor("a1ows0", [128, 16 * 128], BF16) as ows0,
              nc.sbuf_tensor("a1ows1", [128, 16 * 128], BF16) as ows1,
              nc.sbuf_tensor("a1xproj", [128, D], BF16) as xproj):
            ows = [ows0, ows1]
            pp = [PS_A, PS_B]
            ldo = [0] * 18
            mmo = [0] * 18
            evo = [0] * 18

            def wo_load(i):
                if i < 0 or i > 15:
                    return
                p.wait_val(G, "eP", mmo[i - 2] if i >= 2 else 0)
                p.gdma(ows[i % 2][:].rearrange("p (c j) -> p c j", j=128),
                       Wo[:, i * 128:(i + 1) * 128].rearrange(
                           "(c p) j -> p c j", p=128))
                ldo[i] = p.snap("dW")

            p.wait(G, "eP", "eV", "eA")
            wo_load(0)
            wo_load(1)
            p.wait(PE, "eV", "eA")
            for dc in range(NC):
                p.wait_val(PE, "dW", ldo[dc])
                p.wait_val(PE, "eV", evo[dc - 1] if dc >= 1 else 0)
                for ts in range(2):
                    ps = pp[ts][:]
                    for c in range(NC):
                        PE.matmul(ps, ows[dc % 2][:, c * 128:(c + 1) * 128],
                                  ctx_sb[:, c * T + ts * 512: c * T + ts * 512 + 512],
                                  start=(c == 0), stop=False)
                    ins = PE.matmul(ps, ob16[0:1, dc * 128:(dc + 1) * 128],
                                    ones16[0:1, 0:512], start=False, stop=True)
                    p.inc(ins, "eP")
                mmo[dc] = p.snap("eP")
                wo_load(dc + 2)
                p.wait_val(V, "eP", mmo[dc])
                for ts in range(2):
                    ins = V.tensor_copy(
                        proj_bf[:, dc * T + ts * 512: dc * T + (ts + 1) * 512],
                        pp[ts][:])
                p.inc(ins, "eV")
                evo[dc] = p.snap("eV")

            for tt in range(NT):
                p.wait(SY, "eV", "eP")
                p.sdma(xres_sb[:], x_own[tt * 128:(tt + 1) * 128, :])
                for half in range(2):
                    p.wait(PE, "eV")
                    for j in range(8):
                        dc = half * 8 + j
                        ins = PE.transpose(
                            PS_T[:, j * 128:(j + 1) * 128],
                            proj_bf[:, dc * T + tt * 128: dc * T + (tt + 1) * 128],
                            ident_bf[:])
                        if j == 7:
                            p.inc(ins, "eP")
                    p.wait(V, "eP")
                    for j in range(8):
                        dc = half * 8 + j
                        ins = V.tensor_copy(xproj[:, dc * 128:(dc + 1) * 128],
                                            PS_T[:, j * 128:(j + 1) * 128])
                        if j == 7:
                            p.inc(ins, "eV")
                p.wait(V, "dX")
                emit_ln_tail(p, xproj[:], xres_sb[:], xsum_sb[:], xout_sb[:],
                             xres_sb[:], stats, tt, eps_col)
                p.wait(SY, "eV")
                p.sdma(x3_out[tt * 128:(tt + 1) * 128, :], xout_sb[:])
            p.wait(SY, "dX")


def emit_l1f(nc, p, io):
    x_own, w1t, w3t, w2t = io["x_in"], io["w1t"], io["w3t"], io["w2t"]
    h128, ident, x4_out = io["h128"], io["ident"], io["x_out"]
    PE, V, Sc, G, SY = nc.tensor, nc.vector, nc.scalar, nc.gpsimd, nc.sync
    p.barrier()
    with contextlib.ExitStack() as stk:
        E = lambda n, *a, **k: stk.enter_context(nc.sbuf_tensor("f1f1" + n, *a, **k))
        EP = lambda n, *a, **k: stk.enter_context(nc.psum_tensor("f1" + n, *a, **k))
        PS_BIG = EP("PS_BIG", [128, 2048], F32)
        PS_A = EP("PS_A", [128, 512], F32)
        PS_B = EP("PS_B", [128, 512], F32)
        PS_C = EP("PS_C", [128, 512], F32)
        PS_T = EP("PS_T", [128, 1024], F16)

        h128_16 = E("h128_16", [128, 128], F16)
        ident_16 = E("ident_16", [128, 128], F16)
        fsc_col = E("fsc_col", [128, 1], F32)
        eps_col = E("eps_col", [128, 1], F32)
        stats = {"ssum": E("ssum", [128, NT], F32),
                 "ssq": E("ssq", [128, NT], F32),
                 "m": E("mcol", [128, 1], F32),
                 "tmp": E("tmpc", [128, 6], F32)}

        p.gdma(h128_16[:], h128)
        p.gdma(ident_16[:], ident)
        V.memset(fsc_col[:], FWHT_SCALE)
        p.inc(V.memset(eps_col[:], LN_EPS), "eV")

        hid = E("hid", [128, 64 * 1024], F16)

        with (nc.sbuf_tensor("f1xf", [128, NC * T], F16) as xf,
              nc.sbuf_tensor("f1xot", [128, 1024], F16) as xot,
              nc.sbuf_tensor("f1w1s0", [128, 16 * 128], F16) as w1s0,
              nc.sbuf_tensor("f1w1s1", [128, 16 * 128], F16) as w1s1,
              nc.sbuf_tensor("f1w3s0", [128, 16 * 128], F16) as w3s0,
              nc.sbuf_tensor("f1w3s1", [128, 16 * 128], F16) as w3s1,
              nc.sbuf_tensor("f1gsil", [128, 1024], F16) as gsil):
            w1s = [w1s0, w1s1]
            w3s = [w3s0, w3s1]
            emit_stream_to_fm(p, lambda t: x_own[t * 128:(t + 1) * 128, :],
                              xf, xot, PS_T, ident_16)
            ldu = [0] * 66
            mmu = [0] * 66
            silu_s = [0] * 66
            hidu = [0] * 66

            def u_load(i):
                if i < 0 or i > 63:
                    return
                p.wait_val(G, "eP", mmu[i - 2] if i >= 2 else 0)
                p.gdma(w1s[i % 2][:].rearrange("p (c j) -> p c j", j=128),
                       w1t[:, i * 128:(i + 1) * 128].rearrange(
                           "(c p) j -> p c j", p=128))
                p.gdma(w3s[i % 2][:].rearrange("p (c j) -> p c j", j=128),
                       w3t[:, i * 128:(i + 1) * 128].rearrange(
                           "(c p) j -> p c j", p=128))
                ldu[i] = p.snap("dW")

            u_load(0)
            u_load(1)
            for hc in range(64):
                p.wait_val(PE, "dW", ldu[hc])
                if hc >= 1:
                    p.wait_val(PE, "eA", silu_s[hc - 1])
                    p.wait_val(PE, "eV", hidu[hc - 1])
                else:
                    p.wait(PE, "eV")
                for ws, off in ((w1s[hc % 2], 0), (w3s[hc % 2], 1024)):
                    for ts in range(2):
                        dst = PS_BIG[:, off + ts * 512: off + (ts + 1) * 512]
                        for c in range(NC):
                            ins = PE.matmul(
                                dst, ws[:, c * 128:(c + 1) * 128],
                                xf[:, c * T + ts * 512: c * T + ts * 512 + 512],
                                start=(c == 0), stop=(c == NC - 1))
                p.inc(ins, "eP")
                mmu[hc] = p.snap("eP")
                u_load(hc + 2)
                p.wait_val(Sc, "eP", mmu[hc])
                p.inc(Sc.activation(gsil[:], PS_BIG[:, 0:1024], AF.Silu), "eA")
                silu_s[hc] = p.snap("eA")
                p.wait_val(V, "eA", silu_s[hc])
                p.inc(V.tensor_tensor(hid[:, hc * 1024:(hc + 1) * 1024], gsil[:],
                                      PS_BIG[:, 1024:2048], ALU.mult), "eV")
                hidu[hc] = p.snap("eV")

        with nc.sbuf_tensor("f1tmp16", [128, 1024], F16) as tmp16:
            for bit in (1, 2, 4, 8, 16, 32):
                p.wait(V, "eV")
                for c in range(64):
                    if c & bit:
                        continue
                    a = hid[:, c * 1024:(c + 1) * 1024]
                    b = hid[:, (c + bit) * 1024:(c + bit + 1) * 1024]
                    V.tensor_tensor(tmp16[:], a, b, ALU.subtract)
                    V.tensor_tensor(a, a, b, ALU.add)
                    V.tensor_copy(b, tmp16[:])
                p.inc(V.tensor_copy(tmp16[:, 0:1], tmp16[:, 0:1]), "eV")

        for hc in range(64):
            p.wait(PE, "eV", "eA")
            for ts in range(2):
                ps = [PS_A, PS_B][ts][:]
                ins = PE.matmul(ps, h128_16[:],
                                hid[:, hc * 1024 + ts * 512: hc * 1024 + ts * 512 + 512],
                                start=True, stop=True)
            p.inc(ins, "eP")
            p.wait(Sc, "eP")
            for ts in range(2):
                ins = Sc.activation(
                    hid[:, hc * 1024 + ts * 512: hc * 1024 + ts * 512 + 512],
                    [PS_A, PS_B][ts][:], AF.Copy, scale=fsc_col[:])
            p.inc(ins, "eA")

        proj_16 = E("proj_16", [128, NC * T], F16)
        with (nc.sbuf_tensor("f1w2s0", [128, 64 * 128], F16) as w2sA,
              nc.sbuf_tensor("f1w2s1", [128, 64 * 128], F16) as w2sB):
            w2s = [w2sA, w2sB]
            pp = [PS_A, PS_B]
            ld2 = [0] * 18
            mm2 = [0] * 18
            ev2 = [0] * 18

            def p2_load(i):
                if i < 0 or i > 15:
                    return
                p.wait_val(G, "eP", mm2[i - 2] if i >= 2 else 0)
                for qj in range(4):
                    p.gdma(w2s[i % 2][:, qj * 16 * 128:(qj + 1) * 16 * 128].rearrange(
                               "p (c j) -> p c j", j=128),
                           w2t[:, i * 128:(i + 1) * 128].rearrange(
                               "(c p) j -> p c j", p=128)[:, qj * 16:(qj + 1) * 16, :])
                ld2[i] = p.snap("dW")

            p.wait(G, "eP", "eV", "eA")
            p2_load(0)
            p2_load(1)
            for dc in range(NC):
                p.wait_val(PE, "dW", ld2[dc])
                p.wait_val(PE, "eV", ev2[dc - 1] if dc >= 1 else 0)
                p.wait(PE, "eA")
                for ts in range(2):
                    ps = pp[ts][:]
                    for hc in range(64):
                        ins = PE.matmul(ps, w2s[dc % 2][:, hc * 128:(hc + 1) * 128],
                                        hid[:, hc * 1024 + ts * 512:
                                            hc * 1024 + ts * 512 + 512],
                                        start=(hc == 0), stop=(hc == 63))
                    p.inc(ins, "eP")
                mm2[dc] = p.snap("eP")
                p2_load(dc + 2)
                p.wait_val(V, "eP", mm2[dc])
                for ts in range(2):
                    ins = V.tensor_copy(
                        proj_16[:, dc * T + ts * 512: dc * T + (ts + 1) * 512],
                        pp[ts][:])
                p.inc(ins, "eV")
                ev2[dc] = p.snap("eV")

        with (nc.sbuf_tensor("f1xproj", [128, D], F16) as xproj,
              nc.sbuf_tensor("f1xres", [128, D], F32) as xres_t,
              nc.sbuf_tensor("f1xsum", [128, D], F32) as xsum_t):
            xres_sb = xres_t
            xsum_sb = xsum_t
            xout_sb = xres_sb
            for tt in range(NT):
                p.wait(SY, "eV", "eP")
                p.sdma(xres_sb[:], x_own[tt * 128:(tt + 1) * 128, :])
                for half in range(2):
                    p.wait(PE, "eV")
                    for j in range(8):
                        dc = half * 8 + j
                        ins = PE.transpose(
                            PS_T[:, j * 128:(j + 1) * 128],
                            proj_16[:, dc * T + tt * 128: dc * T + (tt + 1) * 128],
                            ident_16[:])
                        if j == 7:
                            p.inc(ins, "eP")
                    p.wait(V, "eP")
                    for j in range(8):
                        dc = half * 8 + j
                        ins = V.tensor_copy(xproj[:, dc * 128:(dc + 1) * 128],
                                            PS_T[:, j * 128:(j + 1) * 128])
                        if j == 7:
                            p.inc(ins, "eV")
                p.wait(V, "dX")
                emit_ln_tail(p, xproj[:], xres_sb[:], xsum_sb[:], xout_sb[:],
                             xres_sb[:], stats, tt, eps_col)
                p.wait(SY, "eV")
                p.sdma(x4_out[tt * 128:(tt + 1) * 128, :], xout_sb[:])
            p.wait(SY, "dX")


def emit_exchange(nc, p, x2_d, kv_sh, fl_sh, magic):
    G = nc.gpsimd
    p.barrier()
    with nc.sbuf_tensor("xm_sb", [1, 64], U32) as m_sb, \
         nc.sbuf_tensor("xf_sb", [1, 64], U32) as f_sb:
        p.sdma(m_sb[:], magic)
        p.wait(G, "dX", "dW")
        pid = G.partition_id()
        par = pid % 2
        sib = 1 - par
        p.gdma(kv_sh[bass.ds(par * T, T), :], x2_d)
        p.wait(G, "dW")
        p.gdma(fl_sh[bass.ds(par, 1), :], m_sb[:])
        p.wait(G, "dW")
        psem = nc.alloc_semaphore("psem")
        with G.register("pv") as pv, G.register("sv") as sv, \
             G.register("it") as it, G.register("mg") as mg, \
             G.register("cond") as cond:
            G.reg_load(mg, m_sb[0:1, 0:1])
            G.reg_mov(pv, 0)
            G.reg_mov(it, 0)
            G.reg_mov(sv, 0)
            G.reg_mov(cond, 1)
            with G.While(cond):
                G.reg_add(sv, sv, 16)
                G.dma_start(out=f_sb[:], in_=fl_sh[bass.ds(sib, 1), :]).then_inc(
                    psem, 16)
                G.wait_ge(psem, sv)
                G.reg_load(pv, f_sb[0:1, 0:1])
                G.reg_add(it, it, 1)
                G.reg_alu(cond, pv, mg, ALU.not_equal)
                with G.register("lim") as lim:
                    G.reg_alu(lim, it, 40000, ALU.is_lt)
                    G.reg_alu(cond, cond, lim, ALU.mult)



def build_fused():
    nc = bass.Bass("TRN2", target_bir_lowering=False)
    p = Ph(nc)

    def ein(name, shape):
        return nc.dram_tensor(name, shape, F32, kind="ExternalInput").ap()

    x_own = ein("x_own", [T, D])
    x_ctx = ein("x_ctx", [T, D])
    qkv_w = ein("qkv_w", [D, 3 * D])
    qkv_b = ein("qkv_b", [3 * D])
    out_w = ein("out_w", [D, D])
    out_b = ein("out_b", [D])
    maskB = nc.dram_tensor("maskB", [128, 2048], F16, kind="ExternalInput").ap()
    tri = ein("tri", [128, 128])
    ident = ein("ident", [128, 128])
    wgu = ein("wgu", [HGU, D])
    wpr = ein("wpr", [D, HH])
    Wq = ein("Wq", [D, D]); bq = ein("bq", [D])
    Wk = ein("Wk", [D, 512]); bk = ein("bk", [512])
    Wv = ein("Wv", [D, 512]); bv = ein("bv", [512])
    Wo = ein("Wo", [D, D]); bo = ein("bo", [D])
    w1t = ein("w1t", [D, HH])
    w3t = ein("w3t", [D, HH])
    w2t = ein("w2t", [HH, D])
    h128 = ein("h128", [128, 128])
    magic = nc.dram_tensor("magic", [1, 64], U32, kind="ExternalInput").ap()
    x4_out = nc.dram_tensor("x4", [T, D], F32, kind="ExternalOutput").ap()

    x1_d = nc.dram_tensor("x1_d", [T, D], F32, kind="Internal").ap()
    x2_d = nc.dram_tensor("x2_d", [T, D], F32, kind="Internal").ap()
    x3_d = nc.dram_tensor("x3_d", [T, D], F32, kind="Internal").ap()
    wpt_d = nc.dram_tensor("wpt_d", [64, 16, 128, 128], BF16, kind="Internal").ap()
    kv_sh = nc.dram_tensor("kv_sh", [2 * T, D], F32, kind="Internal",
                           addr_space="Shared").ap()
    fl_sh = nc.dram_tensor("fl_sh", [2, 64], U32, kind="Internal",
                           addr_space="Shared").ap()

    DEBUG = globals().get("FUSED_DEBUG", False)
    dbg2 = None
    if DEBUG:
        dx1 = nc.dram_tensor("dx1", [T, D], F32, kind="ExternalOutput").ap()
        dx2 = nc.dram_tensor("dx2", [T, D], F32, kind="ExternalOutput").ap()
        dx3 = nc.dram_tensor("dx3", [T, D], F32, kind="ExternalOutput").ap()
        dbg2 = {n: nc.dram_tensor(n, s, F32, kind="ExternalOutput").ap()
                for n, s in [("d_weffg", [128, 4096]), ("d_hid", [128, 2048]),
                             ("d_rms", [128, 8]), ("d_proj", [128, 1024])]}
    with nc.body("main"):
        emit_l0a(nc, p, dict(x_in=x_own, x_ctx=x_ctx, qkv_w=qkv_w, qkv_b=qkv_b,
                             out_w=out_w, out_b=out_b, maskB=maskB, tri=tri,
                             ident=ident, x_out=x1_d))
        emit_l0f(nc, p, dict(x_in=x1_d, wgu=wgu, wpr=wpr, ident=ident,
                             x_out=x2_d, wpt=wpt_d, dbg=dbg2))
        emit_exchange(nc, p, x2_d, kv_sh, fl_sh, magic)
        emit_l1a(nc, p, dict(x_in=x2_d, x_kv=kv_sh, Wq=Wq, bq=bq, Wk=Wk, bk=bk,
                             Wv=Wv, bv=bv, Wo=Wo, bo=bo, ident=ident,
                             x_out=x3_d))
        emit_l1f(nc, p, dict(x_in=x3_d, w1t=w1t, w3t=w3t, w2t=w2t, h128=h128,
                             ident=ident, x_out=x4_out))
        if DEBUG:
            p.barrier()
            p.gdma(dx1, x1_d)
            p.gdma(dx2, x2_d)
            p.gdma(dx3, x3_d)
            nc.gpsimd.wait_ge(p.sem["dW"], p.cnt["dW"])
    return nc



def make_l0a_consts():
    iden = np.eye(128, dtype=np.float32)
    trim = np.where(np.arange(128)[None, :] <= np.arange(128)[:, None],
                    0.0, MASKVAL).astype(np.float32)
    masks = []
    for c in range(8):
        half = c % 2
        B = np.zeros((128, 2048), dtype=np.float16)
        if half == 0:
            B[:, 0:1024] = np.float16(MASKVAL)
            B[:, 1024:2048] = np.arange(0, 1024, dtype=np.float16)[None, :]
        else:
            B[:, :] = np.arange(0, 2048, dtype=np.float16)[None, :]
        masks.append(B)
    return masks, trim, iden


def make_h128():
    h = np.array([[1.0]], dtype=np.float32)
    while h.shape[0] < 128:
        h = np.block([[h, h], [h, -h]])
    return h.astype(np.float32)


def shard_rows(x):
    return [np.ascontiguousarray(x[c * T:(c + 1) * T]) for c in range(8)]


def ctx_rows(x):
    out = []
    for c in range(8):
        b, half = c // 2, c % 2
        o = (b * 2 + (1 - half)) * T
        out.append(np.ascontiguousarray(x[o:o + T]))
    return out


_BUILT = None


def kernel(**inputs):
    global _BUILT
    ins = {k: np.ascontiguousarray(np.asarray(v)) for k, v in inputs.items()}
    x0 = ins["L0_attn_x"].reshape(8192, 2048)
    masks, trim, iden = make_l0a_consts()
    owns = shard_rows(x0)
    ctxs = ctx_rows(x0)
    w1tt = np.ascontiguousarray(ins["L1_ffn_W1"].T)
    w3tt = np.ascontiguousarray(ins["L1_ffn_W3"].T)
    w2tt = np.ascontiguousarray(ins["L1_ffn_W2"].T)
    magic = np.full((1, 64), np.random.randint(1, 2**31, dtype=np.int64),
                    dtype=np.uint32)
    if _BUILT is None:
        _BUILT = build_fused()
    nc = _BUILT
    in_maps = []
    for c in range(8):
        in_maps.append({
            "x_own": owns[c], "x_ctx": ctxs[c],
            "qkv_w": ins["L0_attn_qkv_w"], "qkv_b": ins["L0_attn_qkv_b"],
            "out_w": ins["L0_attn_out_w"], "out_b": ins["L0_attn_out_b"],
            "maskB": masks[c], "tri": trim, "ident": iden,
            "wgu": ins["L0_ffn_w_gate_up"], "wpr": ins["L0_ffn_w_proj"],
            "Wq": ins["L1_attn_Wq"], "bq": ins["L1_attn_bq"],
            "Wk": ins["L1_attn_Wk"], "bk": ins["L1_attn_bk"],
            "Wv": ins["L1_attn_Wv"], "bv": ins["L1_attn_bv"],
            "Wo": ins["L1_attn_Wo"], "bo": ins["L1_attn_bo"],
            "w1t": w1tt, "w3t": w3tt, "w2t": w2tt,
            "h128": make_h128(), "magic": magic,
        })
    res = run_bass_kernel_spmd(nc, in_maps, core_ids=list(range(8)))
    x4 = np.concatenate([r["x4"] for r in res.results], axis=0)
    return x4.reshape(4, 2048, 2048)
